# version 02
# baseline (speedup 1.0000x reference)
import numpy as np
import ml_dtypes
from contextlib import ExitStack

import concourse.bass as bass
import concourse.mybir as mybir
from concourse.bass import IndirectOffsetOnAxis
from concourse.bass_utils import run_bass_kernel_spmd
from concourse.alu_op_type import AluOpType as ALU

F32 = mybir.dt.float32
BF16 = mybir.dt.bfloat16
U32 = mybir.dt.uint32
AF = mybir.ActivationFunctionType
BFNP = ml_dtypes.bfloat16

NCORE = 8
T = 4096
D = 1024
NT = 32

E0, K0, CAP0, PLOC0, DFF0 = 8, 2, 4096, 2048, 3072
E1, K1, CAP1, PLOC1, I1 = 64, 4, 1024, 512, 512

OOB = float(1 << 30)


class L:

    def __init__(self, name):
        self.nc = bass.Bass("TRN2", target_bir_lowering=False, debug=False)
        self.ctx = ExitStack()
        self.name = name
        self._sems = {}
        self.cnt = {}

    def dram(self, name, shape, dt, kind="Internal"):
        return self.nc.dram_tensor(name, shape, dt, kind=kind).ap()

    def sb(self, name, shape, dt):
        return self.ctx.enter_context(self.nc.sbuf_tensor(name, shape, dt))

    def ps(self, name, shape, dt=F32):
        return self.ctx.enter_context(self.nc.psum_tensor(name, shape, dt))

    def sem(self, name):
        h = self.ctx.enter_context(self.nc.semaphore(name))
        self._sems[name] = h
        self.cnt[name] = 0
        return h

    def s(self, name):
        return self._sems[name]

    def inc(self, instr, name, k=1):
        instr.then_inc(self._sems[name], k)
        self.cnt[name] += k
        return self.cnt[name]

    def dinc(self, instr, name):
        return self.inc(instr, name, 16)

    def open_block(self):
        self.block = self.ctx.enter_context(self.nc.Block())
        for nm in ("qLD", "qSC", "qGP", "pe", "dv", "ac", "po", "st",
                   "qX", "qW", "qN", "qC"):
            self.sem(nm)



def emit_consts(l, identb_h, trib_h, identf_h=None, warm=("Ln", "Exp", "Square")):
    block = l.block
    wrm = l.sb("warm_sb", [128, 8], F32)

    @block.scalar
    def _(a):
        for fn in warm:
            l.inc(a.activation(wrm[:], wrm[:], getattr(AF, fn)), "ac")
    identb = l.sb("identb_sb", [128, 128], BF16)
    tri_us = l.sb("tri_us", [128, 128], BF16)
    ones_b = l.sb("ones_b", [128, 128], BF16)
    onescol_b = l.sb("onescol_b", [128, 1], BF16)
    onesrow_b = l.sb("onesrow_b", [1, 128], BF16)
    identf = l.sb("identf_sb", [128, 128], F32) if identf_h is not None else None

    @block.sync
    def _(s):
        l.dinc(s.dma_start(out=identb[:], in_=identb_h[:]), "qC")
        l.dinc(s.dma_start(out=tri_us[:], in_=trib_h[:]), "qC")
        if identf_h is not None:
            l.dinc(s.dma_start(out=identf[:], in_=identf_h[:]), "qC")
    ld = l.cnt["qC"]

    @block.vector
    def _(v):
        v.wait_ge(l.s("qC"), ld)
        l.inc(v.memset(ones_b[:], 1.0), "dv")
        l.inc(v.memset(onescol_b[:], 1.0), "dv")
        l.inc(v.memset(onesrow_b[:], 1.0), "dv")
    return dict(identf=identf, identb=identb, tri_us=tri_us, ones_b=ones_b,
                onescol_b=onescol_b, onesrow_b=onesrow_b)


def emit_ln(l, x_sb, xn_sb, stat, ready=(), pad_out=None, dbg=None, G=NT,
            scratch=None):
    block = l.block
    sx = stat[:, 0:G]
    sxx = stat[:, G:2 * G]
    mu = stat[:, 2 * G:3 * G]
    rstd = stat[:, 3 * G:4 * G]
    if scratch is None:
        dmp_a = l.ps(f"lndmp_a{l.cnt['ac']}", [128, D], F32)
        lnps = l.sb(f"lnps{l.cnt['dv']}", [128, 5, G], F32)
    else:
        dmp_a, lnps = scratch
    scr = lnps[:, 0, :]


    @block.vector
    def _(v):
        def fin(instr):
            n = l.inc(instr, "dv")
            v.wait_ge(l.s("dv"), n)
        for nm, vv in ready:
            v.wait_ge(l.s(nm), vv)
        fin(v.memset(stat[:], -1.0e9))
        for i in range(G):
            l.inc(v.tensor_scalar(xn_sb[:, i, :], x_sb[:, i, :], 1.0, 0.0, ALU.mult,
                                  ALU.add, accum_out=sx[:, i:i + 1]), "dv")
            l.inc(v.tensor_tensor(dmp_a[:], x_sb[:, i, :], x_sb[:, i, :], ALU.mult), "dv")
            l.inc(v.tensor_scalar(xn_sb[:, i, :], dmp_a[:], 1.0, 0.0, ALU.mult,
                                  ALU.add, accum_out=sxx[:, i:i + 1]), "dv")
        v.wait_ge(l.s("dv"), l.cnt["dv"])
        fin(v.tensor_scalar(mu[:], sx[:], 1.0 / D, None, ALU.mult))
        fin(v.tensor_tensor(scr[:], mu[:], mu[:], ALU.mult))
        fin(v.scalar_tensor_tensor(rstd[:], sxx[:], 1.0 / D, scr[:],
                                   ALU.mult, ALU.subtract))
        if dbg is not None:
            fin(v.tensor_copy(dbg[:, 0:G], rstd))
    dv_var = l.cnt["dv"]

    scr2 = lnps[:, 1, :]
    scr3 = lnps[:, 2, :]
    scr5 = scr
    scr4 = lnps[:, 3, :]
    dmp_pad = dmp_a

    @block.vector
    def _(v):
        def fin(instr):
            n = l.inc(instr, "dv")
            v.wait_ge(l.s("dv"), n)
        fin(v.memset(scr2[:], -1.0e9))
        fin(v.memset(scr3[:], -1.0e9))
        fin(v.memset(scr4[:], -1.0e9))
        fin(v.tensor_scalar(scr2[:], rstd, 1e-5, None, ALU.add))
        fin(v.tensor_scalar(scr3[:], scr2[:], 1.0, None, ALU.add))
        fin(v.reciprocal(scr4[:], scr3[:]))
        if dbg is not None:
            fin(v.tensor_copy(dbg[:, G:2 * G], scr4[:]))
        rA, rB = scr4, lnps[:, 4, :]
        for it in range(24):
            fin(v.tensor_tensor(scr3[:], rA[:], scr2[:], ALU.mult))
            fin(v.tensor_tensor(scr5[:], rA[:], scr3[:], ALU.mult))
            fin(v.tensor_scalar(scr3[:], scr5[:], -0.5, 1.5, ALU.mult, ALU.add))
            fin(v.tensor_tensor(rB[:], rA[:], scr3[:], ALU.mult))
            rA, rB = rB, rA
        fin(v.tensor_copy(rstd, rA[:]))
    ac_rstd = l.cnt["ac"]

    @block.vector
    def _(v):
        v.wait_ge(l.s("ac"), ac_rstd)
        for i in range(G):
            l.inc(v.tensor_scalar(xn_sb[:, i, :], x_sb[:, i, :], mu[:, i:i + 1],
                                  rstd[:, i:i + 1], ALU.subtract, ALU.mult), "dv")
    if pad_out is not None:
        pad_out.append(dmp_a)
    return l.cnt["dv"]


def emit_xn_store(l, xn_sb, xn_dram, dv_ready):
    block = l.block

    @block.sync
    def _(s):
        s.wait_ge(l.s("dv"), dv_ready)
        xr = xn_dram.rearrange("(i p) d -> p i d", p=128)
        for q in range(4):
            l.dinc(s.dma_start(out=xr[:, q * 8:(q + 1) * 8, :],
                               in_=xn_sb[:, q * 8:(q + 1) * 8, :]), "qN")
    return l.cnt["qN"]


def emit_prep(l, io, EE, KK, PLOC, xn_sb, xn_dram, consts, xn_ready, tag=""):
    nc, block = l.nc, l.block

    wg_b = l.sb(f"wg_b{tag}", [128, 8, EE], BF16)
    pad_b = io["pad"]

    @block.gpsimd
    def _(g):
        l.dinc(g.dma_start(out=wg_b[:],
                           in_=io["wgate"].rearrange("(c p) e -> p c e", p=128)), "qW")
    ld_wg = l.cnt["qW"]

    @block.vector
    def _(v):
        v.wait_ge(l.s("qW"), ld_wg)
        l.inc(v.tensor_copy(pad_b[:], xn_sb[:, 0, 0:256]), "dv")
    dv_wg = l.cnt["dv"]

    SL = 128 if EE > 8 else 256
    NSL, TPJ = T // SL, SL // 128
    xnt_buf = l.sb(f"xnt{tag}", [128, 2, 8, SL], BF16)
    lg_buf = l.sb(f"lgbuf{tag}", [EE, 2, SL], BF16)
    lg_tmS = l.sb(f"lg_tmS{tag}", [128, 2, TPJ, EE], BF16)
    lgt = l.sb(f"lgt{tag}", [128, 2, EE], BF16)
    lg_dram = l.dram(f"lg_scr{tag}", [NT, 128, EE], BF16)
    lg_ps = l.ps(f"lg_ps{tag}", [EE, SL])
    tp_ps = l.ps(f"tp_ps{tag}", [128, EE], BF16)
    lgst_after = {}
    lgld_after = {}
    mmS, evS, tpS, tpE = (f"mmS{tag}", f"evS{tag}", f"tpS{tag}", f"tpE{tag}")
    for _nm in (mmS, evS, tpS, tpE):
        l.sem(_nm)
    xp = [f"xp0{tag}", f"xp1{tag}"]
    for _nm in xp:
        l.sem(_nm)

    @block.scalar
    def _(a):
        a.wait_ge(l.s("qN"), xn_ready)
        for ts_ in range(NSL):
            if ts_ >= 2:
                a.wait_ge(l.s(mmS), ts_ - 1)
            for dc in range(8):
                l.dinc(a.dma_start(
                    out=xnt_buf[:, ts_ % 2, dc, :],
                    in_=xn_dram[ts_ * SL:(ts_ + 1) * SL, dc * 128:(dc + 1) * 128],
                    transpose=True), xp[ts_ % 2])

    for ts_ in range(NSL):
        @block.tensor
        def _(t, ts_=ts_):
            if ts_ == 0:
                t.wait_ge(l.s("dv"), dv_wg)
            t.wait_ge(l.s(xp[ts_ % 2]), (ts_ // 2 + 1) * 8 * 16)
            if ts_ > 0:
                t.wait_ge(l.s(evS), ts_)
            for dc in range(8):
                mm = t.matmul(lg_ps[:], wg_b[:, dc, :],
                              xnt_buf[:, ts_ % 2, dc, :],
                              start=(dc == 0), stop=(dc == 7))
            l.inc(mm, mmS)

        @block.vector
        def _(v, ts_=ts_):
            v.wait_ge(l.s(mmS), ts_ + 1)
            if ts_ >= 2:
                v.wait_ge(l.s(tpS), (ts_ - 1) * TPJ)
            v.tensor_copy(lg_buf[:, ts_ % 2, :], lg_ps[:])
            l.inc(v.tensor_copy(pad_b[:], xn_sb[:, 0, 0:256]), evS)

        @block.tensor
        def _(t, ts_=ts_):
            t.wait_ge(l.s(evS), ts_ + 1)
            for j in range(TPJ):
                i = ts_ * TPJ + j
                if i > 0:
                    t.wait_ge(l.s(tpE), i)
                l.inc(t.transpose(tp_ps[:], lg_buf[0:EE, ts_ % 2, j * 128:(j + 1) * 128],
                                  consts["identb"][0:EE, 0:EE]), tpS)

        @block.vector
        def _(v, ts_=ts_):
            for j in range(TPJ):
                i = ts_ * TPJ + j
                v.wait_ge(l.s(tpS), i + 1)
                if ts_ >= 2:
                    v.wait_ge(l.s("qLD"), lgst_after[(ts_ - 2) * TPJ + j])
                l.inc(v.tensor_copy(lg_tmS[:, ts_ % 2, j, :], tp_ps[:]), tpE)

        @block.sync
        def _(s, ts_=ts_):
            for j in range(TPJ):
                i = ts_ * TPJ + j
                s.wait_ge(l.s(tpE), i + 1)
                l.dinc(s.dma_start(out=lg_dram[i], in_=lg_tmS[:, ts_ % 2, j, :]),
                       "qLD")
                lgst_after[i] = l.cnt["qLD"]
    dv_lgtm = l.cnt["dv"]

    probs = l.sb(f"probs{tag}", [128, EE], F32)
    sume = l.sb(f"sume{tag}", [128, 3], F32)
    mx8 = l.sb(f"mx8{tag}", [128, 8], F32)
    idx8 = l.sb(f"idx8{tag}", [128, 8], U32)
    oh = l.sb(f"oh{tag}", [128, KK, EE], BF16)
    ind = l.sb(f"ind{tag}", [128, EE], BF16)
    ind2 = l.sb(f"ind2{tag}", [128, EE], BF16)
    ind3 = l.sb(f"ind3{tag}", [128, EE], BF16)
    iotaEf = l.sb(f"iotaEf{tag}", [128, EE], F32)
    ek = l.sb(f"ek{tag}", [128, NT, KK], F32)
    posk = l.sb(f"posk{tag}", [128, NT, KK], F32)
    wk = l.sb(f"wk{tag}", [128, NT, KK], F32)
    sidx_u = l.sb(f"sidx_u{tag}", [128, NT * KK], U32)
    scrE2 = l.sb(f"scrE2{tag}", [128, EE], F32)
    runoff_all = l.sb(f"runoff{tag}", [128, 2 * EE], F32)
    runoff2 = lambda a: runoff_all[:, a * EE:(a + 1) * EE]

    routeps = l.ps(f"routeps{tag}", [128, 3 * EE + NT * KK], F32)
    probs2 = routeps[:, 0:EE]
    pos_t = routeps[:, EE:2 * EE]
    scrE = routeps[:, 2 * EE:3 * EE]
    sidx = routeps[:, 3 * EE:3 * EE + NT * KK]

    psAB = l.ps(f"psAB{tag}", [128, 2, EE])
    psA = psAB[:, 0, :]
    psB = psAB[:, 1, :]

    dbg_sm = None
    pe_pos_prev = [None]

    @block.gpsimd
    def _(g):
        l.inc(g.iota(iotaEf[:], pattern=[[1, EE]], base=0, channel_multiplier=0,
                     allow_small_or_imprecise_dtypes=True), "po")
    po_i0 = l.cnt["po"]

    @block.vector
    def _(v):
        v.wait_ge(l.s("po"), po_i0)
        l.inc(v.memset(runoff_all[:], 0.0), "dv")
    po_init = l.cnt["po"]

    def fin(v, instr):
        n = l.inc(instr, "dv")
        v.wait_ge(l.s("dv"), n)
        return n

    ac_exp_hist = {}
    for i in range(NT):
        @block.sync
        def _(s, i=i):
            s.wait_ge(l.s("qLD"), lgst_after[i])
            if i >= 2:
                s.wait_ge(l.s("ac"), ac_exp_hist[i - 2])
            l.dinc(s.dma_start(out=lgt[:, i % 2, :], in_=lg_dram[i]), "qSC")
        lgld = l.cnt["qSC"]

        @block.vector
        def _(v, i=i):
            if i == 0:
                v.wait_ge(l.s("po"), po_init)
            if pe_pos_prev[0] is not None:
                v.wait_ge(l.s("pe"), pe_pos_prev[0])
            pass
        dv_mx = l.cnt["dv"]

        @block.scalar
        def _(a, i=i):
            a.wait_ge(l.s("qSC"), lgld)
            a.wait_ge(l.s("dv"), dv_mx)
            l.inc(a.activation(probs[:], lgt[:, i % 2, :], AF.Exp,
                               accum_out=sume[:, 0:1]), "ac")
            l.inc(a.activation(sume[:, 2:3], sume[:, 0:1], AF.Copy), "ac")
        ac_exp = l.cnt["ac"]
        ac_exp_hist[i] = ac_exp

        @block.vector
        def _(v, i=i):
            v.wait_ge(l.s("ac"), ac_exp)
            fin(v, v.reciprocal(sume[:, 1:2], sume[:, 2:3]))
            fin(v, v.tensor_scalar(probs2[:], probs[:], sume[:, 1:2], None, ALU.mult))
            fin(v, v.max(mx8[:], probs2[:]))
            fin(v, v.max_index(idx8[:], mx8[:], probs2[:]))
            for k in range(KK):
                fin(v, v.tensor_copy(ek[:, i, k:k + 1], idx8[:, k:k + 1]))
                fin(v, v.tensor_copy(wk[:, i, k:k + 1], mx8[:, k:k + 1]))
            for k in range(KK):
                fin(v, v.tensor_scalar(oh[:, k, :], iotaEf[:], ek[:, i, k:k + 1], None,
                                       ALU.is_equal))
            if KK == 2:
                fin(v, v.tensor_tensor(ind[:], oh[:, 0, :], oh[:, 1, :], ALU.add))
            elif KK == 4:
                fin(v, v.tensor_tensor(ind2[:], oh[:, 0, :], oh[:, 1, :], ALU.add))
                fin(v, v.tensor_tensor(ind3[:], oh[:, 2, :], oh[:, 3, :], ALU.add))
                fin(v, v.tensor_tensor(ind[:], ind2[:], ind3[:], ALU.add))
            else:
                raise NotImplementedError
            l.inc(v.tensor_copy(pad_b[:], xn_sb[:, 0, 0:256]), "dv")
        dv_ind = l.cnt["dv"]

        @block.tensor
        def _(t, i=i):
            t.wait_ge(l.s("dv"), dv_ind)
            t.matmul(psA[:], consts["tri_us"][:], ind[:], start=True, stop=True)
            l.inc(t.matmul(psB[:], consts["ones_b"][:], ind[:], start=True, stop=True),
                  "pe")
        pe_pos = l.cnt["pe"]
        pe_pos_prev[0] = pe_pos

        @block.vector
        def _(v, i=i):
            v.wait_ge(l.s("pe"), pe_pos)
            p_, q_ = i % 2, 1 - i % 2
            fin(v, v.tensor_tensor(pos_t[:], psA[:], runoff2(p_)[:], ALU.add))
            fin(v, v.scalar_tensor_tensor(runoff2(q_)[:], psB[:], 1.0,
                                          runoff2(p_)[:], ALU.mult, ALU.add))

            for k in range(KK):
                fin(v, v.tensor_tensor(scrE[:], pos_t[:], oh[:, k, :], ALU.mult))
                fin(v, v.tensor_scalar(scrE2[:], scrE[:], 1.0, 0.0, ALU.mult,
                                       ALU.add, accum_out=posk[:, i, k:k + 1]))
                col = i * KK + k
                fin(v, v.scalar_tensor_tensor(sidx[:, col:col + 1], ek[:, i, k:k + 1],
                                              float(PLOC), posk[:, i, k:k + 1],
                                              ALU.mult, ALU.add))
                fin(v, v.tensor_scalar(scrE2[:, 0:1], posk[:, i, k:k + 1], float(PLOC),
                                       None, ALU.is_ge))
                fin(v, v.scalar_tensor_tensor(sidx[:, col:col + 1], scrE2[:, 0:1],
                                              OOB, sidx[:, col:col + 1],
                                              ALU.mult, ALU.add))
                fin(v, v.tensor_copy(sidx_u[:, col:col + 1], sidx[:, col:col + 1]))

    cnt_sb = l.sb(f"cnt_sb{tag}", [1, EE], F32)

    @block.vector
    def _(v):
        v.tensor_copy(cnt_sb[:], runoff2(NT % 2)[0:1, :])
        l.inc(v.tensor_copy(pad_b[:], xn_sb[:, 0, 0:256]), "dv")
    dv_all = l.cnt["dv"]
    dv_cnt = l.cnt["dv"]

    @block.gpsimd
    def _(g):
        g.wait_ge(l.s("dv"), dv_all)
        breg = g.alloc_register()
        g.reg_mov(breg, EE * PLOC - 1)
        for i in range(NT):
            for k in range(KK):
                col = i * KK + k
                l.dinc(g.indirect_dma_start(
                    out=io["slotTab"][:, :],
                    out_offset=IndirectOffsetOnAxis(sidx_u[:, col:col + 1], 0),
                    in_=xn_sb[:, i, :], in_offset=None,
                    bounds_check=breg, oob_is_err=False), "qGP")
        g.free_register(breg)

    @block.sync
    def _(s):
        s.wait_ge(l.s("dv"), dv_cnt)
        l.dinc(s.dma_start(out=io["meta"].rearrange("p (i k) -> p i k", i=NT), in_=ek[:]), "st")
        l.dinc(s.dma_start(out=io["meta2"].rearrange("p (i k) -> p i k", i=NT), in_=posk[:]), "st")
        l.dinc(s.dma_start(out=io["meta3"].rearrange("p (i k) -> p i k", i=NT), in_=wk[:]), "st")
        l.dinc(s.dma_start(out=io["counts"], in_=cnt_sb[:]), "st")
        if "dbg_stat" in io:
            l.dinc(s.dma_start(out=io["dbg_stat"], in_=io["_stat"][:]), "st")


def emit_tail(l):
    block = l.block

    @block.sync
    def _(s):
        s.wait_ge(l.s("st"), l.cnt["st"])
        s.wait_ge(l.s("qGP"), l.cnt["qGP"])



def build_prep_launch(EE, KK, PLOC, debug=False):
    l = L("prep")
    xin = l.dram("xin", [T, D], F32, kind="ExternalInput")
    wgate = l.dram("wgate", [D, EE], F32, kind="ExternalInput")
    identb_h = l.dram("identb", [128, 128], BF16, kind="ExternalInput")
    trib_h = l.dram("trib", [128, 128], BF16, kind="ExternalInput")
    io = dict(
        slotTab=l.dram("slotTab", [EE * PLOC, D], BF16, kind="ExternalOutput"),
        counts=l.dram("counts", [1, EE], F32, kind="ExternalOutput"),
        meta=l.dram("meta", [128, NT * KK], F32, kind="ExternalOutput"),
        meta2=l.dram("meta2", [128, NT * KK], F32, kind="ExternalOutput"),
        meta3=l.dram("meta3", [128, NT * KK], F32, kind="ExternalOutput"),
        wgate=wgate,
    )
    xn_dram = l.dram("xn_scr", [T, D], BF16,
                     kind="ExternalOutput" if debug else "Internal")
    if debug:
        io["dbg_stat"] = l.dram("dbg_stat", [128, 4 * NT], F32, kind="ExternalOutput")
        io["dbg_r0"] = l.dram("dbg_r0", [128, 2 * NT], F32, kind="ExternalOutput")

    x_sb = l.sb("x_sb", [128, NT, D], F32)
    xn_sb = l.sb("xn_sb", [128, NT, D], BF16)
    stat = l.sb("stat", [128, 4 * NT], F32)
    dbg_sb = l.ps("dbg_ps", [128, 2 * NT]) if debug else None

    l.open_block()
    block = l.block

    @block.sync
    def _(s):
        xr = xin.rearrange("(i p) d -> p i d", p=128)
        for q in range(4):
            l.dinc(s.dma_start(out=x_sb[:, q * 8:(q + 1) * 8, :],
                               in_=xr[:, q * 8:(q + 1) * 8, :]), "qX")
    ld_x = l.cnt["qX"]

    consts = emit_consts(l, identb_h, trib_h)
    pads = []
    dv_xn = emit_ln(l, x_sb, xn_sb, stat, ready=[("qX", ld_x)], pad_out=pads,
                    dbg=dbg_sb)
    xn_ready = emit_xn_store(l, xn_sb, xn_dram, dv_xn)
    io["_stat"] = stat
    io["_dbg_sb"] = dbg_sb
    io["pad"] = pads[0][:, 0:256]
    emit_prep(l, io, EE, KK, PLOC, xn_sb, xn_dram, consts, xn_ready)
    if debug:
        dv_now = l.cnt["dv"]

        @block.vector
        def _(v):
            v.wait_ge(l.s("dv"), dv_now)
            l.inc(v.tensor_copy(x_sb[:, 0, 0:2 * NT], dbg_sb[:]), "dv")
        dv_dbg = l.cnt["dv"]

        @block.sync
        def _(s):
            s.wait_ge(l.s("dv"), dv_dbg)
            l.dinc(s.dma_start(out=io["dbg_r0"], in_=x_sb[:, 0, 0:2 * NT]), "st")
    emit_tail(l)
    return l


def build_expert_launch(NEXP_L, CAP, PLOC, DFF, debug=False):
    TCH = 512
    NTC = CAP // 128
    NCH = CAP // TCH
    NH2 = DFF // 256
    NROWS = NEXP_L * 8 * PLOC
    WPAR = 1 if NEXP_L == 1 else 2
    NGC = NEXP_L * NCH

    l = L("exp")
    tabs = l.dram("tabs", [NROWS + 1, D], BF16, kind="ExternalInput")
    cnts_h = l.dram("cnts", [1, NEXP_L * 8], F32, kind="ExternalInput")
    wfc_h = l.dram("wfc", [NEXP_L, D, DFF], F32, kind="ExternalInput")
    wpr_h = l.dram("wpr", [NEXP_L, DFF // 2, D], F32, kind="ExternalInput")
    i128_h = l.dram("i128", [1, NTC], F32, kind="ExternalInput")
    y_h = l.dram("yout", [NEXP_L, D, CAP], BF16, kind="ExternalOutput")
    xg_dram = l.dram("xg_scr", [NEXP_L, CAP, D], BF16,
                     kind="ExternalOutput" if debug else "Internal")

    xg = l.sb("xg", [128, WPAR * NTC, D], BF16)
    wfc = l.sb("wfc_sb", [128, WPAR * 8, DFF], BF16)
    wpr = l.sb("wpr_sb", [128, WPAR * NH2, D], BF16)
    xT = l.sb("xT", [128, 2, 8, TCH], BF16)
    hsb = l.sb("hsb", [128, 2, NH2, TCH], BF16)
    yT = l.sb("yT", [128, 2, 8, TCH], BF16)
    silA = l.sb("silA", [128, 2, TCH], F32)
    cnt_sb = l.sb("cnt_sb", [1, NEXP_L * 8], F32)
    catrow = l.sb("catrow", [1, 16 + NTC], F32)
    ones8 = l.sb("ones8", [1, 8], F32)
    onesf = l.sb("onesf", [1, 128], F32)
    iotaP = l.sb("iotaP", [128, 1], U32)
    iotaPf = l.sb("iotaPf", [128, 1], F32)
    cumb = l.sb("cumb", [128, 16 + NTC], F32)
    exclb = l.sb("exclb", [128, 8], F32)
    svb = l.sb("svb", [128, NTC], F32)
    accA = l.sb("accA", [128, NTC], F32)
    accB = l.sb("accB", [128, NTC], F32)
    eqb = l.sb("eqb", [128, NTC], F32)
    rowA = l.sb("rowA", [128, NTC], F32)
    rowB = l.sb("rowB", [128, NTC], F32)
    oobm = l.sb("oobm", [128, NTC], F32)
    rowu = l.sb("rowu", [128, NEXP_L * NTC], U32)
    rowf = l.sb("rowf", [128, NEXP_L * NTC], F32)
    wrm = l.sb("wrm", [128, 8], F32)

    l.open_block()
    block = l.block
    l.sem("trS")

    psH = [l.ps(f"psH{j}", [128, TCH]) for j in range(4)]
    psY = [l.ps(f"psY{j}", [128, TCH]) for j in range(2)]
    psCum = l.ps("psCum", [128, 16 + NTC])

    @block.scalar
    def _(a):
        l.inc(a.activation(wrm[:], wrm[:], AF.Silu), "ac")

    @block.sync
    def _(s):
        l.dinc(s.dma_start(out=cnt_sb[:], in_=cnts_h[:]), "qC")
        l.dinc(s.dma_start(out=catrow[:, 16:16 + NTC], in_=i128_h[:]), "qC")
    ld_c = l.cnt["qC"]

    @block.gpsimd
    def _(g):
        l.inc(g.iota(iotaP[:], pattern=[[0, 1]], base=0, channel_multiplier=1), "po")
    po_iota = l.cnt["po"]

    @block.vector
    def _(v):
        def fin(i):
            n = l.inc(i, "dv")
            v.wait_ge(l.s("dv"), n)
        v.wait_ge(l.s("po"), po_iota)
        v.wait_ge(l.s("qC"), ld_c)
        fin(v.memset(ones8[:], 1.0))
        fin(v.memset(onesf[:], 1.0))
        fin(v.tensor_copy(iotaPf[:], iotaP[:]))

    dv_gather = {}
    for e in range(NEXP_L):
        @block.vector
        def _(v, e=e):
            def fin(i):
                n = l.inc(i, "dv")
                v.wait_ge(l.s("dv"), n)
            fin(v.tensor_copy(catrow[:, 8:16], cnt_sb[:, e * 8:(e + 1) * 8]))
            fin(v.tensor_tensor_scan(catrow[:, 0:8], ones8[:], catrow[:, 8:16],
                                     0.0, ALU.mult, ALU.add))

        @block.tensor
        def _(t, e=e):
            t.wait_ge(l.s("dv"), l.cnt["dv"])
            l.inc(t.matmul(psCum[:], onesf[:], catrow[:], start=True, stop=True),
                  "pe")
        pe_cum = l.cnt["pe"]

        @block.vector
        def _(v, e=e):
            def fin(i):
                n = l.inc(i, "dv")
                v.wait_ge(l.s("dv"), n)
            v.wait_ge(l.s("pe"), pe_cum)
            fin(v.tensor_copy(cumb[:], psCum[:]))
            fin(v.tensor_tensor(exclb[:], cumb[:, 0:8], cumb[:, 8:16], ALU.subtract))
            fin(v.tensor_scalar(svb[:], cumb[:, 16:16 + NTC], iotaPf[:, 0:1],
                                None, ALU.add))
            fin(v.memset(accA[:], 0.0))
            a, b = accA, accB
            for sg in range(8):
                fin(v.scalar_tensor_tensor(b[:], svb[:], cumb[:, sg:sg + 1], a[:],
                                           ALU.is_ge, ALU.add))
                a, b = b, a
            sig = a
            fin(v.memset(rowA[:], 0.0))
            a2, b2 = rowA, rowB
            for sg in range(8):
                fin(v.tensor_scalar(eqb[:], sig[:], float(sg), None, ALU.is_equal))
                fin(v.scalar_tensor_tensor(b2[:], eqb[:], exclb[:, sg:sg + 1], a2[:],
                                           ALU.mult, ALU.add))
                a2, b2 = b2, a2
            base = a2
            fin(v.scalar_tensor_tensor(b2[:], sig[:], float(PLOC), svb[:],
                                       ALU.mult, ALU.add))
            fin(v.tensor_tensor(eqb[:], b2[:], base[:], ALU.subtract))
            fin(v.tensor_scalar(oobm[:], svb[:], cumb[:, 7:8], None, ALU.is_ge))
            fin(v.scalar_tensor_tensor(b2[:], oobm[:], float(1 << 22), eqb[:],
                                       ALU.mult, ALU.add))
            fin(v.tensor_scalar(eqb[:], b2[:], float(e * 8 * PLOC), None, ALU.add))
            fin(v.tensor_scalar_min(rowf[:, e * NTC:(e + 1) * NTC], eqb[:],
                                    float(NROWS)))
            fin(v.tensor_copy(rowu[:, e * NTC:(e + 1) * NTC],
                              rowf[:, e * NTC:(e + 1) * NTC]))
        dv_gather[e] = l.cnt["dv"]

    wq_after = {}
    gp_after = {}
    qn_after = {}
    trs_after = {}
    peG = {}
    peY = {}
    peY_last = {}
    acS = {}
    gate_dv = {}
    gate_all = {}
    evac_dv = {}
    st_after = {}
    last_a_sil = {0: 0, 1: 0}
    last_g_gate = {0: 0, 1: 0}
    last_sil_gate = {0: 0, 1: 0}
    last_evac = {0: 0, 1: 0}
    pe_all_e = {}

    def ech(gc):
        return gc // NCH, gc % NCH

    def emit_gpsimd(e):
        @block.gpsimd
        def _(g, e=e):
            wp = e % WPAR
            if WPAR == 2 and e >= 2:
                g.wait_ge(l.s("pe"), pe_all_e[e - 2])
            for ft in range(8):
                l.dinc(g.dma_start(out=wfc[:, wp * 8 + ft, :],
                                   in_=wfc_h[e, ft * 128:(ft + 1) * 128, :]), "qW")
            for hp in range(NH2):
                l.dinc(g.dma_start(out=wpr[:, wp * NH2 + hp, :],
                                   in_=wpr_h[e, hp * 128:(hp + 1) * 128, :]), "qW")
            wq_after[e] = l.cnt["qW"]
            g.wait_ge(l.s("dv"), dv_gather[e])
            if WPAR == 2 and e >= 2:
                g.wait_ge(l.s("qN"), qn_after[(e - 2, NCH - 1)])
            breg = g.alloc_register()
            g.reg_mov(breg, NROWS)
            for i in range(NTC):
                l.dinc(g.indirect_dma_start(
                    out=xg[:, wp * NTC + i, :], out_offset=None,
                    in_=tabs[:, :],
                    in_offset=IndirectOffsetOnAxis(
                        rowu[:, e * NTC + i:e * NTC + i + 1], 0),
                    bounds_check=breg, oob_is_err=False), "qGP")
            g.free_register(breg)
            gp_after[e] = l.cnt["qGP"]

    def emit_bounce(e):
        @block.sync
        def _(s, e=e):
            s.wait_ge(l.s("qGP"), gp_after[e])
            wp = e % WPAR
            xr = xg_dram[e].rearrange("(i p) d -> p i d", p=128)
            for ch in range(NCH):
                l.dinc(s.dma_start(
                    out=xr[:, ch * 4:(ch + 1) * 4, :],
                    in_=xg[:, wp * NTC + ch * 4:wp * NTC + (ch + 1) * 4, :]), "qN")
                qn_after[(e, ch)] = l.cnt["qN"]

    def emit_ystore(gc):
        e, ch = ech(gc)

        @block.sync
        def _(s, gc=gc, e=e, ch=ch):
            s.wait_ge(l.s("dv"), evac_dv[(gc, 7)])
            yr = y_h[e].rearrange("(dt p) t -> p dt t", p=128)
            l.dinc(s.dma_start(out=yr[:, :, ch * TCH:(ch + 1) * TCH],
                               in_=yT[:, gc % 2, :, :]), "st")
            st_after[gc] = l.cnt["st"]

    def emit_transp(gc):
        e, ch = ech(gc)

        @block.scalar
        def _(a, gc=gc, e=e, ch=ch):
            a.wait_ge(l.s("qN"), qn_after[(e, ch)])
            for ft in range(8):
                l.dinc(a.dma_start(
                    out=xT[:, gc % 2, ft, :],
                    in_=xg_dram[e, ch * TCH:(ch + 1) * TCH, ft * 128:(ft + 1) * 128],
                    transpose=True), "trS")
            trs_after[gc] = l.cnt["trS"]

    def emit_pe_pair(gc, mh):
        e, ch = ech(gc)
        wp = e % WPAR
        pb = mh % 2

        @block.tensor
        def _(t, gc=gc, mh=mh, wp=wp, pb=pb, ch=ch, e=e):
            if mh == 0:
                t.wait_ge(l.s("trS"), trs_after[gc])
                if ch == 0:
                    t.wait_ge(l.s("qW"), wq_after[e])
            if last_a_sil[pb]:
                t.wait_ge(l.s("ac"), last_a_sil[pb])
            if last_g_gate[pb]:
                t.wait_ge(l.s("dv"), last_g_gate[pb])
            for ft in range(8):
                t.matmul(psH[pb][:], wfc[:, wp * 8 + ft, mh * 128:(mh + 1) * 128],
                         xT[:, gc % 2, ft, :], start=(ft == 0), stop=(ft == 7))
            for ft in range(8):
                mm = t.matmul(psH[2 + pb][:],
                              wfc[:, wp * 8 + ft,
                                  DFF // 2 + mh * 128:DFF // 2 + (mh + 1) * 128],
                              xT[:, gc % 2, ft, :], start=(ft == 0), stop=(ft == 7))
            l.inc(mm, "pe")
            peG[(gc, mh)] = l.cnt["pe"]

    def emit_silu(gc, mh):
        pb = mh % 2

        @block.scalar
        def _(a, gc=gc, mh=mh, pb=pb):
            a.wait_ge(l.s("pe"), peG[(gc, mh)])
            if last_sil_gate[pb]:
                a.wait_ge(l.s("dv"), last_sil_gate[pb])
            l.inc(a.activation(silA[:, pb, :], psH[pb][:], AF.Silu), "ac")
        acS[(gc, mh)] = l.cnt["ac"]
        last_a_sil[pb] = l.cnt["ac"]

    def emit_gate(gc, mh):
        pb = mh % 2

        @block.vector
        def _(v, gc=gc, mh=mh, pb=pb):
            v.wait_ge(l.s("ac"), acS[(gc, mh)])
            if mh == 0 and gc >= 2:
                v.wait_ge(l.s("pe"), peY_last[gc - 2])
            l.inc(v.tensor_tensor(hsb[:, gc % 2, mh, :], silA[:, pb, :],
                                  psH[2 + pb][:], ALU.mult), "dv")
        gate_dv[(gc, mh)] = l.cnt["dv"]
        last_g_gate[pb] = l.cnt["dv"]
        last_sil_gate[pb] = l.cnt["dv"]

    def emit_pe_y(gc, dt):
        e, ch = ech(gc)
        wp = e % WPAR
        yb = dt % 2

        @block.tensor
        def _(t, gc=gc, dt=dt, wp=wp, yb=yb):
            if dt == 0:
                t.wait_ge(l.s("dv"), gate_all[gc])
            if last_evac[yb]:
                t.wait_ge(l.s("dv"), last_evac[yb])
            for hp in range(NH2):
                mm = t.matmul(psY[yb][:],
                              wpr[:, wp * NH2 + hp, dt * 128:(dt + 1) * 128],
                              hsb[:, gc % 2, hp, :],
                              start=(hp == 0), stop=(hp == NH2 - 1))
            l.inc(mm, "pe")
        peY[(gc, dt)] = l.cnt["pe"]
        if dt == 7:
            peY_last[gc] = l.cnt["pe"]
            pe_all_e[ech(gc)[0]] = l.cnt["pe"]

    def emit_evac(gc, dt):
        yb = dt % 2

        @block.vector
        def _(v, gc=gc, dt=dt, yb=yb):
            v.wait_ge(l.s("pe"), peY[(gc, dt)])
            if dt == 0 and gc >= 2:
                v.wait_ge(l.s("st"), st_after[gc - 2])
            l.inc(v.tensor_copy(yT[:, gc % 2, dt, :], psY[yb][:]), "dv")
        evac_dv[(gc, dt)] = l.cnt["dv"]
        last_evac[yb] = l.cnt["dv"]

    emit_gpsimd(0)
    emit_bounce(0)
    if NEXP_L > 1:
        emit_gpsimd(1)
        emit_bounce(1)
    emit_transp(0)
    if NGC > 1:
        emit_transp(1)
    for gc in range(NGC):
        emit_pe_pair(gc, 0)
        emit_pe_pair(gc, 1)
        for mh in range(2, NH2):
            emit_silu(gc, mh - 2)
            emit_gate(gc, mh - 2)
            emit_pe_pair(gc, mh)
        for mh in (NH2 - 2, NH2 - 1):
            emit_silu(gc, mh)
            emit_gate(gc, mh)
        gate_all[gc] = l.cnt["dv"]
        emit_pe_y(gc, 0)
        emit_pe_y(gc, 1)
        for dt in range(2, 8):
            emit_evac(gc, dt - 2)
            emit_pe_y(gc, dt)
        emit_evac(gc, 6)
        emit_evac(gc, 7)
        emit_ystore(gc)
        if gc + 2 < NGC:
            ee, cch = ech(gc + 2)
            if cch == 0 and ee >= 2:
                emit_gpsimd(ee)
                emit_bounce(ee)
            emit_transp(gc + 2)

    @block.sync
    def _(s):
        s.wait_ge(l.s("st"), l.cnt["st"])
        s.wait_ge(l.s("qN"), l.cnt["qN"])
        s.wait_ge(l.s("qGP"), l.cnt["qGP"])
        s.wait_ge(l.s("trS"), l.cnt["trS"])
        s.wait_ge(l.s("qW"), l.cnt["qW"])
    return l


def build_combine_launch(EE, KK, CAP, drop):
    NC = NT * KK
    l = L("comb")
    xin = l.dram("xin", [T, D], F32, kind="ExternalInput")
    yall_h = l.dram("yall", [EE * CAP, D], BF16, kind="ExternalInput")
    meta_h = l.dram("meta", [128, NC], F32, kind="ExternalInput")
    meta2_h = l.dram("meta2", [128, NC], F32, kind="ExternalInput")
    meta3_h = l.dram("meta3", [128, NC], F32, kind="ExternalInput")
    cntall_h = l.dram("cntall", [8, EE], F32, kind="ExternalInput")
    pmask_h = l.dram("pmask", [8, 1], F32, kind="ExternalInput")
    xout_h = l.dram("xout", [T, D], F32, kind="ExternalOutput")

    x_sb = l.sb("x_sb", [128, NT, D], F32)
    ekall = l.sb("ekall", [128, NC], F32)
    pkall = l.sb("pkall", [128, NC], F32)
    wkall = l.sb("wkall", [128, NC], F32)
    cnt_sb = l.sb("cnt_sb", [8, EE], F32)
    pm_sb = l.sb("pm_sb", [8, 1], F32)
    onesf = l.sb("onesf", [1, 128], F32)
    offrow = l.sb("offrow", [1, EE], F32)
    off_b = l.sb("off_b", [128, EE], F32)
    eqm = l.sb("eqm", [128, NC], F32)
    oaccA = l.sb("oaccA", [128, NC], F32)
    oaccB = l.sb("oaccB", [128, NC], F32)
    posg = l.sb("posg", [128, NC], F32)
    wv = l.sb("wv", [128, NC], F32)
    gidx = l.sb("gidx", [128, NC], F32)
    gidx_u = l.sb("gidx_u", [128, NC], U32)
    yr = l.sb("yr", [128, 4, D], BF16)
    acc = l.sb("acc", [128, 4, D], F32)

    l.open_block()
    block = l.block
    psO = l.ps("psO", [1, EE])
    psB2 = l.ps("psB2", [128, EE])

    @block.sync
    def _(s):
        xr = xin.rearrange("(i p) d -> p i d", p=128)
        for q in range(4):
            l.dinc(s.dma_start(out=x_sb[:, q * 8:(q + 1) * 8, :],
                               in_=xr[:, q * 8:(q + 1) * 8, :]), "qX")
        l.dinc(s.dma_start(out=ekall[:], in_=meta_h[:]), "qC")
        l.dinc(s.dma_start(out=pkall[:], in_=meta2_h[:]), "qC")
        l.dinc(s.dma_start(out=wkall[:], in_=meta3_h[:]), "qC")
        l.dinc(s.dma_start(out=cnt_sb[:], in_=cntall_h[:]), "qC")
        l.dinc(s.dma_start(out=pm_sb[:], in_=pmask_h[:]), "qC")
    ld_x = l.cnt["qX"]
    ld_c = l.cnt["qC"]

    @block.vector
    def _(v):
        v.wait_ge(l.s("qC"), ld_c)
        n = l.inc(v.memset(onesf[:], 1.0), "dv")
        v.wait_ge(l.s("dv"), n)
    dv_ones = l.cnt["dv"]

    @block.tensor
    def _(t):
        t.wait_ge(l.s("dv"), dv_ones)
        l.inc(t.matmul(psO[:], pm_sb[:], cnt_sb[:], start=True, stop=True), "pe")
    pe_off1 = l.cnt["pe"]

    @block.vector
    def _(v):
        v.wait_ge(l.s("pe"), pe_off1)
        n = l.inc(v.tensor_copy(offrow[:], psO[:]), "dv")
        v.wait_ge(l.s("dv"), n)
    dv_offrow = l.cnt["dv"]

    @block.tensor
    def _(t):
        t.wait_ge(l.s("dv"), dv_offrow)
        l.inc(t.matmul(psB2[:], onesf[:], offrow[:], start=True, stop=True), "pe")
    pe_off2 = l.cnt["pe"]

    @block.vector
    def _(v):
        def fin(i):
            n = l.inc(i, "dv")
            v.wait_ge(l.s("dv"), n)
        v.wait_ge(l.s("pe"), pe_off2)
        fin(v.tensor_copy(off_b[:], psB2[:]))
        fin(v.memset(oaccA[:], 0.0))
        a, b = oaccA, oaccB
        for e in range(EE):
            fin(v.tensor_scalar(eqm[:], ekall[:], float(e), off_b[:, e:e + 1],
                                ALU.is_equal, ALU.mult))
            fin(v.tensor_tensor(b[:], a[:], eqm[:], ALU.add))
            a, b = b, a
        fin(v.tensor_tensor(posg[:], a[:], pkall[:], ALU.add))
        fin(v.tensor_scalar_min(b[:], posg[:], float(CAP - 1)))
        fin(v.scalar_tensor_tensor(gidx[:], ekall[:], float(CAP), b[:],
                                   ALU.mult, ALU.add))
        fin(v.tensor_copy(gidx_u[:], gidx[:]))
        if drop:
            fin(v.tensor_scalar(eqm[:], posg[:], float(CAP), None, ALU.is_lt))
            fin(v.tensor_tensor(wv[:], wkall[:], eqm[:], ALU.mult))
        else:
            fin(v.tensor_copy(wv[:], wkall[:]))
    dv_gidx = l.cnt["dv"]

    gp_col = {}
    dv_use = {0: 0, 1: 0, 2: 0, 3: 0}
    dv_tile = {}

    for i in range(NT):
        for k in range(KK):
            col = i * KK + k
            r = col % 4

            @block.gpsimd
            def _(g, col=col, r=r):
                if col == 0:
                    g.wait_ge(l.s("dv"), dv_gidx)
                    breg_l = g.alloc_register()
                    g.reg_mov(breg_l, EE * CAP - 1)
                    l._breg = breg_l
                if dv_use[r]:
                    g.wait_ge(l.s("dv"), dv_use[r])
                l.dinc(g.indirect_dma_start(
                    out=yr[:, r, :], out_offset=None,
                    in_=yall_h[:, :],
                    in_offset=IndirectOffsetOnAxis(gidx_u[:, col:col + 1], 0),
                    bounds_check=l._breg, oob_is_err=False), "qGP")
                if col == NC - 1:
                    g.free_register(l._breg)
            gp_col[col] = l.cnt["qGP"]

            @block.vector
            def _(v, i=i, k=k, col=col, r=r):
                def fin(instr):
                    n = l.inc(instr, "dv")
                    v.wait_ge(l.s("dv"), n)
                v.wait_ge(l.s("qGP"), gp_col[col])
                if i == 0 and k == 0:
                    v.wait_ge(l.s("qX"), ld_x)
                ai = (i % 2) * 2
                if k == 0:
                    fin(v.tensor_scalar(acc[:, ai, :], yr[:, r, :],
                                        wv[:, col:col + 1], None, ALU.mult))
                else:
                    src = ai + (k - 1) % 2
                    dst = ai + k % 2
                    fin(v.scalar_tensor_tensor(acc[:, dst, :], yr[:, r, :],
                                               wv[:, col:col + 1], acc[:, src, :],
                                               ALU.mult, ALU.add))
                dv_use[r] = l.cnt["dv"]
                if k == KK - 1:
                    last = ai + (KK - 1) % 2
                    fin(v.scalar_tensor_tensor(x_sb[:, i, :], acc[:, last, :], 0.5,
                                               x_sb[:, i, :], ALU.mult, ALU.add))
                    dv_tile[i] = l.cnt["dv"]

    @block.sync
    def _(s):
        xr = xout_h.rearrange("(i p) d -> p i d", p=128)
        for q in range(4):
            s.wait_ge(l.s("dv"), dv_tile[q * 8 + 7])
            l.dinc(s.dma_start(out=xr[:, q * 8:(q + 1) * 8, :],
                               in_=x_sb[:, q * 8:(q + 1) * 8, :]), "st")
        s.wait_ge(l.s("st"), l.cnt["st"])
        s.wait_ge(l.s("qGP"), l.cnt["qGP"])
    return l


def build_griffin_launch(debug=False):
    G = 8
    NG = NT // G
    TCH = 512
    NTCH = T // TCH
    l = L("grif")
    xin = l.dram("xin", [T, D], F32, kind="ExternalInput")
    win_h = l.dram("win", [D, D], F32, kind="ExternalInput")
    wr_h = l.dram("wr", [D, D], F32, kind="ExternalInput")
    wi_h = l.dram("wi", [D, D], F32, kind="ExternalInput")
    wout_h = l.dram("wout", [D, D], F32, kind="ExternalInput")
    bin_h = l.dram("binb", [8, 128], F32, kind="ExternalInput")
    br_h = l.dram("brb", [8, 128], F32, kind="ExternalInput")
    bi_h = l.dram("bib", [8, 128], F32, kind="ExternalInput")
    bout_h = l.dram("boutb", [8, 128], F32, kind="ExternalInput")
    loga_h = l.dram("logab", [8, 128], F32, kind="ExternalInput")
    xout_h = l.dram("xout", [T, D], F32, kind="ExternalOutput")
    xn_dram = l.dram("xn_scr", [T, D], BF16, kind="Internal")
    a1fm = l.dram("a1fm", [D, T], BF16,
                  kind="ExternalOutput" if debug else "Internal")

    xq = l.sb("xq", [128, G, D], F32)
    xng = l.sb("xng", [128, 2, G, D], BF16)
    stat = l.sb("stat", [128, 4 * G], F32)
    lnps = l.sb("lnps", [128, 5, G], F32)
    wini = l.sb("wini", [128, 8, D], BF16)
    wri = l.sb("wri", [128, 8, D], BF16)
    wii = l.sb("wii", [128, 8, D], BF16)
    wouti = l.sb("wouti", [128, 8, D], BF16)
    xnT = l.sb("xnT", [128, 2, 8, TCH], BF16)
    bb = l.sb("bb", [128, 4, 8], F32)
    loga_sb = l.sb("loga_sb", [128, 8], F32)
    a_ch = l.sb("a_ch", [128, 8], F32)
    rsb = l.sb("rsb", [128, TCH], F32)
    isb = l.sb("isb", [128, TCH], F32)
    inbb = l.sb("inbb", [128, TCH], F32)
    h1b = l.sb("h1b", [128, TCH], F32)
    cpb = l.sb("cpb", [128, TCH], F32)
    ones512 = l.sb("ones512", [128, TCH], F32)
    hstate = l.sb("hstate", [128, 2, 8], F32)
    hb = l.sb("hb", [128, 2, 8, TCH], BF16)
    evb = l.sb("evb", [128, 8, TCH], BF16)
    a1tm = l.sb("a1tm", [128, 2, D], BF16)
    a1f = l.sb("a1f", [128, D], F32)
    x1t = l.sb("x1t", [128, 2, D], F32)
    x2t = l.sb("x2t", [128, 2, D], F32)
    wrm = l.sb("wrm", [128, 8], F32)

    l.open_block()
    block = l.block
    for nm in ("trS", "stA", "trA", "qX2", "stX"):
        l.sem(nm)
    dmp_a = l.ps("lndmp", [128, D], F32)
    psR = l.ps("psR", [128, TCH])
    psI = l.ps("psI", [128, TCH])
    psIn = l.ps("psIn", [128, TCH])
    psO = [l.ps(f"psO{j}", [128, TCH]) for j in range(2)]

    @block.scalar
    def _(a):
        l.inc(a.activation(wrm[:], wrm[:], AF.Sigmoid), "ac")
    ac_warm = l.cnt["ac"]

    @block.sync
    def _(s):
        for bh, j in ((bin_h, 0), (br_h, 1), (bi_h, 2), (bout_h, 3)):
            l.dinc(s.dma_start(out=bb[:, j, :], in_=bh.rearrange("c p -> p c"),
                               allow_slow_non_contiguous=True), "qC")
        l.dinc(s.dma_start(out=loga_sb[:], in_=loga_h.rearrange("c p -> p c"),
                           allow_slow_non_contiguous=True), "qC")
    ld_c = l.cnt["qC"]

    @block.gpsimd
    def _(g):
        for wh, wsb in ((win_h, wini), (wr_h, wri), (wi_h, wii), (wout_h, wouti)):
            for ft in range(8):
                l.dinc(g.dma_start(out=wsb[:, ft, :],
                                   in_=wh[ft * 128:(ft + 1) * 128, :]), "qW")
    ld_w = l.cnt["qW"]

    @block.scalar
    def _(a):
        a.wait_ge(l.s("qC"), ld_c)
        l.inc(a.activation(a_ch[:], loga_sb[:], AF.Sigmoid), "ac")
    ac_ach = l.cnt["ac"]

    @block.vector
    def _(v):
        n = l.inc(v.memset(ones512[:], 1.0), "dv")
        n = l.inc(v.memset(hstate[:], 0.0), "dv")
        v.wait_ge(l.s("dv"), n)
    dv_init = l.cnt["dv"]

    apply_dv = {}
    xnst_after = {}
    xr_in = xin.rearrange("(i p) d -> p i d", p=128)
    xnr = xn_dram.rearrange("(i p) d -> p i d", p=128)
    for g in range(NG):
        @block.sync
        def _(s, g=g):
            if g >= 1:
                s.wait_ge(l.s("dv"), apply_dv[g - 1])
            l.dinc(s.dma_start(out=xq[:], in_=xr_in[:, g * G:(g + 1) * G, :]), "qX")
        ld_x = l.cnt["qX"]
        ready = [("qX", ld_x)]
        if g >= 2:
            ready.append(("qN", xnst_after[g - 2]))
        apply_dv[g] = emit_ln(l, xq, xng[:, g % 2], stat, ready=ready, G=G,
                              scratch=(dmp_a, lnps))

        @block.sync
        def _(s, g=g):
            s.wait_ge(l.s("dv"), apply_dv[g])
            l.dinc(s.dma_start(out=xnr[:, g * G:(g + 1) * G, :],
                               in_=xng[:, g % 2]), "qN")
            xnst_after[g] = l.cnt["qN"]

    trs_after = {}
    peProj = {}
    peProj_all = {}
    actG = {}
    dveG = {}
    hb_all = {}
    peO = {}
    peO_all = {}
    evbAct = {}
    stA_after = {}

    def emit_transp(tch):
        @block.sync
        def _(s, tch=tch):
            s.wait_ge(l.s("qN"), xnst_after[(tch * TCH) // (G * 128)])
            if tch >= 2:
                s.wait_ge(l.s("pe"), peProj_all[tch - 2])
            for ft in range(8):
                l.dinc(s.dma_start(
                    out=xnT[:, tch % 2, ft, :],
                    in_=xn_dram[tch * TCH:(tch + 1) * TCH, ft * 128:(ft + 1) * 128],
                    transpose=True), "trS")
            trs_after[tch] = l.cnt["trS"]

    emit_transp(0)
    emit_transp(1)
    for tch in range(NTCH):
        for ct in range(8):
            @block.tensor
            def _(t, tch=tch, ct=ct):
                if ct == 0:
                    t.wait_ge(l.s("trS"), trs_after[tch])
                    if tch == 0:
                        t.wait_ge(l.s("qW"), ld_w)
                prev = (tch, ct - 1) if ct > 0 else (tch - 1, 7)
                if prev in actG:
                    t.wait_ge(l.s("ac"), actG[prev])
                if prev in dveG:
                    t.wait_ge(l.s("dv"), dveG[prev])
                for ps, wsb in ((psR, wri), (psI, wii), (psIn, wini)):
                    for ft in range(8):
                        mm = t.matmul(ps[:], wsb[:, ft, ct * 128:(ct + 1) * 128],
                                      xnT[:, tch % 2, ft, :],
                                      start=(ft == 0), stop=(ft == 7))
                l.inc(mm, "pe")
                peProj[(tch, ct)] = l.cnt["pe"]
                if ct == 7:
                    peProj_all[tch] = l.cnt["pe"]

            @block.scalar
            def _(a, tch=tch, ct=ct):
                a.wait_ge(l.s("pe"), peProj[(tch, ct)])
                if tch == 0 and ct == 0:
                    a.wait_ge(l.s("ac"), ac_ach)
                prev = (tch, ct - 1) if ct > 0 else (tch - 1, 7)
                if prev in dveG:
                    a.wait_ge(l.s("dv"), dveG[prev])
                l.inc(a.activation(rsb[:], psR[:], AF.Sigmoid,
                                   bias=bb[:, 1, ct:ct + 1]), "ac")
                l.inc(a.activation(isb[:], psI[:], AF.Sigmoid,
                                   bias=bb[:, 2, ct:ct + 1]), "ac")
                actG[(tch, ct)] = l.cnt["ac"]

            @block.vector
            def _(v, tch=tch, ct=ct):
                def fin(instr):
                    n = l.inc(instr, "dv")
                    v.wait_ge(l.s("dv"), n)
                v.wait_ge(l.s("ac"), actG[(tch, ct)])
                if tch == 0 and ct == 0:
                    v.wait_ge(l.s("dv"), dv_init)
                if ct == 0 and tch >= 2:
                    v.wait_ge(l.s("pe"), peO_all[tch - 2])
                fin(v.tensor_scalar(inbb[:], psIn[:], bb[:, 0, ct:ct + 1], None,
                                    ALU.add))
                fin(v.tensor_scalar(h1b[:], rsb[:], a_ch[:, ct:ct + 1], None,
                                    ALU.mult))
                at = h1b
                fin(v.tensor_tensor(rsb[:], isb[:], inbb[:], ALU.mult))
                fin(v.tensor_scalar(isb[:], at[:], -1.0, 1.0, ALU.mult,
                                    ALU.add))
                fin(v.tensor_tensor(inbb[:], isb[:], rsb[:], ALU.mult))
                fin(v.tensor_tensor_scan(cpb[:], at[:], inbb[:], 0.0,
                                         ALU.mult, ALU.add))
                fin(v.tensor_tensor_scan(rsb[:], at[:], ones512[:], 1.0,
                                         ALU.mult, ALU.mult))
                fin(v.scalar_tensor_tensor(isb[:], rsb[:],
                                           hstate[:, tch % 2, ct:ct + 1], cpb[:],
                                           ALU.mult, ALU.add))
                fin(v.tensor_copy(hstate[:, (tch + 1) % 2, ct:ct + 1],
                                  isb[:, TCH - 1:TCH]))
                fin(v.tensor_copy(hb[:, tch % 2, ct, :], isb[:]))
                dveG[(tch, ct)] = l.cnt["dv"]
        hb_all[tch] = l.cnt["dv"]

        for dt in range(8):
            @block.tensor
            def _(t, tch=tch, dt=dt):
                if dt == 0:
                    t.wait_ge(l.s("dv"), hb_all[tch])
                prevo = (tch, dt - 2) if dt >= 2 else (tch - 1, dt % 2 + 6)
                if prevo in evbAct:
                    t.wait_ge(l.s("dv"), evbAct[prevo])
                for ct in range(8):
                    mm = t.matmul(psO[dt % 2][:],
                                  wouti[:, ct, dt * 128:(dt + 1) * 128],
                                  hb[:, tch % 2, ct, :],
                                  start=(ct == 0), stop=(ct == 7))
                l.inc(mm, "pe")
                peO[(tch, dt)] = l.cnt["pe"]
                if dt == 7:
                    peO_all[tch] = l.cnt["pe"]

            @block.vector
            def _(v, tch=tch, dt=dt):
                v.wait_ge(l.s("pe"), peO[(tch, dt)])
                if dt == 0 and tch >= 1:
                    v.wait_ge(l.s("stA"), stA_after[tch - 1])
                n = l.inc(v.tensor_scalar(evb[:, dt, :], psO[dt % 2][:],
                                          bb[:, 3, dt:dt + 1], None, ALU.add),
                          "dv")
                v.wait_ge(l.s("dv"), n)
                evbAct[(tch, dt)] = l.cnt["dv"]

        @block.sync
        def _(s, tch=tch):
            s.wait_ge(l.s("dv"), evbAct[(tch, 7)])
            ar = a1fm.rearrange("(dt p) t -> p dt t", p=128)
            l.dinc(s.dma_start(out=ar[:, :, tch * TCH:(tch + 1) * TCH],
                               in_=evb[:]), "stA")
            stA_after[tch] = l.cnt["stA"]
        if tch + 2 < NTCH:
            emit_transp(tch + 2)

    add_dv = {}
    stx_after = {}
    xo_r = xout_h.rearrange("(i p) d -> p i d", p=128)
    for i in range(NT):
        @block.sync
        def _(s, i=i):
            s.wait_ge(l.s("stA"), stA_after[(i * 128) // TCH])
            if i >= 2:
                s.wait_ge(l.s("dv"), add_dv[i - 2])
            l.dinc(s.dma_start(out=a1tm[:, i % 2, :],
                               in_=a1fm[:, i * 128:(i + 1) * 128],
                               transpose=True), "trA")
            l.dinc(s.dma_start(out=x1t[:, i % 2, :], in_=xr_in[:, i, :]), "qX2")
        tra = l.cnt["trA"]
        qx2 = l.cnt["qX2"]

        @block.vector
        def _(v, i=i):
            def fin(instr):
                n = l.inc(instr, "dv")
                v.wait_ge(l.s("dv"), n)
            v.wait_ge(l.s("trA"), tra)
            v.wait_ge(l.s("qX2"), qx2)
            if i >= 2:
                v.wait_ge(l.s("stX"), stx_after[i - 2])
            fin(v.tensor_copy(a1f[:], a1tm[:, i % 2, :]))
            fin(v.tensor_tensor(x2t[:, i % 2, :], x1t[:, i % 2, :], a1f[:],
                                ALU.add))
            add_dv[i] = l.cnt["dv"]

        @block.sync
        def _(s, i=i):
            s.wait_ge(l.s("dv"), add_dv[i])
            l.dinc(s.dma_start(out=xo_r[:, i, :], in_=x2t[:, i % 2, :]), "stX")
            stx_after[i] = l.cnt["stX"]

    @block.sync
    def _(s):
        s.wait_ge(l.s("stX"), l.cnt["stX"])
        s.wait_ge(l.s("stA"), l.cnt["stA"])
        s.wait_ge(l.s("qN"), l.cnt["qN"])
    return l


def build_mesa_launch(debug=False, ns_it=None):
    H, C, NCH, NGR = 128, 128, 32, 4
    TCH, NTCH = 512, 8
    NS_IT, KCH, CHEB_L = 15, 16, 16.0
    if ns_it is not None:
        NS_IT = ns_it
    EPS = 1e-6
    G = 8
    NG = NT // G

    l = L("mesa")
    xin = l.dram("xin", [T, D], F32, kind="ExternalInput")
    wq_h = l.dram("wqT", [D, H], F32, kind="ExternalInput")
    wk_h = l.dram("wkT", [D, H], F32, kind="ExternalInput")
    wv_h = l.dram("wvT", [D, H], F32, kind="ExternalInput")
    wo_h = l.dram("woT", [H, D], F32, kind="ExternalInput")
    bq_h = l.dram("bq", [128, 1], F32, kind="ExternalInput")
    bk_h = l.dram("bk", [128, 1], F32, kind="ExternalInput")
    bv_h = l.dram("bv", [128, 1], F32, kind="ExternalInput")
    bob_h = l.dram("bob", [8, 128], F32, kind="ExternalInput")
    identb_h = l.dram("identb", [128, 128], BF16, kind="ExternalInput")
    trif_h = l.dram("trif", [128, 128], F32, kind="ExternalInput")
    identf_h = l.dram("identf", [128, 128], F32, kind="ExternalInput")
    xout_h = l.dram("xout", [T, D], F32, kind="ExternalOutput")
    xn_dram = l.dram("xn_scr", [T, D], BF16, kind="Internal")
    a2fm = l.dram("a2fm", [D, T], BF16,
                  kind="ExternalOutput" if debug else "Internal")
    dbg = {}
    if debug:
        for nm, w in (("dbg_qt", T), ("dbg_ut", T), ("dbg_vst", T),
                      ("dbg_x", NCH * H), ("dbg_s", NCH * H),
                      ("dbg_yn", NCH * H), ("dbg_ot", T), ("dbg_wt", NCH * H),
                      ("dbg_ms", NCH * H), ("dbg_utok", NCH * H)):
            dbg[nm] = l.dram(nm, [128, w], BF16, kind="ExternalOutput")

    xq = l.sb("xq", [128, G, D], F32)
    xng = l.sb("xng", [128, G, D], BF16)
    stat = l.sb("stat", [128, 4 * G], F32)
    lnps = l.sb("lnps", [128, 5, G], F32)
    xnT = l.sb("xnT", [128, 2, 8, TCH], BF16)
    wq_sb = l.sb("wq_sb", [128, 8, H], BF16)
    wk_sb = l.sb("wk_sb", [128, 8, H], BF16)
    wv_sb = l.sb("wv_sb", [128, 8, H], BF16)
    wo_sb = l.sb("wo_sb", [128, 8 * H], BF16)
    bqc = l.sb("bqc", [128, 1], F32)
    bkc = l.sb("bkc", [128, 1], F32)
    bvc = l.sb("bvc", [128, 1], F32)
    bob_sb = l.sb("bob_sb", [128, 8], F32)
    identb = l.sb("identb_sb", [128, 128], BF16)
    trif = l.sb("trif_sb", [128, 128], F32)
    identf = l.sb("identf_sb", [128, 128], F32)
    QT = l.sb("QT", [128, T], BF16)
    UT = l.sb("UT", [128, T], BF16)
    VsT = l.sb("VsT", [128, T], BF16)
    OT = l.sb("OT", [128, T], BF16)
    arena = l.sb("arena", [128, 2048], F32)
    KTf = arena[:, 0:TCH]
    VTf = arena[:, TCH:2 * TCH]
    prodsb = arena[:, 2 * TCH:3 * TCH]
    sqb = arena[:, 3 * TCH:4 * TCH]
    rowA = l.sb("rowA", [1, TCH], F32)
    rowB = l.sb("rowB", [1, TCH], F32)
    srow = l.sb("srow", [1, TCH], F32)
    nfrow = l.sb("nfrow", [1, NCH], F32)
    nfs = l.sb("nfs", [1, 4 * NCH], F32)
    rowscr = l.sb("rowscr", [1, 128], F32)
    ones1 = l.sb("ones1", [1, 128], F32)
    onescol = l.sb("onescol", [128, 1], F32)
    cbsb = l.sb("cbsb", [128, NCH], F32)
    Gf = arena[:, 0:128]
    Gfsq = arena[:, 128:256]
    Utok = l.sb("Utok", [128, NCH * H], BF16)
    Sb = Utok
    Ms = l.sb("Ms", [128, NCH * H], BF16)
    Xb = l.sb("Xb", [128, NCH * H], BF16)
    tbb = l.sb("tbb", [128, 8 * H], BF16)
    XTb = l.sb("XTb", [128, 8 * H], BF16)
    mtw = l.sb("mtw", [128, 8 * H], F32)
    twoI = mtw
    maskb = mtw
    WTb = l.sb("WTb", [128, NCH * H], BF16)
    Bbf = l.sb("Bbf", [128, NCH * H], BF16)
    Yb = l.sb("Yb", [128, 8 * H], F32)
    Dl = l.sb("Dl", [128, 8 * H], F32)
    Ybn = l.sb("Ybn", [128, NCH * H], BF16)
    tA = arena[:, 0:8 * H]
    tB = arena[:, 8 * H:16 * H]
    ZTsb = l.sb("ZTsb", [128, H], BF16)
    Wtok = l.sb("Wtok", [128, H], BF16)
    Ztok = l.sb("Ztok", [128, H], BF16)
    Vstok = l.sb("Vstok", [128, H], BF16)
    awm = l.sb("awm", [128, H], BF16)
    MTsb = l.sb("MTsb", [128, H], BF16)
    wrm = l.sb("wrm", [128, 8], F32)

    evb = xnT[:, 0, :, :]
    a2tm = xng[:, 0:2, :]
    l.open_block()
    block = l.block
    for nm in ("trS", "stA", "trA", "qX2", "stX"):
        l.sem(nm)
    P1 = l.ps("P1", [128, TCH])
    P2 = l.ps("P2", [128, TCH])
    PG = l.ps("PG", [128, H])
    PW = l.ps("PW", [128, 1024])
    PV = l.ps("PV", [128, 1024], BF16)
    PT = l.ps("PT", [128, H], BF16)

    @block.scalar
    def _(a):
        l.inc(a.activation(wrm[:], wrm[:], AF.Sqrt), "ac")

    @block.sync
    def _(s):
        for hh, sb_ in ((bq_h, bqc), (bk_h, bkc), (bv_h, bvc)):
            l.dinc(s.dma_start(out=sb_[:], in_=hh[:],
                               allow_slow_non_contiguous=True), "qC")
        l.dinc(s.dma_start(out=bob_sb[:], in_=bob_h.rearrange("c p -> p c"),
                           allow_slow_non_contiguous=True), "qC")
        l.dinc(s.dma_start(out=identb[:], in_=identb_h[:]), "qC")
        l.dinc(s.dma_start(out=trif[:], in_=trif_h[:]), "qC")
        l.dinc(s.dma_start(out=identf[:], in_=identf_h[:]), "qC")
    ld_c = l.cnt["qC"]

    @block.gpsimd
    def _(g):
        for wh, wsb in ((wq_h, wq_sb), (wk_h, wk_sb), (wv_h, wv_sb)):
            for ft in range(8):
                l.dinc(g.dma_start(out=wsb[:, ft, :],
                                   in_=wh[ft * 128:(ft + 1) * 128, :]), "qW")
        l.dinc(g.dma_start(out=wo_sb[:], in_=wo_h[:, :]), "qW")
    ld_w = l.cnt["qW"]

    @block.vector
    def _(v):
        def fin(instr):
            n = l.inc(instr, "dv")
            v.wait_ge(l.s("dv"), n)
        v.wait_ge(l.s("qC"), ld_c)
        fin(v.memset(ones1[:], 1.0))
        fin(v.memset(onescol[:], 1.0))
        fin(v.memset(MTsb[:], 0.0))
        fin(v.memset(nfrow[:], 0.0))
        for j in range(8):
            fin(v.tensor_scalar(twoI[:, j * H:(j + 1) * H], identf[:], 2.0,
                                None, ALU.mult))
    dv_init = l.cnt["dv"]

    apply_dv = {}
    xnst_after = {}
    xr_in = xin.rearrange("(i p) d -> p i d", p=128)
    xnr = xn_dram.rearrange("(i p) d -> p i d", p=128)
    for g in range(NG):
        @block.sync
        def _(s, g=g):
            if g >= 1:
                s.wait_ge(l.s("dv"), apply_dv[g - 1])
            l.dinc(s.dma_start(out=xq[:], in_=xr_in[:, g * G:(g + 1) * G, :]), "qX")
        ld_x = l.cnt["qX"]
        ready = [("qX", ld_x)]
        if g >= 1:
            ready.append(("qN", xnst_after[g - 1]))
        apply_dv[g] = emit_ln(l, xq, xng, stat, ready=ready, G=G,
                              scratch=(PW, lnps))

        @block.sync
        def _(s, g=g):
            s.wait_ge(l.s("dv"), apply_dv[g])
            l.dinc(s.dma_start(out=xnr[:, g * G:(g + 1) * G, :], in_=xng[:]), "qN")
            xnst_after[g] = l.cnt["qN"]
    ln_total = l.cnt["dv"]

    trs_after = {}
    peProj_all = {}
    dvU = {}
    dvVs = {}
    dvP1free = {-1: 0}
    dvP2free = {-1: 0}
    dvPWfree = {-1: ln_total}

    def emit_transp(tch):
        @block.sync
        def _(s, tch=tch):
            s.wait_ge(l.s("qN"), xnst_after[(tch * TCH) // (G * 128)])
            if tch >= 2:
                s.wait_ge(l.s("pe"), peProj_all[tch - 2])
            for ft in range(8):
                l.dinc(s.dma_start(
                    out=xnT[:, tch % 2, ft, :],
                    in_=xn_dram[tch * TCH:(tch + 1) * TCH, ft * 128:(ft + 1) * 128],
                    transpose=True), "trS")
            trs_after[tch] = l.cnt["trS"]

    emit_transp(0)
    emit_transp(1)
    for tch in range(NTCH):
        sl = slice(tch * TCH, (tch + 1) * TCH)

        @block.tensor
        def _(t, tch=tch):
            t.wait_ge(l.s("trS"), trs_after[tch])
            if tch == 0:
                t.wait_ge(l.s("qW"), ld_w)
            t.wait_ge(l.s("dv"), dvP1free[tch - 1])
            for ft in range(8):
                mm = t.matmul(P1[:], wq_sb[:, ft, :], xnT[:, tch % 2, ft, :],
                              start=(ft == 0), stop=(ft == 7))
            l.inc(mm, "pe")
        peQ = l.cnt["pe"]

        @block.vector
        def _(v, tch=tch, sl=sl):
            def fin(instr):
                n = l.inc(instr, "dv")
                v.wait_ge(l.s("dv"), n)
            v.wait_ge(l.s("pe"), peQ)
            fin(v.tensor_scalar(QT[:, sl], P1[:], bqc[:, 0:1], None, ALU.add))
        dvQT = l.cnt["dv"]

        @block.tensor
        def _(t, tch=tch):
            t.wait_ge(l.s("dv"), dvP2free[tch - 1])
            for ft in range(8):
                mm = t.matmul(P2[:], wk_sb[:, ft, :], xnT[:, tch % 2, ft, :],
                              start=(ft == 0), stop=(ft == 7))
            l.inc(mm, "pe")
        peK = l.cnt["pe"]

        @block.vector
        def _(v, tch=tch, sl=sl):
            def fin(instr):
                n = l.inc(instr, "dv")
                v.wait_ge(l.s("dv"), n)
            v.wait_ge(l.s("pe"), peK)
            fin(v.tensor_scalar(KTf[:], P2[:], bkc[:, 0:1], None, ALU.add))
            fin(v.scalar_tensor_tensor(prodsb[:], P1[:], bqc[:, 0:1], KTf[:],
                                       ALU.add, ALU.mult))
            fin(v.tensor_tensor(sqb[:], KTf[:], KTf[:], ALU.mult))
        dvProd = l.cnt["dv"]

        @block.tensor
        def _(t, tch=tch):
            t.wait_ge(l.s("dv"), dvProd)
            for ft in range(8):
                mm = t.matmul(P1[:], wv_sb[:, ft, :], xnT[:, tch % 2, ft, :],
                              start=(ft == 0), stop=(ft == 7))
            t.matmul(P2[0:1, :], onescol[:], prodsb[:], start=True, stop=True)
            mm = t.matmul(P2[32:33, :], onescol[:], sqb[:], start=True, stop=True)
            l.inc(mm, "pe")
        peV = l.cnt["pe"]

        @block.scalar
        def _(a, tch=tch):
            a.wait_ge(l.s("pe"), peV)
            l.inc(a.activation(rowA[:], P2[32:33, :], AF.Sqrt), "ac")
        acSq = l.cnt["ac"]

        @block.vector
        def _(v, tch=tch, sl=sl):
            def fin(instr):
                n = l.inc(instr, "dv")
                v.wait_ge(l.s("dv"), n)
            v.wait_ge(l.s("ac"), acSq)
            fin(v.tensor_scalar(VTf[:], P1[:], bvc[:, 0:1], None, ALU.add))
            fin(v.tensor_scalar(rowB[:], rowA[:], EPS, None, ALU.add))
            fin(v.reciprocal(rowA[:], rowB[:]))
            fin(v.tensor_copy(srow[:], P2[0:1, :]))
        dvRows = l.cnt["dv"]

        @block.tensor
        def _(t, tch=tch):
            t.wait_ge(l.s("dv"), dvRows)
            t.wait_ge(l.s("dv"), dvPWfree[tch - 1])
            t.matmul(PW[:, 0:TCH], ones1[:], rowA[:], start=True, stop=True)
            mm = t.matmul(PW[:, TCH:2 * TCH], ones1[:], srow[:], start=True,
                          stop=True)
            l.inc(mm, "pe")
        peBc = l.cnt["pe"]

        @block.vector
        def _(v, tch=tch, sl=sl):
            def fin(instr):
                n = l.inc(instr, "dv")
                v.wait_ge(l.s("dv"), n)
            v.wait_ge(l.s("pe"), peBc)
            fin(v.tensor_tensor(UT[:, sl], KTf[:], PW[:, 0:TCH], ALU.mult))
            fin(v.tensor_tensor(VsT[:, sl], VTf[:], PW[:, TCH:2 * TCH], ALU.mult))
        dvU[tch] = l.cnt["dv"]
        dvVs[tch] = l.cnt["dv"]
        dvP1free[tch] = dvRows
        dvP2free[tch] = dvRows
        dvPWfree[tch] = l.cnt["dv"]
        peProj_all[tch] = peV
        if tch + 2 < NTCH:
            emit_transp(tch + 2)
    proj_dv_total = l.cnt["dv"]

    dvUtok = {-1: 0}
    dvSnap = {-1: 0}
    nfacc = {-1: 0}
    peG_i = {}
    for i in range(NCH):
        csl = slice(i * H, (i + 1) * H)

        @block.tensor
        def _(t, i=i, csl=csl):
            t.wait_ge(l.s("dv"), dvU[i // 4])
            t.wait_ge(l.s("dv"), dvUtok[i - 1])
            l.inc(t.transpose(PT[:], UT[:, csl], identb[:]), "pe")
        peT = l.cnt["pe"]

        @block.vector
        def _(v, i=i, csl=csl):
            def fin(instr):
                n = l.inc(instr, "dv")
                v.wait_ge(l.s("dv"), n)
            v.wait_ge(l.s("pe"), peT)
            fin(v.tensor_copy(Utok[:, csl], PT[:]))
            if i == 0:
                fin(v.tensor_copy(Ms[:, csl], identf[:]))
                fin(v.memset(Gf[:], 0.0))
            else:
                fin(v.tensor_tensor(Ms[:, csl], PG[:], identf[:], ALU.add))
                fin(v.tensor_copy(Gf[:], PG[:]))
            fin(v.tensor_tensor(Gfsq[:], Gf[:], Gf[:], ALU.mult))
        dvUtok[i] = l.cnt["dv"]
        dvSnap[i] = l.cnt["dv"]

        @block.tensor
        def _(t, i=i):
            t.wait_ge(l.s("dv"), dvSnap[i])
            t.wait_ge(l.s("dv"), nfacc[i - 1])
            t.matmul(P2[64:65, 0:128], onescol[:], Gfsq[:], start=True, stop=True)
            mm = t.matmul(PG[:], Utok[:, i * H:(i + 1) * H],
                          Utok[:, i * H:(i + 1) * H],
                          start=(i == 0), stop=True)
            l.inc(mm, "pe")
        peG_i[i] = l.cnt["pe"]

        @block.vector
        def _(v, i=i):
            v.wait_ge(l.s("pe"), peG_i[i])
            n = l.inc(v.tensor_scalar(rowscr[:], P2[64:65, 0:128], 1.0, 0.0,
                                      ALU.mult, ALU.add,
                                      accum_out=nfrow[:, i:i + 1]), "dv")
            v.wait_ge(l.s("dv"), n)
        nfacc[i] = l.cnt["dv"]

    @block.scalar
    def _(a):
        a.wait_ge(l.s("dv"), nfacc[NCH - 1])
        l.inc(a.activation(nfs[:, 0:NCH], nfrow[:], AF.Sqrt), "ac")
    ac_nf = l.cnt["ac"]

    @block.vector
    def _(v):
        def fin(instr):
            n = l.inc(instr, "dv")
            v.wait_ge(l.s("dv"), n)
        v.wait_ge(l.s("ac"), ac_nf)
        fin(v.tensor_scalar(nfs[:, NCH:2 * NCH], nfs[:, 0:NCH], 2.0, None,
                            ALU.add))
        fin(v.reciprocal(nfs[:, 2 * NCH:3 * NCH], nfs[:, NCH:2 * NCH]))
        fin(v.tensor_scalar(nfs[:, 3 * NCH:4 * NCH], nfs[:, 2 * NCH:3 * NCH],
                            2.0, None, ALU.mult))
    dv_c2 = l.cnt["dv"]

    @block.tensor
    def _(t):
        t.wait_ge(l.s("dv"), dv_c2)
        l.inc(t.matmul(P2[:, 0:NCH], ones1[:], nfs[:, 3 * NCH:4 * NCH],
                       start=True, stop=True), "pe")
    pe_cb = l.cnt["pe"]

    @block.vector
    def _(v):
        def fin(instr):
            n = l.inc(instr, "dv")
            v.wait_ge(l.s("dv"), n)
        v.wait_ge(l.s("pe"), pe_cb)
        fin(v.tensor_copy(cbsb[:], P2[:, 0:NCH]))
        for i in range(NCH):
            fin(v.tensor_scalar(Xb[:, i * H:(i + 1) * H], identf[:],
                                cbsb[:, i:i + 1], None, ALU.mult))
    dv_x0 = l.cnt["dv"]

    last_PW = {"v": dv_x0}
    last_PV = {"v": 0}
    for it in range(NS_IT):
        for grp in range(NGR):
            gsl = slice(grp * 8 * H, (grp + 1) * 8 * H)

            @block.tensor
            def _(t, it=it, grp=grp):
                t.wait_ge(l.s("dv"), last_PW["v"])
                t.wait_ge(l.s("dv"), last_PV["v"])
                for ci in range(8):
                    i = grp * 8 + ci
                    t.transpose(PV[:, ci * H:(ci + 1) * H],
                                Xb[:, i * H:(i + 1) * H], identb[:])
                for ci in range(8):
                    i = grp * 8 + ci
                    mm = t.matmul(PW[:, ci * H:(ci + 1) * H],
                                  Ms[:, i * H:(i + 1) * H],
                                  Xb[:, i * H:(i + 1) * H],
                                  start=True, stop=True)
                l.inc(mm, "pe")
            peMX = l.cnt["pe"]

            @block.vector
            def _(v, it=it, grp=grp, gsl=gsl):
                def fin(instr):
                    n = l.inc(instr, "dv")
                    v.wait_ge(l.s("dv"), n)
                v.wait_ge(l.s("pe"), peMX)
                fin(v.tensor_copy(XTb[:], PV[:]))
                fin(v.scalar_tensor_tensor(tbb[:], PW[:], -1.0, twoI[:],
                                           ALU.mult, ALU.add))
            last_PW["v"] = l.cnt["dv"]
            dvTb = l.cnt["dv"]

            @block.tensor
            def _(t, it=it, grp=grp):
                t.wait_ge(l.s("dv"), dvTb)
                for ci in range(8):
                    mm = t.matmul(PW[:, ci * H:(ci + 1) * H],
                                  XTb[:, ci * H:(ci + 1) * H],
                                  tbb[:, ci * H:(ci + 1) * H],
                                  start=True, stop=True)
                l.inc(mm, "pe")
            peXp = l.cnt["pe"]

            @block.vector
            def _(v, it=it, grp=grp, gsl=gsl):
                v.wait_ge(l.s("pe"), peXp)
                n = l.inc(v.tensor_copy(Xb[:, gsl], PW[:]), "dv")
                v.wait_ge(l.s("dv"), n)
            last_PW["v"] = l.cnt["dv"]
            last_PV["v"] = dvTb
    ns_total = l.cnt["dv"]

    dvWT = {-1: ns_total}
    dvB = {-1: 0}
    for i in range(NCH):
        csl = slice(i * H, (i + 1) * H)

        @block.tensor
        def _(t, i=i, csl=csl):
            t.wait_ge(l.s("dv"), dvWT[i - 1])
            l.inc(t.matmul(P1[0:H, :H], Xb[:, csl], UT[:, csl],
                           start=True, stop=True), "pe")
        peWT = l.cnt["pe"]

        @block.vector
        def _(v, i=i, csl=csl):
            v.wait_ge(l.s("pe"), peWT)
            n = l.inc(v.tensor_copy(WTb[:, csl], P1[0:H, :H]), "dv")
            v.wait_ge(l.s("dv"), n)
        dvWT[i] = l.cnt["dv"]

        @block.tensor
        def _(t, i=i, csl=csl):
            t.wait_ge(l.s("dv"), dvWT[i])
            t.wait_ge(l.s("dv"), dvB[i - 1])
            l.inc(t.matmul(P2[0:H, :H], UT[:, csl], WTb[:, csl],
                           start=True, stop=True), "pe")
        peS = l.cnt["pe"]

        @block.vector
        def _(v, i=i, csl=csl):
            def fin(instr):
                n = l.inc(instr, "dv")
                v.wait_ge(l.s("dv"), n)
            v.wait_ge(l.s("pe"), peS)
            fin(v.tensor_copy(Sb[:, csl], P2[0:H, :H]))
            fin(v.tensor_tensor(Bbf[:, csl], trif[:], P2[0:H, :H], ALU.mult))
        dvB[i] = l.cnt["dv"]

    d_ = 1.0 + CHEB_L / 2.0
    cc_ = CHEB_L / 2.0
    sigma1 = d_ / cc_
    rho = 1.0 / sigma1
    coefs = []
    for it in range(KCH - 1):
        rho_new = 1.0 / (2.0 * sigma1 - rho)
        coefs.append((rho_new * rho, 2.0 * rho_new / cc_))
        rho = rho_new

    @block.vector
    def _(v):
        def fin(instr):
            n = l.inc(instr, "dv")
            v.wait_ge(l.s("dv"), n)
        v.wait_ge(l.s("dv"), dvB[NCH - 1])
        for j in range(8):
            fin(v.tensor_copy(maskb[:, j * H:(j + 1) * H], trif[:]))
    dv_mask = l.cnt["dv"]

    last_PWc = {"v": dv_mask}
    for grp in range(NGR):
        gsl = slice(grp * 8 * H, (grp + 1) * 8 * H)

        @block.vector
        def _(v, grp=grp, gsl=gsl):
            def fin(instr):
                n = l.inc(instr, "dv")
                v.wait_ge(l.s("dv"), n)
            fin(v.tensor_scalar(Dl[:], Bbf[:, gsl], 1.0 / d_, None, ALU.mult))
            fin(v.tensor_copy(Yb[:], Dl[:]))
            fin(v.tensor_scalar(Ybn[:, gsl], Dl[:], -1.0, None, ALU.mult))
        last_PWc["v"] = l.cnt["dv"]

        for it in range(KCH - 1):
            c1, c2 = coefs[it]

            @block.tensor
            def _(t, it=it, grp=grp):
                t.wait_ge(l.s("dv"), last_PWc["v"])
                for ci in range(8):
                    i = grp * 8 + ci
                    mm = t.matmul(PW[:, ci * H:(ci + 1) * H],
                                  Sb[:, i * H:(i + 1) * H],
                                  Ybn[:, i * H:(i + 1) * H],
                                  start=True, stop=True)
                l.inc(mm, "pe")
            peSY = l.cnt["pe"]

            @block.vector
            def _(v, it=it, grp=grp, gsl=gsl, c1=c1, c2=c2):
                def fin(instr):
                    n = l.inc(instr, "dv")
                    v.wait_ge(l.s("dv"), n)
                v.wait_ge(l.s("pe"), peSY)
                fin(v.tensor_tensor(tA[:], maskb[:], PW[:], ALU.mult))
                fin(v.tensor_tensor(tB[:], Yb[:], tA[:], ALU.subtract))
                fin(v.tensor_tensor(tA[:], Bbf[:, gsl], tB[:], ALU.subtract))
                fin(v.tensor_scalar(tB[:], Dl[:], c1, None, ALU.mult))
                fin(v.scalar_tensor_tensor(Dl[:], tA[:], c2, tB[:],
                                           ALU.mult, ALU.add))
                fin(v.tensor_tensor(tA[:], Yb[:], Dl[:], ALU.add))
                fin(v.tensor_copy(Yb[:], tA[:]))
                fin(v.tensor_scalar(Ybn[:, gsl], tA[:], -1.0, None, ALU.mult))
            last_PWc["v"] = l.cnt["dv"]
    cheb_total = l.cnt["dv"]

    dvSeq = {-1: cheb_total}
    dvOT = {}
    dvMT = {-1: dv_init}
    for i in range(NCH):
        csl = slice(i * H, (i + 1) * H)

        @block.tensor
        def _(t, i=i, csl=csl):
            t.wait_ge(l.s("dv"), dvSeq[i - 1])
            l.inc(t.transpose(PT[:], WTb[:, csl], identb[:]), "pe")
        peWtokT = l.cnt["pe"]

        @block.vector
        def _(v, i=i):
            v.wait_ge(l.s("pe"), peWtokT)
            n = l.inc(v.tensor_copy(Wtok[:], PT[:]), "dv")
            v.wait_ge(l.s("dv"), n)
        dvWtok = l.cnt["dv"]

        @block.tensor
        def _(t, i=i, csl=csl):
            t.wait_ge(l.s("dv"), dvWtok)
            t.matmul(P1[0:H, :H], identb[:], WTb[:, csl], start=True, stop=False)
            l.inc(t.matmul(P1[0:H, :H], Wtok[:], Ybn[:, csl], start=False,
                           stop=True), "pe")
        peZT = l.cnt["pe"]

        @block.vector
        def _(v, i=i):
            v.wait_ge(l.s("pe"), peZT)
            n = l.inc(v.tensor_copy(ZTsb[:], P1[0:H, :H]), "dv")
            v.wait_ge(l.s("dv"), n)
        dvZT = l.cnt["dv"]

        @block.tensor
        def _(t, i=i, csl=csl):
            t.wait_ge(l.s("dv"), dvZT)
            l.inc(t.transpose(PT[:], ZTsb[:], identb[:]), "pe")
        peZtokT = l.cnt["pe"]

        @block.vector
        def _(v, i=i):
            v.wait_ge(l.s("pe"), peZtokT)
            n = l.inc(v.tensor_copy(Ztok[:], PT[:]), "dv")
            v.wait_ge(l.s("dv"), n)
        dvZtok = l.cnt["dv"]

        @block.tensor
        def _(t, i=i, csl=csl):
            t.wait_ge(l.s("dv"), dvZtok)
            l.inc(t.transpose(PT[:], VsT[:, csl], identb[:]), "pe")
        peVsT = l.cnt["pe"]

        @block.vector
        def _(v, i=i):
            v.wait_ge(l.s("pe"), peVsT)
            n = l.inc(v.tensor_copy(Vstok[:], PT[:]), "dv")
            v.wait_ge(l.s("dv"), n)
        dvVstok = l.cnt["dv"]

        @block.tensor
        def _(t, i=i, csl=csl):
            t.wait_ge(l.s("dv"), dvVstok)
            l.inc(t.matmul(P2[0:H, :H], ZTsb[:], QT[:, csl], start=True,
                           stop=True), "pe")
        peAW = l.cnt["pe"]

        @block.vector
        def _(v, i=i):
            v.wait_ge(l.s("pe"), peAW)
            n = l.inc(v.tensor_tensor(awm[:], trif[:], P2[0:H, :H], ALU.mult),
                      "dv")
            v.wait_ge(l.s("dv"), n)
        dvAwm = l.cnt["dv"]

        @block.tensor
        def _(t, i=i, csl=csl):
            t.wait_ge(l.s("dv"), dvAwm)
            t.wait_ge(l.s("dv"), dvMT[i - 1])
            t.matmul(P1[0:H, :H], MTsb[:], QT[:, csl], start=True, stop=False)
            l.inc(t.matmul(P1[0:H, :H], Vstok[:], awm[:], start=False,
                           stop=True), "pe")
            l.inc(t.matmul(PG[:], Ztok[:], Vstok[:], start=(i == 0),
                           stop=True), "pe")
        peO_i = l.cnt["pe"]

        @block.vector
        def _(v, i=i, csl=csl):
            def fin(instr):
                n = l.inc(instr, "dv")
                v.wait_ge(l.s("dv"), n)
            v.wait_ge(l.s("pe"), peO_i)
            fin(v.tensor_copy(OT[:, csl], P1[0:H, :H]))
            fin(v.tensor_copy(MTsb[:], PG[:]))
        dvOT[i] = l.cnt["dv"]
        dvMT[i] = l.cnt["dv"]
        dvSeq[i] = l.cnt["dv"]

    stA_after = {-1: 0}
    dvEvb = {-1: 0}
    for tch in range(NTCH):
        sl = slice(tch * TCH, (tch + 1) * TCH)
        for dt in range(8):
            @block.tensor
            def _(t, tch=tch, dt=dt, sl=sl):
                if dt == 0:
                    t.wait_ge(l.s("dv"), dvOT[(tch + 1) * 4 - 1])
                t.wait_ge(l.s("dv"), dvEvb[(tch, dt - 1) if dt else (tch - 1, 7)]
                          if (dt or tch) else -1) if False else None
                prev = (tch, dt - 1) if dt else (tch - 1, 7)
                if prev in dvEvb:
                    t.wait_ge(l.s("dv"), dvEvb[prev])
                l.inc(t.matmul(P2[:], wo_sb[:, dt * H:(dt + 1) * H], OT[:, sl],
                               start=True, stop=True), "pe")
            peOut = l.cnt["pe"]

            @block.vector
            def _(v, tch=tch, dt=dt):
                v.wait_ge(l.s("pe"), peOut)
                if dt == 0 and tch >= 1:
                    v.wait_ge(l.s("stA"), stA_after[tch - 1])
                n = l.inc(v.tensor_scalar(evb[:, dt, :], P2[:],
                                          bob_sb[:, dt:dt + 1], None, ALU.add),
                          "dv")
                v.wait_ge(l.s("dv"), n)
            dvEvb[(tch, dt)] = l.cnt["dv"]

        @block.sync
        def _(s, tch=tch, sl=sl):
            s.wait_ge(l.s("dv"), dvEvb[(tch, 7)])
            ar = a2fm.rearrange("(dt p) t -> p dt t", p=128)
            l.dinc(s.dma_start(out=ar[:, :, sl], in_=evb[:]), "stA")
            stA_after[tch] = l.cnt["stA"]

    add_dv = {}
    stx_after = {}
    xo_r = xout_h.rearrange("(i p) d -> p i d", p=128)
    x4t = xq[:, 0:2, :]
    x5t = xq[:, 2:4, :]
    for i in range(NT):
        @block.sync
        def _(s, i=i):
            if i == 0:
                s.wait_ge(l.s("qN"), l.cnt["qN"])
            s.wait_ge(l.s("stA"), stA_after[(i * 128) // TCH])
            if i >= 2:
                s.wait_ge(l.s("dv"), add_dv[i - 2])
            l.dinc(s.dma_start(out=a2tm[:, i % 2, :],
                               in_=a2fm[:, i * 128:(i + 1) * 128],
                               transpose=True), "trA")
            l.dinc(s.dma_start(out=x4t[:, i % 2, :], in_=xr_in[:, i, :]), "qX2")
        tra = l.cnt["trA"]
        qx2 = l.cnt["qX2"]

        @block.vector
        def _(v, i=i):
            v.wait_ge(l.s("trA"), tra)
            v.wait_ge(l.s("qX2"), qx2)
            if i >= 2:
                v.wait_ge(l.s("stX"), stx_after[i - 2])
            n = l.inc(v.tensor_tensor(x5t[:, i % 2, :], x4t[:, i % 2, :],
                                      a2tm[:, i % 2, :], ALU.add), "dv")
            v.wait_ge(l.s("dv"), n)
            add_dv[i] = l.cnt["dv"]

        @block.sync
        def _(s, i=i):
            s.wait_ge(l.s("dv"), add_dv[i])
            l.dinc(s.dma_start(out=xo_r[:, i, :], in_=x5t[:, i % 2, :]), "stX")
            stx_after[i] = l.cnt["stX"]

    if debug:
        @block.sync
        def _(s):
            s.wait_ge(l.s("dv"), l.cnt["dv"])
            for nm, src in (("dbg_qt", QT), ("dbg_ut", UT), ("dbg_vst", VsT),
                            ("dbg_x", Xb), ("dbg_s", Sb), ("dbg_yn", Ybn),
                            ("dbg_ot", OT), ("dbg_wt", WTb), ("dbg_ms", Ms)):
                l.dinc(s.dma_start(out=dbg[nm], in_=src[:]), "st")

    @block.sync
    def _(s):
        s.wait_ge(l.s("stX"), l.cnt["stX"])
        s.wait_ge(l.s("stA"), l.cnt["stA"])
        s.wait_ge(l.s("st"), l.cnt["st"])
        s.wait_ge(l.s("qN"), l.cnt["qN"])
    return l


_progs = {}


def _prog(key, builder):
    if key not in _progs:
        _progs[key] = builder()
    return _progs[key]


def _consts_inputs():
    return {
        "identb": np.eye(128, dtype=BFNP),
        "trib": np.triu(np.ones((128, 128), dtype=BFNP), 1),
    }


def _run(l, maps):
    return run_bass_kernel_spmd(l.nc, maps, core_ids=list(range(NCORE))).results


def _moe_layer(I, xcur, EE, KK, CAP, PLOC, DFF, wg, wfc, wpr, drop, nexp_l,
               consts):
    ids = list(range(NCORE))
    lp = _prog(("prep", EE), lambda: build_prep_launch(EE, KK, PLOC))
    r1 = _run(lp, [dict(xin=np.ascontiguousarray(xcur[c]), wgate=wg, **consts)
                   for c in ids])
    cnt_dev = np.stack([r1[c]["counts"][0] for c in ids])

    le = _prog(("exp", EE), lambda: build_expert_launch(nexp_l, CAP, PLOC, DFF))
    zrow = np.zeros((1, D), dtype=r1[0]["slotTab"].dtype)
    i128 = (np.arange(CAP // 128) * 128).astype(np.float32).reshape(1, -1)
    maps = []
    for c in ids:
        parts = []
        cnts = np.zeros((1, nexp_l * 8), np.float32)
        for el in range(nexp_l):
            e = nexp_l * c + el
            for sc in ids:
                parts.append(r1[sc]["slotTab"].reshape(EE, PLOC, D)[e])
                cnts[0, el * 8 + sc] = cnt_dev[sc, e]
        tabs = np.concatenate(parts + [zrow], axis=0)
        maps.append(dict(tabs=tabs, cnts=cnts,
                         wfc=np.ascontiguousarray(wfc[nexp_l * c:nexp_l * (c + 1)]),
                         wpr=np.ascontiguousarray(wpr[nexp_l * c:nexp_l * (c + 1)]),
                         i128=i128))
    r2 = _run(le, maps)

    lc = _prog(("comb", EE), lambda: build_combine_launch(EE, KK, CAP, drop))
    yall = np.concatenate(
        [np.ascontiguousarray(r2[e // nexp_l]["yout"][e % nexp_l].T)
         for e in range(EE)], axis=0)
    maps = [dict(xin=np.ascontiguousarray(xcur[c]), yall=yall,
                 meta=r1[c]["meta"], meta2=r1[c]["meta2"], meta3=r1[c]["meta3"],
                 cntall=cnt_dev,
                 pmask=(np.arange(8) < c).astype(np.float32).reshape(8, 1))
            for c in ids]
    r3 = _run(lc, maps)
    return np.stack([r3[c]["xout"] for c in ids])


def kernel(**inputs):
    I = inputs
    ids = list(range(NCORE))
    x0 = np.ascontiguousarray(np.asarray(I["L0_attn_x"], np.float32))
    consts = _consts_inputs()

    def rs8(v):
        return np.ascontiguousarray(np.asarray(v, np.float32).reshape(8, 128))

    x1 = _moe_layer(I, x0, E0, K0, CAP0, PLOC0, DFF0, I["L0_ffn_w_gate"],
                    I["L0_ffn_c_fc"], I["L0_ffn_c_proj"], True, 1, consts)

    lg = _prog(("grif",), build_griffin_launch)
    gmaps = [dict(xin=np.ascontiguousarray(x1[c]),
                  win=I["L0_attn_W_in"], wr=I["L0_attn_W_r"],
                  wi=I["L0_attn_W_i"], wout=I["L0_attn_W_out"],
                  binb=rs8(I["L0_attn_b_in"]), brb=rs8(I["L0_attn_b_r"]),
                  bib=rs8(I["L0_attn_b_i"]), boutb=rs8(I["L0_attn_b_out"]),
                  logab=rs8(I["L0_attn_log_a"])) for c in ids]
    rg = _run(lg, gmaps)
    x2 = np.stack([rg[c]["xout"] for c in ids])

    x3 = _moe_layer(I, x2, E0, K0, CAP0, PLOC0, DFF0, I["L0_ffn_w_gate"],
                    I["L0_ffn_c_fc"], I["L0_ffn_c_proj"], True, 1, consts)

    x4 = _moe_layer(I, x3, E1, K1, CAP1, PLOC1, 2 * I1, I["L1_ffn_router_w"],
                    I["L1_ffn_w1"], I["L1_ffn_w2"], False, 8, consts)

    lm = _prog(("mesa",), build_mesa_launch)
    mmaps = [dict(xin=np.ascontiguousarray(x4[c]),
                  wqT=np.ascontiguousarray(np.asarray(I["L1_attn_Wq"]).T),
                  wkT=np.ascontiguousarray(np.asarray(I["L1_attn_Wk"]).T),
                  wvT=np.ascontiguousarray(np.asarray(I["L1_attn_Wv"]).T),
                  woT=np.ascontiguousarray(np.asarray(I["L1_attn_Wo"]).T),
                  bq=np.asarray(I["L1_attn_bq"], np.float32).reshape(128, 1),
                  bk=np.asarray(I["L1_attn_bk"], np.float32).reshape(128, 1),
                  bv=np.asarray(I["L1_attn_bv"], np.float32).reshape(128, 1),
                  bob=rs8(I["L1_attn_bo"]),
                  identb=np.eye(128, dtype=BFNP),
                  trif=np.triu(np.ones((128, 128), np.float32)),
                  identf=np.eye(128, dtype=np.float32)) for c in ids]
    rm = _run(lm, mmaps)
    x5 = np.stack([rm[c]["xout"] for c in ids])

    x6 = _moe_layer(I, x5, E1, K1, CAP1, PLOC1, 2 * I1, I["L1_ffn_router_w"],
                    I["L1_ffn_w1"], I["L1_ffn_w2"], False, 8, consts)
    return x6


# version 04
# speedup vs baseline: 1.0605x; 1.0605x over previous
import numpy as np
import ml_dtypes
from contextlib import ExitStack

import concourse.bass as bass
import concourse.mybir as mybir
from concourse.bass import IndirectOffsetOnAxis
from concourse.bass_utils import run_bass_kernel_spmd
from concourse.alu_op_type import AluOpType as ALU

F32 = mybir.dt.float32
BF16 = mybir.dt.bfloat16
U32 = mybir.dt.uint32
AF = mybir.ActivationFunctionType
BFNP = ml_dtypes.bfloat16

NCORE = 8
T = 4096
D = 1024
NT = 32

E0, K0, CAP0, PLOC0, DFF0 = 8, 2, 4096, 2048, 3072
E1, K1, CAP1, PLOC1, I1 = 64, 4, 1024, 512, 512

OOB = float(1 << 30)


class L:

    def __init__(self, name):
        self.nc = bass.Bass("TRN2", target_bir_lowering=False, debug=False)
        self.ctx = ExitStack()
        self.name = name
        self._sems = {}
        self.cnt = {}

    def dram(self, name, shape, dt, kind="Internal"):
        return self.nc.dram_tensor(name, shape, dt, kind=kind).ap()

    def sb(self, name, shape, dt):
        return self.ctx.enter_context(self.nc.sbuf_tensor(name, shape, dt))

    def ps(self, name, shape, dt=F32):
        return self.ctx.enter_context(self.nc.psum_tensor(name, shape, dt))

    def sem(self, name):
        h = self.ctx.enter_context(self.nc.semaphore(name))
        self._sems[name] = h
        self.cnt[name] = 0
        return h

    def s(self, name):
        return self._sems[name]

    def inc(self, instr, name, k=1):
        instr.then_inc(self._sems[name], k)
        self.cnt[name] += k
        return self.cnt[name]

    def dinc(self, instr, name):
        return self.inc(instr, name, 16)

    def open_block(self):
        self.block = self.ctx.enter_context(self.nc.Block())
        for nm in ("qLD", "qSC", "qGP", "pe", "dv", "ac", "po", "st",
                   "qX", "qW", "qN", "qC"):
            self.sem(nm)



def emit_consts(l, identb_h, trib_h, identf_h=None, warm=("Ln", "Exp", "Square")):
    block = l.block
    wrm = l.sb("warm_sb", [128, 8], F32)

    @block.scalar
    def _(a):
        for fn in warm:
            l.inc(a.activation(wrm[:], wrm[:], getattr(AF, fn)), "ac")
    identb = l.sb("identb_sb", [128, 128], BF16)
    tri_us = l.sb("tri_us", [128, 128], BF16)
    ones_b = l.sb("ones_b", [128, 128], BF16)
    onescol_b = l.sb("onescol_b", [128, 1], BF16)
    onesrow_b = l.sb("onesrow_b", [1, 128], BF16)
    identf = l.sb("identf_sb", [128, 128], F32) if identf_h is not None else None

    @block.sync
    def _(s):
        l.dinc(s.dma_start(out=identb[:], in_=identb_h[:]), "qC")
        l.dinc(s.dma_start(out=tri_us[:], in_=trib_h[:]), "qC")
        if identf_h is not None:
            l.dinc(s.dma_start(out=identf[:], in_=identf_h[:]), "qC")
    ld = l.cnt["qC"]

    @block.vector
    def _(v):
        v.wait_ge(l.s("qC"), ld)
        l.inc(v.memset(ones_b[:], 1.0), "dv")
        l.inc(v.memset(onescol_b[:], 1.0), "dv")
        l.inc(v.memset(onesrow_b[:], 1.0), "dv")
    return dict(identf=identf, identb=identb, tri_us=tri_us, ones_b=ones_b,
                onescol_b=onescol_b, onesrow_b=onesrow_b)


def emit_ln(l, x_sb, xn_sb, stat, ready=(), pad_out=None, dbg=None, G=NT,
            scratch=None):
    block = l.block
    sx = stat[:, 0:G]
    sxx = stat[:, G:2 * G]
    mu = stat[:, 2 * G:3 * G]
    rstd = stat[:, 3 * G:4 * G]
    if scratch is None:
        dmp_a = l.ps(f"lndmp_a{l.cnt['ac']}", [128, D], F32)
        lnps = l.sb(f"lnps{l.cnt['dv']}", [128, 5, G], F32)
    else:
        dmp_a, lnps = scratch
    scr = lnps[:, 0, :]


    @block.vector
    def _(v):
        def fin(instr):
            n = l.inc(instr, "dv")
            v.wait_ge(l.s("dv"), n)
        for nm, vv in ready:
            v.wait_ge(l.s(nm), vv)
        fin(v.memset(stat[:], -1.0e9))
        for i in range(G):
            l.inc(v.tensor_scalar(xn_sb[:, i, :], x_sb[:, i, :], 1.0, 0.0, ALU.mult,
                                  ALU.add, accum_out=sx[:, i:i + 1]), "dv")
            l.inc(v.tensor_tensor(dmp_a[:], x_sb[:, i, :], x_sb[:, i, :], ALU.mult), "dv")
            l.inc(v.tensor_scalar(xn_sb[:, i, :], dmp_a[:], 1.0, 0.0, ALU.mult,
                                  ALU.add, accum_out=sxx[:, i:i + 1]), "dv")
        v.wait_ge(l.s("dv"), l.cnt["dv"])
        fin(v.tensor_scalar(mu[:], sx[:], 1.0 / D, None, ALU.mult))
        fin(v.tensor_tensor(scr[:], mu[:], mu[:], ALU.mult))
        fin(v.scalar_tensor_tensor(rstd[:], sxx[:], 1.0 / D, scr[:],
                                   ALU.mult, ALU.subtract))
        if dbg is not None:
            fin(v.tensor_copy(dbg[:, 0:G], rstd))
    dv_var = l.cnt["dv"]

    scr2 = lnps[:, 1, :]
    scr3 = lnps[:, 2, :]
    scr5 = scr
    scr4 = lnps[:, 3, :]
    dmp_pad = dmp_a

    @block.vector
    def _(v):
        def fin(instr):
            n = l.inc(instr, "dv")
            v.wait_ge(l.s("dv"), n)
        fin(v.memset(scr2[:], -1.0e9))
        fin(v.memset(scr3[:], -1.0e9))
        fin(v.memset(scr4[:], -1.0e9))
        fin(v.tensor_scalar(scr2[:], rstd, 1e-5, None, ALU.add))
        fin(v.tensor_scalar(scr3[:], scr2[:], 1.0, None, ALU.add))
        fin(v.reciprocal(scr4[:], scr3[:]))
        if dbg is not None:
            fin(v.tensor_copy(dbg[:, G:2 * G], scr4[:]))
        rA, rB = scr4, lnps[:, 4, :]
        for it in range(14):
            fin(v.tensor_tensor(scr3[:], rA[:], scr2[:], ALU.mult))
            fin(v.tensor_tensor(scr5[:], rA[:], scr3[:], ALU.mult))
            fin(v.tensor_scalar(scr3[:], scr5[:], -0.5, 1.5, ALU.mult, ALU.add))
            fin(v.tensor_tensor(rB[:], rA[:], scr3[:], ALU.mult))
            rA, rB = rB, rA
        fin(v.tensor_copy(rstd, rA[:]))
    ac_rstd = l.cnt["ac"]

    @block.vector
    def _(v):
        v.wait_ge(l.s("ac"), ac_rstd)
        for i in range(G):
            l.inc(v.tensor_scalar(xn_sb[:, i, :], x_sb[:, i, :], mu[:, i:i + 1],
                                  rstd[:, i:i + 1], ALU.subtract, ALU.mult), "dv")
    if pad_out is not None:
        pad_out.append(dmp_a)
    return l.cnt["dv"]


def emit_xn_store(l, xn_sb, xn_dram, dv_ready):
    block = l.block

    @block.sync
    def _(s):
        s.wait_ge(l.s("dv"), dv_ready)
        xr = xn_dram.rearrange("(i p) d -> p i d", p=128)
        for q in range(4):
            l.dinc(s.dma_start(out=xr[:, q * 8:(q + 1) * 8, :],
                               in_=xn_sb[:, q * 8:(q + 1) * 8, :]), "qN")
    return l.cnt["qN"]


def emit_prep(l, io, EE, KK, PLOC, xn_sb, xn_dram, consts, xn_ready, tag=""):
    nc, block = l.nc, l.block

    wg_b = l.sb(f"wg_b{tag}", [128, 8, EE], BF16)
    pad_b = io["pad"]

    @block.gpsimd
    def _(g):
        l.dinc(g.dma_start(out=wg_b[:],
                           in_=io["wgate"].rearrange("(c p) e -> p c e", p=128)), "qW")
    ld_wg = l.cnt["qW"]

    @block.vector
    def _(v):
        v.wait_ge(l.s("qW"), ld_wg)
        l.inc(v.tensor_copy(pad_b[:], xn_sb[:, 0, 0:256]), "dv")
    dv_wg = l.cnt["dv"]

    SL = 128 if EE > 8 else 256
    NSL, TPJ = T // SL, SL // 128
    xnt_buf = l.sb(f"xnt{tag}", [128, 2, 8, SL], BF16)
    lg_buf = l.sb(f"lgbuf{tag}", [EE, 2, SL], BF16)
    lg_tmS = l.sb(f"lg_tmS{tag}", [128, 2, TPJ, EE], BF16)
    lgt = l.sb(f"lgt{tag}", [128, 2, EE], BF16)
    lg_dram = l.dram(f"lg_scr{tag}", [NT, 128, EE], BF16)
    lg_ps = l.ps(f"lg_ps{tag}", [EE, SL])
    tp_ps = l.ps(f"tp_ps{tag}", [128, EE], BF16)
    lgst_after = {}
    lgld_after = {}
    mmS, evS, tpS, tpE = (f"mmS{tag}", f"evS{tag}", f"tpS{tag}", f"tpE{tag}")
    for _nm in (mmS, evS, tpS, tpE):
        l.sem(_nm)
    xp = [f"xp0{tag}", f"xp1{tag}"]
    for _nm in xp:
        l.sem(_nm)

    @block.scalar
    def _(a):
        a.wait_ge(l.s("qN"), xn_ready)
        for ts_ in range(NSL):
            if ts_ >= 2:
                a.wait_ge(l.s(mmS), ts_ - 1)
            for dc in range(8):
                l.dinc(a.dma_start(
                    out=xnt_buf[:, ts_ % 2, dc, :],
                    in_=xn_dram[ts_ * SL:(ts_ + 1) * SL, dc * 128:(dc + 1) * 128],
                    transpose=True), xp[ts_ % 2])

    for ts_ in range(NSL):
        @block.tensor
        def _(t, ts_=ts_):
            if ts_ == 0:
                t.wait_ge(l.s("dv"), dv_wg)
            t.wait_ge(l.s(xp[ts_ % 2]), (ts_ // 2 + 1) * 8 * 16)
            if ts_ > 0:
                t.wait_ge(l.s(evS), ts_)
            for dc in range(8):
                mm = t.matmul(lg_ps[:], wg_b[:, dc, :],
                              xnt_buf[:, ts_ % 2, dc, :],
                              start=(dc == 0), stop=(dc == 7))
            l.inc(mm, mmS)

        @block.vector
        def _(v, ts_=ts_):
            v.wait_ge(l.s(mmS), ts_ + 1)
            if ts_ >= 2:
                v.wait_ge(l.s(tpS), (ts_ - 1) * TPJ)
            v.tensor_copy(lg_buf[:, ts_ % 2, :], lg_ps[:])
            l.inc(v.tensor_copy(pad_b[:], xn_sb[:, 0, 0:256]), evS)

        @block.tensor
        def _(t, ts_=ts_):
            t.wait_ge(l.s(evS), ts_ + 1)
            for j in range(TPJ):
                i = ts_ * TPJ + j
                if i > 0:
                    t.wait_ge(l.s(tpE), i)
                l.inc(t.transpose(tp_ps[:], lg_buf[0:EE, ts_ % 2, j * 128:(j + 1) * 128],
                                  consts["identb"][0:EE, 0:EE]), tpS)

        @block.vector
        def _(v, ts_=ts_):
            for j in range(TPJ):
                i = ts_ * TPJ + j
                v.wait_ge(l.s(tpS), i + 1)
                if ts_ >= 2:
                    v.wait_ge(l.s("qLD"), lgst_after[(ts_ - 2) * TPJ + j])
                l.inc(v.tensor_copy(lg_tmS[:, ts_ % 2, j, :], tp_ps[:]), tpE)

        @block.sync
        def _(s, ts_=ts_):
            for j in range(TPJ):
                i = ts_ * TPJ + j
                s.wait_ge(l.s(tpE), i + 1)
                l.dinc(s.dma_start(out=lg_dram[i], in_=lg_tmS[:, ts_ % 2, j, :]),
                       "qLD")
                lgst_after[i] = l.cnt["qLD"]
    dv_lgtm = l.cnt["dv"]

    probs = l.sb(f"probs{tag}", [128, EE], F32)
    sume = l.sb(f"sume{tag}", [128, 3], F32)
    mx8 = l.sb(f"mx8{tag}", [128, 8], F32)
    idx8 = l.sb(f"idx8{tag}", [128, 8], U32)
    oh = l.sb(f"oh{tag}", [128, KK, EE], BF16)
    ind = l.sb(f"ind{tag}", [128, EE], BF16)
    ind2 = l.sb(f"ind2{tag}", [128, EE], BF16)
    ind3 = l.sb(f"ind3{tag}", [128, EE], BF16)
    iotaEf = l.sb(f"iotaEf{tag}", [128, EE], F32)
    ek = l.sb(f"ek{tag}", [128, NT * KK], F32)
    posk = l.sb(f"posk{tag}", [128, NT * KK], F32)
    wk = l.sb(f"wk{tag}", [128, NT * KK], F32)
    ovf = l.sb(f"ovf{tag}", [128, NT * KK], F32)
    sidx_u = l.sb(f"sidx_u{tag}", [128, NT * KK], U32)
    scrE2 = l.sb(f"scrE2{tag}", [128, EE], F32)
    runoff_all = l.sb(f"runoff{tag}", [128, 2 * EE], F32)
    runoff2 = lambda a: runoff_all[:, a * EE:(a + 1) * EE]

    routeps = l.ps(f"routeps{tag}", [128, 3 * EE + 2 * NT * KK], F32)
    probs2 = routeps[:, 0:EE]
    pos_t = routeps[:, EE:2 * EE]
    scrE = routeps[:, 2 * EE:3 * EE]
    sidx = routeps[:, 3 * EE:3 * EE + NT * KK]
    sidx2 = routeps[:, 3 * EE + NT * KK:3 * EE + 2 * NT * KK]

    psAB = l.ps(f"psAB{tag}", [128, 2, EE])
    psA = psAB[:, 0, :]
    psB = psAB[:, 1, :]

    dbg_sm = None
    pe_pos_prev = [None]

    @block.gpsimd
    def _(g):
        l.inc(g.iota(iotaEf[:], pattern=[[1, EE]], base=0, channel_multiplier=0,
                     allow_small_or_imprecise_dtypes=True), "po")
    po_i0 = l.cnt["po"]

    @block.vector
    def _(v):
        v.wait_ge(l.s("po"), po_i0)
        l.inc(v.memset(runoff_all[:], 0.0), "dv")
    po_init = l.cnt["po"]

    def fin(v, instr):
        n = l.inc(instr, "dv")
        v.wait_ge(l.s("dv"), n)
        return n

    ac_exp_hist = {}
    for i in range(NT):
        @block.sync
        def _(s, i=i):
            s.wait_ge(l.s("qLD"), lgst_after[i])
            if i >= 2:
                s.wait_ge(l.s("ac"), ac_exp_hist[i - 2])
            l.dinc(s.dma_start(out=lgt[:, i % 2, :], in_=lg_dram[i]), "qSC")
        lgld = l.cnt["qSC"]

        @block.vector
        def _(v, i=i):
            if i == 0:
                v.wait_ge(l.s("po"), po_init)
            if pe_pos_prev[0] is not None:
                v.wait_ge(l.s("pe"), pe_pos_prev[0])
            pass
        dv_mx = l.cnt["dv"]

        @block.scalar
        def _(a, i=i):
            a.wait_ge(l.s("qSC"), lgld)
            a.wait_ge(l.s("dv"), dv_mx)
            l.inc(a.activation(probs[:], lgt[:, i % 2, :], AF.Exp,
                               accum_out=sume[:, 0:1]), "ac")
            l.inc(a.activation(sume[:, 2:3], sume[:, 0:1], AF.Copy), "ac")
        ac_exp = l.cnt["ac"]
        ac_exp_hist[i] = ac_exp

        @block.vector
        def _(v, i=i):
            v.wait_ge(l.s("ac"), ac_exp)
            fin(v, v.reciprocal(sume[:, 1:2], sume[:, 2:3]))
            fin(v, v.tensor_scalar(probs2[:], probs[:], sume[:, 1:2], None, ALU.mult))
            fin(v, v.max(mx8[:], probs2[:]))
            fin(v, v.max_index(idx8[:], mx8[:], probs2[:]))
            for k in range(KK):
                fin(v, v.tensor_copy(ek[:, i * KK + k:i * KK + k + 1],
                                     idx8[:, k:k + 1]))
                fin(v, v.tensor_copy(wk[:, i * KK + k:i * KK + k + 1],
                                     mx8[:, k:k + 1]))
            for k in range(KK):
                fin(v, v.tensor_scalar(oh[:, k, :], iotaEf[:],
                                       ek[:, i * KK + k:i * KK + k + 1], None,
                                       ALU.is_equal))
            if KK == 2:
                fin(v, v.tensor_tensor(ind[:], oh[:, 0, :], oh[:, 1, :], ALU.add))
            elif KK == 4:
                fin(v, v.tensor_tensor(ind2[:], oh[:, 0, :], oh[:, 1, :], ALU.add))
                fin(v, v.tensor_tensor(ind3[:], oh[:, 2, :], oh[:, 3, :], ALU.add))
                fin(v, v.tensor_tensor(ind[:], ind2[:], ind3[:], ALU.add))
            else:
                raise NotImplementedError
            l.inc(v.tensor_copy(pad_b[:], xn_sb[:, 0, 0:256]), "dv")
        dv_ind = l.cnt["dv"]

        @block.tensor
        def _(t, i=i):
            t.wait_ge(l.s("dv"), dv_ind)
            t.matmul(psA[:], consts["tri_us"][:], ind[:], start=True, stop=True)
            l.inc(t.matmul(psB[:], consts["ones_b"][:], ind[:], start=True, stop=True),
                  "pe")
        pe_pos = l.cnt["pe"]
        pe_pos_prev[0] = pe_pos

        @block.vector
        def _(v, i=i):
            v.wait_ge(l.s("pe"), pe_pos)
            p_, q_ = i % 2, 1 - i % 2
            fin(v, v.tensor_tensor(pos_t[:], psA[:], runoff2(p_)[:], ALU.add))
            fin(v, v.scalar_tensor_tensor(runoff2(q_)[:], psB[:], 1.0,
                                          runoff2(p_)[:], ALU.mult, ALU.add))

            for k in range(KK):
                col = i * KK + k
                fin(v, v.tensor_tensor(scrE[:], pos_t[:], oh[:, k, :], ALU.mult))
                fin(v, v.tensor_scalar(scrE2[:], scrE[:], 1.0, 0.0, ALU.mult,
                                       ALU.add, accum_out=posk[:, col:col + 1]))

    cnt_sb = l.sb(f"cnt_sb{tag}", [1, EE], F32)

    @block.vector
    def _(v):
        def finw(instr):
            n = l.inc(instr, "dv")
            v.wait_ge(l.s("dv"), n)
        finw(v.scalar_tensor_tensor(sidx[:], ek[:], float(PLOC), posk[:],
                                    ALU.mult, ALU.add))
        finw(v.tensor_scalar(ovf[:], posk[:], float(PLOC), None, ALU.is_ge))
        finw(v.scalar_tensor_tensor(sidx2[:], ovf[:], OOB, sidx[:],
                                    ALU.mult, ALU.add))
        finw(v.tensor_copy(sidx_u[:], sidx2[:]))
        v.tensor_copy(cnt_sb[:], runoff2(NT % 2)[0:1, :])
        l.inc(v.tensor_copy(pad_b[:], xn_sb[:, 0, 0:256]), "dv")
    dv_all = l.cnt["dv"]
    dv_cnt = l.cnt["dv"]

    @block.gpsimd
    def _(g):
        g.wait_ge(l.s("dv"), dv_all)
        breg = g.alloc_register()
        g.reg_mov(breg, EE * PLOC - 1)
        for i in range(NT):
            for k in range(KK):
                col = i * KK + k
                l.dinc(g.indirect_dma_start(
                    out=io["slotTab"][:, :],
                    out_offset=IndirectOffsetOnAxis(sidx_u[:, col:col + 1], 0),
                    in_=xn_sb[:, i, :], in_offset=None,
                    bounds_check=breg, oob_is_err=False), "qGP")
        g.free_register(breg)

    @block.sync
    def _(s):
        s.wait_ge(l.s("dv"), dv_cnt)
        l.dinc(s.dma_start(out=io["meta"], in_=ek[:]), "st")
        l.dinc(s.dma_start(out=io["meta2"], in_=posk[:]), "st")
        l.dinc(s.dma_start(out=io["meta3"], in_=wk[:]), "st")
        l.dinc(s.dma_start(out=io["counts"], in_=cnt_sb[:]), "st")
        if "dbg_stat" in io:
            l.dinc(s.dma_start(out=io["dbg_stat"], in_=io["_stat"][:]), "st")


def emit_tail(l):
    block = l.block

    @block.sync
    def _(s):
        s.wait_ge(l.s("st"), l.cnt["st"])
        s.wait_ge(l.s("qGP"), l.cnt["qGP"])



def build_prep_launch(EE, KK, PLOC, debug=False):
    l = L("prep")
    xin = l.dram("xin", [T, D], F32, kind="ExternalInput")
    wgate = l.dram("wgate", [D, EE], F32, kind="ExternalInput")
    identb_h = l.dram("identb", [128, 128], BF16, kind="ExternalInput")
    trib_h = l.dram("trib", [128, 128], BF16, kind="ExternalInput")
    io = dict(
        slotTab=l.dram("slotTab", [EE * PLOC, D], BF16, kind="ExternalOutput"),
        counts=l.dram("counts", [1, EE], F32, kind="ExternalOutput"),
        meta=l.dram("meta", [128, NT * KK], F32, kind="ExternalOutput"),
        meta2=l.dram("meta2", [128, NT * KK], F32, kind="ExternalOutput"),
        meta3=l.dram("meta3", [128, NT * KK], F32, kind="ExternalOutput"),
        wgate=wgate,
    )
    xn_dram = l.dram("xn_scr", [T, D], BF16,
                     kind="ExternalOutput" if debug else "Internal")
    if debug:
        io["dbg_stat"] = l.dram("dbg_stat", [128, 4 * NT], F32, kind="ExternalOutput")
        io["dbg_r0"] = l.dram("dbg_r0", [128, 2 * NT], F32, kind="ExternalOutput")

    x_sb = l.sb("x_sb", [128, NT, D], F32)
    xn_sb = l.sb("xn_sb", [128, NT, D], BF16)
    stat = l.sb("stat", [128, 4 * NT], F32)
    dbg_sb = l.ps("dbg_ps", [128, 2 * NT]) if debug else None

    l.open_block()
    block = l.block

    @block.sync
    def _(s):
        xr = xin.rearrange("(i p) d -> p i d", p=128)
        for q in range(4):
            l.dinc(s.dma_start(out=x_sb[:, q * 8:(q + 1) * 8, :],
                               in_=xr[:, q * 8:(q + 1) * 8, :]), "qX")
    ld_x = l.cnt["qX"]

    consts = emit_consts(l, identb_h, trib_h)
    pads = []
    dv_xn = emit_ln(l, x_sb, xn_sb, stat, ready=[("qX", ld_x)], pad_out=pads,
                    dbg=dbg_sb)
    xn_ready = emit_xn_store(l, xn_sb, xn_dram, dv_xn)
    io["_stat"] = stat
    io["_dbg_sb"] = dbg_sb
    io["pad"] = pads[0][:, 0:256]
    emit_prep(l, io, EE, KK, PLOC, xn_sb, xn_dram, consts, xn_ready)
    if debug:
        dv_now = l.cnt["dv"]

        @block.vector
        def _(v):
            v.wait_ge(l.s("dv"), dv_now)
            l.inc(v.tensor_copy(x_sb[:, 0, 0:2 * NT], dbg_sb[:]), "dv")
        dv_dbg = l.cnt["dv"]

        @block.sync
        def _(s):
            s.wait_ge(l.s("dv"), dv_dbg)
            l.dinc(s.dma_start(out=io["dbg_r0"], in_=x_sb[:, 0, 0:2 * NT]), "st")
    emit_tail(l)
    return l


def build_expert_launch(NEXP_L, CAP, PLOC, DFF, debug=False):
    TCH = 512
    NTC = CAP // 128
    NCH = CAP // TCH
    NH2 = DFF // 256
    NROWS = NEXP_L * 8 * PLOC
    WPAR = 1 if NEXP_L == 1 else 2
    NGC = NEXP_L * NCH

    l = L("exp")
    tabs = l.dram("tabs", [NROWS + 1, D], BF16, kind="ExternalInput")
    cnts_h = l.dram("cnts", [1, NEXP_L * 8], F32, kind="ExternalInput")
    wfc_h = l.dram("wfc", [NEXP_L, D, DFF], F32, kind="ExternalInput")
    wpr_h = l.dram("wpr", [NEXP_L, DFF // 2, D], F32, kind="ExternalInput")
    i128_h = l.dram("i128", [1, NTC], F32, kind="ExternalInput")
    y_h = l.dram("yout", [NEXP_L, D, CAP], BF16, kind="ExternalOutput")
    xg_dram = l.dram("xg_scr", [NEXP_L, CAP, D], BF16,
                     kind="ExternalOutput" if debug else "Internal")

    xg = l.sb("xg", [128, WPAR * NTC, D], BF16)
    wfc = l.sb("wfc_sb", [128, WPAR * 8, DFF], BF16)
    wpr = l.sb("wpr_sb", [128, WPAR * NH2, D], BF16)
    xT = l.sb("xT", [128, 2, 8, TCH], BF16)
    hsb = l.sb("hsb", [128, 2, NH2, TCH], BF16)
    yT = l.sb("yT", [128, 2, 8, TCH], BF16)
    silA = l.sb("silA", [128, 2, TCH], F32)
    cnt_sb = l.sb("cnt_sb", [1, NEXP_L * 8], F32)
    catrow = l.sb("catrow", [1, 16 + NTC], F32)
    ones8 = l.sb("ones8", [1, 8], F32)
    onesf = l.sb("onesf", [1, 128], F32)
    iotaP = l.sb("iotaP", [128, 1], U32)
    iotaPf = l.sb("iotaPf", [128, 1], F32)
    cumb = l.sb("cumb", [128, 16 + NTC], F32)
    exclb = l.sb("exclb", [128, 8], F32)
    svb = l.sb("svb", [128, NTC], F32)
    accA = l.sb("accA", [128, NTC], F32)
    accB = l.sb("accB", [128, NTC], F32)
    eqb = l.sb("eqb", [128, NTC], F32)
    rowA = l.sb("rowA", [128, NTC], F32)
    rowB = l.sb("rowB", [128, NTC], F32)
    oobm = l.sb("oobm", [128, NTC], F32)
    rowu = l.sb("rowu", [128, NEXP_L * NTC], U32)
    rowf = l.sb("rowf", [128, NEXP_L * NTC], F32)
    wrm = l.sb("wrm", [128, 8], F32)

    l.open_block()
    block = l.block
    l.sem("trS")

    psH = [l.ps(f"psH{j}", [128, TCH]) for j in range(4)]
    psY = [l.ps(f"psY{j}", [128, TCH]) for j in range(2)]
    psCum = l.ps("psCum", [128, 16 + NTC])

    @block.scalar
    def _(a):
        l.inc(a.activation(wrm[:], wrm[:], AF.Silu), "ac")

    @block.sync
    def _(s):
        l.dinc(s.dma_start(out=cnt_sb[:], in_=cnts_h[:]), "qC")
        l.dinc(s.dma_start(out=catrow[:, 16:16 + NTC], in_=i128_h[:]), "qC")
    ld_c = l.cnt["qC"]

    @block.gpsimd
    def _(g):
        l.inc(g.iota(iotaP[:], pattern=[[0, 1]], base=0, channel_multiplier=1), "po")
    po_iota = l.cnt["po"]

    @block.vector
    def _(v):
        def fin(i):
            n = l.inc(i, "dv")
            v.wait_ge(l.s("dv"), n)
        v.wait_ge(l.s("po"), po_iota)
        v.wait_ge(l.s("qC"), ld_c)
        fin(v.memset(ones8[:], 1.0))
        fin(v.memset(onesf[:], 1.0))
        fin(v.tensor_copy(iotaPf[:], iotaP[:]))

    dv_gather = {}
    for e in range(NEXP_L):
        @block.vector
        def _(v, e=e):
            def fin(i):
                n = l.inc(i, "dv")
                v.wait_ge(l.s("dv"), n)
            fin(v.tensor_copy(catrow[:, 8:16], cnt_sb[:, e * 8:(e + 1) * 8]))
            fin(v.tensor_tensor_scan(catrow[:, 0:8], ones8[:], catrow[:, 8:16],
                                     0.0, ALU.mult, ALU.add))

        @block.tensor
        def _(t, e=e):
            t.wait_ge(l.s("dv"), l.cnt["dv"])
            l.inc(t.matmul(psCum[:], onesf[:], catrow[:], start=True, stop=True),
                  "pe")
        pe_cum = l.cnt["pe"]

        @block.vector
        def _(v, e=e):
            def fin(i):
                n = l.inc(i, "dv")
                v.wait_ge(l.s("dv"), n)
            v.wait_ge(l.s("pe"), pe_cum)
            fin(v.tensor_copy(cumb[:], psCum[:]))
            fin(v.tensor_tensor(exclb[:], cumb[:, 0:8], cumb[:, 8:16], ALU.subtract))
            fin(v.tensor_scalar(svb[:], cumb[:, 16:16 + NTC], iotaPf[:, 0:1],
                                None, ALU.add))
            fin(v.memset(accA[:], 0.0))
            a, b = accA, accB
            for sg in range(8):
                fin(v.scalar_tensor_tensor(b[:], svb[:], cumb[:, sg:sg + 1], a[:],
                                           ALU.is_ge, ALU.add))
                a, b = b, a
            sig = a
            fin(v.memset(rowA[:], 0.0))
            a2, b2 = rowA, rowB
            for sg in range(8):
                fin(v.tensor_scalar(eqb[:], sig[:], float(sg), None, ALU.is_equal))
                fin(v.scalar_tensor_tensor(b2[:], eqb[:], exclb[:, sg:sg + 1], a2[:],
                                           ALU.mult, ALU.add))
                a2, b2 = b2, a2
            base = a2
            fin(v.scalar_tensor_tensor(b2[:], sig[:], float(PLOC), svb[:],
                                       ALU.mult, ALU.add))
            fin(v.tensor_tensor(eqb[:], b2[:], base[:], ALU.subtract))
            fin(v.tensor_scalar(oobm[:], svb[:], cumb[:, 7:8], None, ALU.is_ge))
            fin(v.scalar_tensor_tensor(b2[:], oobm[:], float(1 << 22), eqb[:],
                                       ALU.mult, ALU.add))
            fin(v.tensor_scalar(eqb[:], b2[:], float(e * 8 * PLOC), None, ALU.add))
            fin(v.tensor_scalar_min(rowf[:, e * NTC:(e + 1) * NTC], eqb[:],
                                    float(NROWS)))
            fin(v.tensor_copy(rowu[:, e * NTC:(e + 1) * NTC],
                              rowf[:, e * NTC:(e + 1) * NTC]))
        dv_gather[e] = l.cnt["dv"]

    wq_after = {}
    gp_after = {}
    qn_after = {}
    trs_after = {}
    peG = {}
    peY = {}
    peY_last = {}
    acS = {}
    gate_dv = {}
    gate_all = {}
    evac_dv = {}
    st_after = {}
    last_a_sil = {0: 0, 1: 0}
    last_g_gate = {0: 0, 1: 0}
    last_sil_gate = {0: 0, 1: 0}
    last_evac = {0: 0, 1: 0}
    pe_all_e = {}

    def ech(gc):
        return gc // NCH, gc % NCH

    def emit_gpsimd(e):
        @block.gpsimd
        def _(g, e=e):
            wp = e % WPAR
            if WPAR == 2 and e >= 2:
                g.wait_ge(l.s("pe"), pe_all_e[e - 2])
            for ft in range(8):
                l.dinc(g.dma_start(out=wfc[:, wp * 8 + ft, :],
                                   in_=wfc_h[e, ft * 128:(ft + 1) * 128, :]), "qW")
            for hp in range(NH2):
                l.dinc(g.dma_start(out=wpr[:, wp * NH2 + hp, :],
                                   in_=wpr_h[e, hp * 128:(hp + 1) * 128, :]), "qW")
            wq_after[e] = l.cnt["qW"]
            g.wait_ge(l.s("dv"), dv_gather[e])
            if WPAR == 2 and e >= 2:
                g.wait_ge(l.s("qN"), qn_after[(e - 2, NCH - 1)])
            breg = g.alloc_register()
            g.reg_mov(breg, NROWS)
            for i in range(NTC):
                l.dinc(g.indirect_dma_start(
                    out=xg[:, wp * NTC + i, :], out_offset=None,
                    in_=tabs[:, :],
                    in_offset=IndirectOffsetOnAxis(
                        rowu[:, e * NTC + i:e * NTC + i + 1], 0),
                    bounds_check=breg, oob_is_err=False), "qGP")
            g.free_register(breg)
            gp_after[e] = l.cnt["qGP"]

    def emit_bounce(e):
        @block.sync
        def _(s, e=e):
            s.wait_ge(l.s("qGP"), gp_after[e])
            wp = e % WPAR
            xr = xg_dram[e].rearrange("(i p) d -> p i d", p=128)
            for ch in range(NCH):
                l.dinc(s.dma_start(
                    out=xr[:, ch * 4:(ch + 1) * 4, :],
                    in_=xg[:, wp * NTC + ch * 4:wp * NTC + (ch + 1) * 4, :]), "qN")
                qn_after[(e, ch)] = l.cnt["qN"]

    def emit_ystore(gc):
        e, ch = ech(gc)

        @block.sync
        def _(s, gc=gc, e=e, ch=ch):
            s.wait_ge(l.s("dv"), evac_dv[(gc, 7)])
            yr = y_h[e].rearrange("(dt p) t -> p dt t", p=128)
            l.dinc(s.dma_start(out=yr[:, :, ch * TCH:(ch + 1) * TCH],
                               in_=yT[:, gc % 2, :, :]), "st")
            st_after[gc] = l.cnt["st"]

    def emit_transp(gc):
        e, ch = ech(gc)

        @block.scalar
        def _(a, gc=gc, e=e, ch=ch):
            a.wait_ge(l.s("qN"), qn_after[(e, ch)])
            for ft in range(8):
                l.dinc(a.dma_start(
                    out=xT[:, gc % 2, ft, :],
                    in_=xg_dram[e, ch * TCH:(ch + 1) * TCH, ft * 128:(ft + 1) * 128],
                    transpose=True), "trS")
            trs_after[gc] = l.cnt["trS"]

    def emit_pe_pair(gc, mh):
        e, ch = ech(gc)
        wp = e % WPAR
        pb = mh % 2

        @block.tensor
        def _(t, gc=gc, mh=mh, wp=wp, pb=pb, ch=ch, e=e):
            if mh == 0:
                t.wait_ge(l.s("trS"), trs_after[gc])
                if ch == 0:
                    t.wait_ge(l.s("qW"), wq_after[e])
            if last_a_sil[pb]:
                t.wait_ge(l.s("ac"), last_a_sil[pb])
            if last_g_gate[pb]:
                t.wait_ge(l.s("dv"), last_g_gate[pb])
            for ft in range(8):
                t.matmul(psH[pb][:], wfc[:, wp * 8 + ft, mh * 128:(mh + 1) * 128],
                         xT[:, gc % 2, ft, :], start=(ft == 0), stop=(ft == 7))
            for ft in range(8):
                mm = t.matmul(psH[2 + pb][:],
                              wfc[:, wp * 8 + ft,
                                  DFF // 2 + mh * 128:DFF // 2 + (mh + 1) * 128],
                              xT[:, gc % 2, ft, :], start=(ft == 0), stop=(ft == 7))
            l.inc(mm, "pe")
            peG[(gc, mh)] = l.cnt["pe"]

    def emit_silu(gc, mh):
        pb = mh % 2

        @block.scalar
        def _(a, gc=gc, mh=mh, pb=pb):
            a.wait_ge(l.s("pe"), peG[(gc, mh)])
            if last_sil_gate[pb]:
                a.wait_ge(l.s("dv"), last_sil_gate[pb])
            l.inc(a.activation(silA[:, pb, :], psH[pb][:], AF.Silu), "ac")
        acS[(gc, mh)] = l.cnt["ac"]
        last_a_sil[pb] = l.cnt["ac"]

    def emit_gate(gc, mh):
        pb = mh % 2

        @block.vector
        def _(v, gc=gc, mh=mh, pb=pb):
            v.wait_ge(l.s("ac"), acS[(gc, mh)])
            if mh == 0 and gc >= 2:
                v.wait_ge(l.s("pe"), peY_last[gc - 2])
            l.inc(v.tensor_tensor(hsb[:, gc % 2, mh, :], silA[:, pb, :],
                                  psH[2 + pb][:], ALU.mult), "dv")
        gate_dv[(gc, mh)] = l.cnt["dv"]
        last_g_gate[pb] = l.cnt["dv"]
        last_sil_gate[pb] = l.cnt["dv"]

    def emit_pe_y(gc, dt):
        e, ch = ech(gc)
        wp = e % WPAR
        yb = dt % 2

        @block.tensor
        def _(t, gc=gc, dt=dt, wp=wp, yb=yb):
            if dt == 0:
                t.wait_ge(l.s("dv"), gate_all[gc])
            if last_evac[yb]:
                t.wait_ge(l.s("dv"), last_evac[yb])
            for hp in range(NH2):
                mm = t.matmul(psY[yb][:],
                              wpr[:, wp * NH2 + hp, dt * 128:(dt + 1) * 128],
                              hsb[:, gc % 2, hp, :],
                              start=(hp == 0), stop=(hp == NH2 - 1))
            l.inc(mm, "pe")
        peY[(gc, dt)] = l.cnt["pe"]
        if dt == 7:
            peY_last[gc] = l.cnt["pe"]
            pe_all_e[ech(gc)[0]] = l.cnt["pe"]

    def emit_evac(gc, dt):
        yb = dt % 2

        @block.vector
        def _(v, gc=gc, dt=dt, yb=yb):
            v.wait_ge(l.s("pe"), peY[(gc, dt)])
            if dt == 0 and gc >= 2:
                v.wait_ge(l.s("st"), st_after[gc - 2])
            l.inc(v.tensor_copy(yT[:, gc % 2, dt, :], psY[yb][:]), "dv")
        evac_dv[(gc, dt)] = l.cnt["dv"]
        last_evac[yb] = l.cnt["dv"]

    emit_gpsimd(0)
    emit_bounce(0)
    if NEXP_L > 1:
        emit_gpsimd(1)
        emit_bounce(1)
    emit_transp(0)
    if NGC > 1:
        emit_transp(1)
    for gc in range(NGC):
        emit_pe_pair(gc, 0)
        emit_pe_pair(gc, 1)
        for mh in range(2, NH2):
            emit_silu(gc, mh - 2)
            emit_gate(gc, mh - 2)
            emit_pe_pair(gc, mh)
        for mh in (NH2 - 2, NH2 - 1):
            emit_silu(gc, mh)
            emit_gate(gc, mh)
        gate_all[gc] = l.cnt["dv"]
        emit_pe_y(gc, 0)
        emit_pe_y(gc, 1)
        for dt in range(2, 8):
            emit_evac(gc, dt - 2)
            emit_pe_y(gc, dt)
        emit_evac(gc, 6)
        emit_evac(gc, 7)
        emit_ystore(gc)
        if gc + 2 < NGC:
            ee, cch = ech(gc + 2)
            if cch == 0 and ee >= 2:
                emit_gpsimd(ee)
                emit_bounce(ee)
            emit_transp(gc + 2)

    @block.sync
    def _(s):
        s.wait_ge(l.s("st"), l.cnt["st"])
        s.wait_ge(l.s("qN"), l.cnt["qN"])
        s.wait_ge(l.s("qGP"), l.cnt["qGP"])
        s.wait_ge(l.s("trS"), l.cnt["trS"])
        s.wait_ge(l.s("qW"), l.cnt["qW"])
    return l


def build_combine_launch(EE, KK, CAP, drop):
    NC = NT * KK
    l = L("comb")
    xin = l.dram("xin", [T, D], F32, kind="ExternalInput")
    yall_h = l.dram("yall", [EE * CAP, D], BF16, kind="ExternalInput")
    meta_h = l.dram("meta", [128, NC], F32, kind="ExternalInput")
    meta2_h = l.dram("meta2", [128, NC], F32, kind="ExternalInput")
    meta3_h = l.dram("meta3", [128, NC], F32, kind="ExternalInput")
    cntall_h = l.dram("cntall", [8, EE], F32, kind="ExternalInput")
    pmask_h = l.dram("pmask", [8, 1], F32, kind="ExternalInput")
    xout_h = l.dram("xout", [T, D], F32, kind="ExternalOutput")

    x_sb = l.sb("x_sb", [128, NT, D], F32)
    ekall = l.sb("ekall", [128, NC], F32)
    pkall = l.sb("pkall", [128, NC], F32)
    wkall = l.sb("wkall", [128, NC], F32)
    cnt_sb = l.sb("cnt_sb", [8, EE], F32)
    pm_sb = l.sb("pm_sb", [8, 1], F32)
    onesf = l.sb("onesf", [1, 128], F32)
    offrow = l.sb("offrow", [1, EE], F32)
    off_b = l.sb("off_b", [128, EE], F32)
    eqm = l.sb("eqm", [128, NC], F32)
    oaccA = l.sb("oaccA", [128, NC], F32)
    oaccB = l.sb("oaccB", [128, NC], F32)
    posg = l.sb("posg", [128, NC], F32)
    wv = l.sb("wv", [128, NC], F32)
    gidx = l.sb("gidx", [128, NC], F32)
    gidx_u = l.sb("gidx_u", [128, NC], U32)
    yr = l.sb("yr", [128, 4, D], BF16)
    acc = l.sb("acc", [128, 4, D], F32)

    l.open_block()
    block = l.block
    psO = l.ps("psO", [1, EE])
    psB2 = l.ps("psB2", [128, EE])

    @block.sync
    def _(s):
        xr = xin.rearrange("(i p) d -> p i d", p=128)
        for q in range(4):
            l.dinc(s.dma_start(out=x_sb[:, q * 8:(q + 1) * 8, :],
                               in_=xr[:, q * 8:(q + 1) * 8, :]), "qX")
        l.dinc(s.dma_start(out=ekall[:], in_=meta_h[:]), "qC")
        l.dinc(s.dma_start(out=pkall[:], in_=meta2_h[:]), "qC")
        l.dinc(s.dma_start(out=wkall[:], in_=meta3_h[:]), "qC")
        l.dinc(s.dma_start(out=cnt_sb[:], in_=cntall_h[:]), "qC")
        l.dinc(s.dma_start(out=pm_sb[:], in_=pmask_h[:]), "qC")
    ld_x = l.cnt["qX"]
    ld_c = l.cnt["qC"]

    @block.vector
    def _(v):
        v.wait_ge(l.s("qC"), ld_c)
        n = l.inc(v.memset(onesf[:], 1.0), "dv")
        v.wait_ge(l.s("dv"), n)
    dv_ones = l.cnt["dv"]

    @block.tensor
    def _(t):
        t.wait_ge(l.s("dv"), dv_ones)
        l.inc(t.matmul(psO[:], pm_sb[:], cnt_sb[:], start=True, stop=True), "pe")
    pe_off1 = l.cnt["pe"]

    @block.vector
    def _(v):
        v.wait_ge(l.s("pe"), pe_off1)
        n = l.inc(v.tensor_copy(offrow[:], psO[:]), "dv")
        v.wait_ge(l.s("dv"), n)
    dv_offrow = l.cnt["dv"]

    @block.tensor
    def _(t):
        t.wait_ge(l.s("dv"), dv_offrow)
        l.inc(t.matmul(psB2[:], onesf[:], offrow[:], start=True, stop=True), "pe")
    pe_off2 = l.cnt["pe"]

    @block.vector
    def _(v):
        def fin(i):
            n = l.inc(i, "dv")
            v.wait_ge(l.s("dv"), n)
        v.wait_ge(l.s("pe"), pe_off2)
        fin(v.tensor_copy(off_b[:], psB2[:]))
        fin(v.memset(oaccA[:], 0.0))
        a, b = oaccA, oaccB
        for e in range(EE):
            fin(v.tensor_scalar(eqm[:], ekall[:], float(e), off_b[:, e:e + 1],
                                ALU.is_equal, ALU.mult))
            fin(v.tensor_tensor(b[:], a[:], eqm[:], ALU.add))
            a, b = b, a
        fin(v.tensor_tensor(posg[:], a[:], pkall[:], ALU.add))
        fin(v.tensor_scalar_min(b[:], posg[:], float(CAP - 1)))
        fin(v.scalar_tensor_tensor(gidx[:], ekall[:], float(CAP), b[:],
                                   ALU.mult, ALU.add))
        fin(v.tensor_copy(gidx_u[:], gidx[:]))
        if drop:
            fin(v.tensor_scalar(eqm[:], posg[:], float(CAP), None, ALU.is_lt))
            fin(v.tensor_tensor(wv[:], wkall[:], eqm[:], ALU.mult))
        else:
            fin(v.tensor_copy(wv[:], wkall[:]))
    dv_gidx = l.cnt["dv"]

    gp_col = {}
    dv_use = {0: 0, 1: 0, 2: 0, 3: 0}
    dv_tile = {}

    for i in range(NT):
        for k in range(KK):
            col = i * KK + k
            r = col % 4

            @block.gpsimd
            def _(g, col=col, r=r):
                if col == 0:
                    g.wait_ge(l.s("dv"), dv_gidx)
                    breg_l = g.alloc_register()
                    g.reg_mov(breg_l, EE * CAP - 1)
                    l._breg = breg_l
                if dv_use[r]:
                    g.wait_ge(l.s("dv"), dv_use[r])
                l.dinc(g.indirect_dma_start(
                    out=yr[:, r, :], out_offset=None,
                    in_=yall_h[:, :],
                    in_offset=IndirectOffsetOnAxis(gidx_u[:, col:col + 1], 0),
                    bounds_check=l._breg, oob_is_err=False), "qGP")
                if col == NC - 1:
                    g.free_register(l._breg)
            gp_col[col] = l.cnt["qGP"]

            @block.vector
            def _(v, i=i, k=k, col=col, r=r):
                def fin(instr):
                    n = l.inc(instr, "dv")
                    v.wait_ge(l.s("dv"), n)
                v.wait_ge(l.s("qGP"), gp_col[col])
                if i == 0 and k == 0:
                    v.wait_ge(l.s("qX"), ld_x)
                ai = (i % 2) * 2
                if k == 0:
                    fin(v.tensor_scalar(acc[:, ai, :], yr[:, r, :],
                                        wv[:, col:col + 1], None, ALU.mult))
                else:
                    src = ai + (k - 1) % 2
                    dst = ai + k % 2
                    fin(v.scalar_tensor_tensor(acc[:, dst, :], yr[:, r, :],
                                               wv[:, col:col + 1], acc[:, src, :],
                                               ALU.mult, ALU.add))
                dv_use[r] = l.cnt["dv"]
                if k == KK - 1:
                    last = ai + (KK - 1) % 2
                    fin(v.scalar_tensor_tensor(x_sb[:, i, :], acc[:, last, :], 0.5,
                                               x_sb[:, i, :], ALU.mult, ALU.add))
                    dv_tile[i] = l.cnt["dv"]

    @block.sync
    def _(s):
        xr = xout_h.rearrange("(i p) d -> p i d", p=128)
        for q in range(4):
            s.wait_ge(l.s("dv"), dv_tile[q * 8 + 7])
            l.dinc(s.dma_start(out=xr[:, q * 8:(q + 1) * 8, :],
                               in_=x_sb[:, q * 8:(q + 1) * 8, :]), "st")
        s.wait_ge(l.s("st"), l.cnt["st"])
        s.wait_ge(l.s("qGP"), l.cnt["qGP"])
    return l


def build_griffin_launch(debug=False):
    G = 8
    NG = NT // G
    TCH = 512
    NTCH = T // TCH
    l = L("grif")
    xin = l.dram("xin", [T, D], F32, kind="ExternalInput")
    win_h = l.dram("win", [D, D], F32, kind="ExternalInput")
    wr_h = l.dram("wr", [D, D], F32, kind="ExternalInput")
    wi_h = l.dram("wi", [D, D], F32, kind="ExternalInput")
    wout_h = l.dram("wout", [D, D], F32, kind="ExternalInput")
    bin_h = l.dram("binb", [8, 128], F32, kind="ExternalInput")
    br_h = l.dram("brb", [8, 128], F32, kind="ExternalInput")
    bi_h = l.dram("bib", [8, 128], F32, kind="ExternalInput")
    bout_h = l.dram("boutb", [8, 128], F32, kind="ExternalInput")
    loga_h = l.dram("logab", [8, 128], F32, kind="ExternalInput")
    xout_h = l.dram("xout", [T, D], F32, kind="ExternalOutput")
    xn_dram = l.dram("xn_scr", [T, D], BF16, kind="Internal")
    a1fm = l.dram("a1fm", [D, T], BF16,
                  kind="ExternalOutput" if debug else "Internal")

    xq = l.sb("xq", [128, G, D], F32)
    xng = l.sb("xng", [128, 2, G, D], BF16)
    stat = l.sb("stat", [128, 4 * G], F32)
    lnps = l.sb("lnps", [128, 5, G], F32)
    wini = l.sb("wini", [128, 8, D], BF16)
    wri = l.sb("wri", [128, 8, D], BF16)
    wii = l.sb("wii", [128, 8, D], BF16)
    wouti = l.sb("wouti", [128, 8, D], BF16)
    xnT = l.sb("xnT", [128, 2, 8, TCH], BF16)
    bb = l.sb("bb", [128, 4, 8], F32)
    loga_sb = l.sb("loga_sb", [128, 8], F32)
    a_ch = l.sb("a_ch", [128, 8], F32)
    rsb = l.sb("rsb", [128, TCH], F32)
    isb = l.sb("isb", [128, TCH], F32)
    inbb = l.sb("inbb", [128, TCH], F32)
    h1b = l.sb("h1b", [128, TCH], F32)
    cpb = l.sb("cpb", [128, TCH], F32)
    ones512 = l.sb("ones512", [128, TCH], F32)
    hstate = l.sb("hstate", [128, 2, 8], F32)
    hb = l.sb("hb", [128, 2, 8, TCH], BF16)
    evb = l.sb("evb", [128, 8, TCH], BF16)
    a1tm = l.sb("a1tm", [128, 2, D], BF16)
    a1f = l.sb("a1f", [128, D], F32)
    x1t = l.sb("x1t", [128, 2, D], F32)
    x2t = l.sb("x2t", [128, 2, D], F32)
    wrm = l.sb("wrm", [128, 8], F32)

    l.open_block()
    block = l.block
    for nm in ("trS", "stA", "trA", "qX2", "stX"):
        l.sem(nm)
    dmp_a = l.ps("lndmp", [128, D], F32)
    psR = l.ps("psR", [128, TCH])
    psI = l.ps("psI", [128, TCH])
    psIn = l.ps("psIn", [128, TCH])
    psO = [l.ps(f"psO{j}", [128, TCH]) for j in range(2)]

    @block.scalar
    def _(a):
        l.inc(a.activation(wrm[:], wrm[:], AF.Sigmoid), "ac")
    ac_warm = l.cnt["ac"]

    @block.sync
    def _(s):
        for bh, j in ((bin_h, 0), (br_h, 1), (bi_h, 2), (bout_h, 3)):
            l.dinc(s.dma_start(out=bb[:, j, :], in_=bh.rearrange("c p -> p c"),
                               allow_slow_non_contiguous=True), "qC")
        l.dinc(s.dma_start(out=loga_sb[:], in_=loga_h.rearrange("c p -> p c"),
                           allow_slow_non_contiguous=True), "qC")
    ld_c = l.cnt["qC"]

    @block.gpsimd
    def _(g):
        for wh, wsb in ((win_h, wini), (wr_h, wri), (wi_h, wii), (wout_h, wouti)):
            for ft in range(8):
                l.dinc(g.dma_start(out=wsb[:, ft, :],
                                   in_=wh[ft * 128:(ft + 1) * 128, :]), "qW")
    ld_w = l.cnt["qW"]

    @block.scalar
    def _(a):
        a.wait_ge(l.s("qC"), ld_c)
        l.inc(a.activation(a_ch[:], loga_sb[:], AF.Sigmoid), "ac")
    ac_ach = l.cnt["ac"]

    @block.vector
    def _(v):
        n = l.inc(v.memset(ones512[:], 1.0), "dv")
        n = l.inc(v.memset(hstate[:], 0.0), "dv")
        v.wait_ge(l.s("dv"), n)
    dv_init = l.cnt["dv"]

    apply_dv = {}
    xnst_after = {}
    xr_in = xin.rearrange("(i p) d -> p i d", p=128)
    xnr = xn_dram.rearrange("(i p) d -> p i d", p=128)
    for g in range(NG):
        @block.sync
        def _(s, g=g):
            if g >= 1:
                s.wait_ge(l.s("dv"), apply_dv[g - 1])
            l.dinc(s.dma_start(out=xq[:], in_=xr_in[:, g * G:(g + 1) * G, :]), "qX")
        ld_x = l.cnt["qX"]
        ready = [("qX", ld_x)]
        if g >= 2:
            ready.append(("qN", xnst_after[g - 2]))
        apply_dv[g] = emit_ln(l, xq, xng[:, g % 2], stat, ready=ready, G=G,
                              scratch=(dmp_a, lnps))

        @block.sync
        def _(s, g=g):
            s.wait_ge(l.s("dv"), apply_dv[g])
            l.dinc(s.dma_start(out=xnr[:, g * G:(g + 1) * G, :],
                               in_=xng[:, g % 2]), "qN")
            xnst_after[g] = l.cnt["qN"]

    trs_after = {}
    peProj = {}
    peProj_all = {}
    actG = {}
    dveG = {}
    hb_all = {}
    peO = {}
    peO_all = {}
    evbAct = {}
    stA_after = {}

    def emit_transp(tch):
        @block.sync
        def _(s, tch=tch):
            s.wait_ge(l.s("qN"), xnst_after[(tch * TCH) // (G * 128)])
            if tch >= 2:
                s.wait_ge(l.s("pe"), peProj_all[tch - 2])
            for ft in range(8):
                l.dinc(s.dma_start(
                    out=xnT[:, tch % 2, ft, :],
                    in_=xn_dram[tch * TCH:(tch + 1) * TCH, ft * 128:(ft + 1) * 128],
                    transpose=True), "trS")
            trs_after[tch] = l.cnt["trS"]

    emit_transp(0)
    emit_transp(1)
    for tch in range(NTCH):
        for ct in range(8):
            @block.tensor
            def _(t, tch=tch, ct=ct):
                if ct == 0:
                    t.wait_ge(l.s("trS"), trs_after[tch])
                    if tch == 0:
                        t.wait_ge(l.s("qW"), ld_w)
                prev = (tch, ct - 1) if ct > 0 else (tch - 1, 7)
                if prev in actG:
                    t.wait_ge(l.s("ac"), actG[prev])
                if prev in dveG:
                    t.wait_ge(l.s("dv"), dveG[prev])
                for ps, wsb in ((psR, wri), (psI, wii), (psIn, wini)):
                    for ft in range(8):
                        mm = t.matmul(ps[:], wsb[:, ft, ct * 128:(ct + 1) * 128],
                                      xnT[:, tch % 2, ft, :],
                                      start=(ft == 0), stop=(ft == 7))
                l.inc(mm, "pe")
                peProj[(tch, ct)] = l.cnt["pe"]
                if ct == 7:
                    peProj_all[tch] = l.cnt["pe"]

            @block.scalar
            def _(a, tch=tch, ct=ct):
                a.wait_ge(l.s("pe"), peProj[(tch, ct)])
                if tch == 0 and ct == 0:
                    a.wait_ge(l.s("ac"), ac_ach)
                prev = (tch, ct - 1) if ct > 0 else (tch - 1, 7)
                if prev in dveG:
                    a.wait_ge(l.s("dv"), dveG[prev])
                l.inc(a.activation(rsb[:], psR[:], AF.Sigmoid,
                                   bias=bb[:, 1, ct:ct + 1]), "ac")
                l.inc(a.activation(isb[:], psI[:], AF.Sigmoid,
                                   bias=bb[:, 2, ct:ct + 1]), "ac")
                actG[(tch, ct)] = l.cnt["ac"]

            @block.vector
            def _(v, tch=tch, ct=ct):
                def fin(instr):
                    n = l.inc(instr, "dv")
                    v.wait_ge(l.s("dv"), n)
                v.wait_ge(l.s("ac"), actG[(tch, ct)])
                if tch == 0 and ct == 0:
                    v.wait_ge(l.s("dv"), dv_init)
                if ct == 0 and tch >= 2:
                    v.wait_ge(l.s("pe"), peO_all[tch - 2])
                fin(v.tensor_scalar(inbb[:], psIn[:], bb[:, 0, ct:ct + 1], None,
                                    ALU.add))
                fin(v.tensor_scalar(h1b[:], rsb[:], a_ch[:, ct:ct + 1], None,
                                    ALU.mult))
                at = h1b
                fin(v.tensor_tensor(rsb[:], isb[:], inbb[:], ALU.mult))
                fin(v.tensor_scalar(isb[:], at[:], -1.0, 1.0, ALU.mult,
                                    ALU.add))
                fin(v.tensor_tensor(inbb[:], isb[:], rsb[:], ALU.mult))
                fin(v.tensor_tensor_scan(cpb[:], at[:], inbb[:], 0.0,
                                         ALU.mult, ALU.add))
                fin(v.tensor_tensor_scan(rsb[:], at[:], ones512[:], 1.0,
                                         ALU.mult, ALU.mult))
                fin(v.scalar_tensor_tensor(isb[:], rsb[:],
                                           hstate[:, tch % 2, ct:ct + 1], cpb[:],
                                           ALU.mult, ALU.add))
                fin(v.tensor_copy(hstate[:, (tch + 1) % 2, ct:ct + 1],
                                  isb[:, TCH - 1:TCH]))
                fin(v.tensor_copy(hb[:, tch % 2, ct, :], isb[:]))
                dveG[(tch, ct)] = l.cnt["dv"]
        hb_all[tch] = l.cnt["dv"]

        for dt in range(8):
            @block.tensor
            def _(t, tch=tch, dt=dt):
                if dt == 0:
                    t.wait_ge(l.s("dv"), hb_all[tch])
                prevo = (tch, dt - 2) if dt >= 2 else (tch - 1, dt % 2 + 6)
                if prevo in evbAct:
                    t.wait_ge(l.s("dv"), evbAct[prevo])
                for ct in range(8):
                    mm = t.matmul(psO[dt % 2][:],
                                  wouti[:, ct, dt * 128:(dt + 1) * 128],
                                  hb[:, tch % 2, ct, :],
                                  start=(ct == 0), stop=(ct == 7))
                l.inc(mm, "pe")
                peO[(tch, dt)] = l.cnt["pe"]
                if dt == 7:
                    peO_all[tch] = l.cnt["pe"]

            @block.vector
            def _(v, tch=tch, dt=dt):
                v.wait_ge(l.s("pe"), peO[(tch, dt)])
                if dt == 0 and tch >= 1:
                    v.wait_ge(l.s("stA"), stA_after[tch - 1])
                n = l.inc(v.tensor_scalar(evb[:, dt, :], psO[dt % 2][:],
                                          bb[:, 3, dt:dt + 1], None, ALU.add),
                          "dv")
                v.wait_ge(l.s("dv"), n)
                evbAct[(tch, dt)] = l.cnt["dv"]

        @block.sync
        def _(s, tch=tch):
            s.wait_ge(l.s("dv"), evbAct[(tch, 7)])
            ar = a1fm.rearrange("(dt p) t -> p dt t", p=128)
            l.dinc(s.dma_start(out=ar[:, :, tch * TCH:(tch + 1) * TCH],
                               in_=evb[:]), "stA")
            stA_after[tch] = l.cnt["stA"]
        if tch + 2 < NTCH:
            emit_transp(tch + 2)

    add_dv = {}
    stx_after = {}
    xo_r = xout_h.rearrange("(i p) d -> p i d", p=128)
    for i in range(NT):
        @block.sync
        def _(s, i=i):
            s.wait_ge(l.s("stA"), stA_after[(i * 128) // TCH])
            if i >= 2:
                s.wait_ge(l.s("dv"), add_dv[i - 2])
            l.dinc(s.dma_start(out=a1tm[:, i % 2, :],
                               in_=a1fm[:, i * 128:(i + 1) * 128],
                               transpose=True), "trA")
            l.dinc(s.dma_start(out=x1t[:, i % 2, :], in_=xr_in[:, i, :]), "qX2")
        tra = l.cnt["trA"]
        qx2 = l.cnt["qX2"]

        @block.vector
        def _(v, i=i):
            def fin(instr):
                n = l.inc(instr, "dv")
                v.wait_ge(l.s("dv"), n)
            v.wait_ge(l.s("trA"), tra)
            v.wait_ge(l.s("qX2"), qx2)
            if i >= 2:
                v.wait_ge(l.s("stX"), stx_after[i - 2])
            fin(v.tensor_copy(a1f[:], a1tm[:, i % 2, :]))
            fin(v.tensor_tensor(x2t[:, i % 2, :], x1t[:, i % 2, :], a1f[:],
                                ALU.add))
            add_dv[i] = l.cnt["dv"]

        @block.sync
        def _(s, i=i):
            s.wait_ge(l.s("dv"), add_dv[i])
            l.dinc(s.dma_start(out=xo_r[:, i, :], in_=x2t[:, i % 2, :]), "stX")
            stx_after[i] = l.cnt["stX"]

    @block.sync
    def _(s):
        s.wait_ge(l.s("stX"), l.cnt["stX"])
        s.wait_ge(l.s("stA"), l.cnt["stA"])
        s.wait_ge(l.s("qN"), l.cnt["qN"])
    return l


def build_mesa_launch(debug=False, ns_it=None):
    H, C, NCH, NGR = 128, 128, 32, 4
    TCH, NTCH = 512, 8
    NS_IT, KCH, CHEB_L = 12, 14, 16.0
    if ns_it is not None:
        NS_IT = ns_it
    EPS = 1e-6
    G = 8
    NG = NT // G

    l = L("mesa")
    xin = l.dram("xin", [T, D], F32, kind="ExternalInput")
    wq_h = l.dram("wqT", [D, H], F32, kind="ExternalInput")
    wk_h = l.dram("wkT", [D, H], F32, kind="ExternalInput")
    wv_h = l.dram("wvT", [D, H], F32, kind="ExternalInput")
    wo_h = l.dram("woT", [H, D], F32, kind="ExternalInput")
    bq_h = l.dram("bq", [128, 1], F32, kind="ExternalInput")
    bk_h = l.dram("bk", [128, 1], F32, kind="ExternalInput")
    bv_h = l.dram("bv", [128, 1], F32, kind="ExternalInput")
    bob_h = l.dram("bob", [8, 128], F32, kind="ExternalInput")
    identb_h = l.dram("identb", [128, 128], BF16, kind="ExternalInput")
    trif_h = l.dram("trif", [128, 128], F32, kind="ExternalInput")
    identf_h = l.dram("identf", [128, 128], F32, kind="ExternalInput")
    xout_h = l.dram("xout", [T, D], F32, kind="ExternalOutput")
    xn_dram = l.dram("xn_scr", [T, D], BF16, kind="Internal")
    a2fm = l.dram("a2fm", [D, T], BF16,
                  kind="ExternalOutput" if debug else "Internal")
    dbg = {}
    if debug:
        for nm, w in (("dbg_qt", T), ("dbg_ut", T), ("dbg_vst", T),
                      ("dbg_x", NCH * H), ("dbg_s", NCH * H),
                      ("dbg_yn", NCH * H), ("dbg_ot", T), ("dbg_wt", NCH * H),
                      ("dbg_ms", NCH * H), ("dbg_utok", NCH * H)):
            dbg[nm] = l.dram(nm, [128, w], BF16, kind="ExternalOutput")

    xq = l.sb("xq", [128, G, D], F32)
    xng = l.sb("xng", [128, G, D], BF16)
    stat = l.sb("stat", [128, 4 * G], F32)
    lnps = l.sb("lnps", [128, 5, G], F32)
    xnT = l.sb("xnT", [128, 2, 8, TCH], BF16)
    wq_sb = l.sb("wq_sb", [128, 8, H], BF16)
    wk_sb = l.sb("wk_sb", [128, 8, H], BF16)
    wv_sb = l.sb("wv_sb", [128, 8, H], BF16)
    wo_sb = l.sb("wo_sb", [128, 8 * H], BF16)
    bqc = l.sb("bqc", [128, 1], F32)
    bkc = l.sb("bkc", [128, 1], F32)
    bvc = l.sb("bvc", [128, 1], F32)
    bob_sb = l.sb("bob_sb", [128, 8], F32)
    identb = l.sb("identb_sb", [128, 128], BF16)
    trif = l.sb("trif_sb", [128, 128], F32)
    identf = l.sb("identf_sb", [128, 128], F32)
    QT = l.sb("QT", [128, T], BF16)
    UT = l.sb("UT", [128, T], BF16)
    VsT = l.sb("VsT", [128, T], BF16)
    OT = l.sb("OT", [128, T], BF16)
    arena = l.sb("arena", [128, 2048], F32)
    KTf = arena[:, 0:TCH]
    VTf = arena[:, TCH:2 * TCH]
    prodsb = arena[:, 2 * TCH:3 * TCH]
    sqb = arena[:, 3 * TCH:4 * TCH]
    rowA = l.sb("rowA", [1, TCH], F32)
    rowB = l.sb("rowB", [1, TCH], F32)
    srow = l.sb("srow", [1, TCH], F32)
    nfrow = l.sb("nfrow", [1, NCH], F32)
    nfs = l.sb("nfs", [1, 4 * NCH], F32)
    rowscr = l.sb("rowscr", [1, 128], F32)
    ones1 = l.sb("ones1", [1, 128], F32)
    onescol = l.sb("onescol", [128, 1], F32)
    cbsb = l.sb("cbsb", [128, NCH], F32)
    Gf = arena[:, 0:128]
    Gfsq = arena[:, 128:256]
    Utok = l.sb("Utok", [128, NCH * H], BF16)
    Sb = Utok
    Ms = l.sb("Ms", [128, NCH * H], BF16)
    Xb = l.sb("Xb", [128, NCH * H], BF16)
    tbb = l.sb("tbb", [128, 8 * H], BF16)
    XTb = l.sb("XTb", [128, 8 * H], BF16)
    mtw = l.sb("mtw", [128, 8 * H], F32)
    twoI = mtw
    maskb = mtw
    WTb = l.sb("WTb", [128, NCH * H], BF16)
    Bbf = l.sb("Bbf", [128, NCH * H], BF16)
    Yb = l.sb("Yb", [128, 8 * H], F32)
    Dl = l.sb("Dl", [128, 8 * H], F32)
    Ybn = l.sb("Ybn", [128, NCH * H], BF16)
    tA = arena[:, 0:8 * H]
    tB = arena[:, 8 * H:16 * H]
    ZTsb = l.sb("ZTsb", [128, H], BF16)
    Wtok = l.sb("Wtok", [128, H], BF16)
    Ztok = l.sb("Ztok", [128, H], BF16)
    Vstok = l.sb("Vstok", [128, H], BF16)
    awm = l.sb("awm", [128, H], BF16)
    MTsb = l.sb("MTsb", [128, H], BF16)
    wrm = l.sb("wrm", [128, 8], F32)

    evb = xnT[:, 0, :, :]
    a2tm = xng[:, 0:2, :]
    l.open_block()
    block = l.block
    for nm in ("trS", "stA", "trA", "qX2", "stX"):
        l.sem(nm)
    P1 = l.ps("P1", [128, TCH])
    P2 = l.ps("P2", [128, TCH])
    PG = l.ps("PG", [128, H])
    PW = l.ps("PW", [128, 1024])
    PV = l.ps("PV", [128, 1024], BF16)
    PT = l.ps("PT", [128, H], BF16)

    @block.scalar
    def _(a):
        l.inc(a.activation(wrm[:], wrm[:], AF.Sqrt), "ac")

    @block.sync
    def _(s):
        for hh, sb_ in ((bq_h, bqc), (bk_h, bkc), (bv_h, bvc)):
            l.dinc(s.dma_start(out=sb_[:], in_=hh[:],
                               allow_slow_non_contiguous=True), "qC")
        l.dinc(s.dma_start(out=bob_sb[:], in_=bob_h.rearrange("c p -> p c"),
                           allow_slow_non_contiguous=True), "qC")
        l.dinc(s.dma_start(out=identb[:], in_=identb_h[:]), "qC")
        l.dinc(s.dma_start(out=trif[:], in_=trif_h[:]), "qC")
        l.dinc(s.dma_start(out=identf[:], in_=identf_h[:]), "qC")
    ld_c = l.cnt["qC"]

    @block.gpsimd
    def _(g):
        for wh, wsb in ((wq_h, wq_sb), (wk_h, wk_sb), (wv_h, wv_sb)):
            for ft in range(8):
                l.dinc(g.dma_start(out=wsb[:, ft, :],
                                   in_=wh[ft * 128:(ft + 1) * 128, :]), "qW")
        l.dinc(g.dma_start(out=wo_sb[:], in_=wo_h[:, :]), "qW")
    ld_w = l.cnt["qW"]

    @block.vector
    def _(v):
        def fin(instr):
            n = l.inc(instr, "dv")
            v.wait_ge(l.s("dv"), n)
        v.wait_ge(l.s("qC"), ld_c)
        fin(v.memset(ones1[:], 1.0))
        fin(v.memset(onescol[:], 1.0))
        fin(v.memset(MTsb[:], 0.0))
        fin(v.memset(nfrow[:], 0.0))
        for j in range(8):
            fin(v.tensor_scalar(twoI[:, j * H:(j + 1) * H], identf[:], 2.0,
                                None, ALU.mult))
    dv_init = l.cnt["dv"]

    apply_dv = {}
    xnst_after = {}
    xr_in = xin.rearrange("(i p) d -> p i d", p=128)
    xnr = xn_dram.rearrange("(i p) d -> p i d", p=128)
    for g in range(NG):
        @block.sync
        def _(s, g=g):
            if g >= 1:
                s.wait_ge(l.s("dv"), apply_dv[g - 1])
            l.dinc(s.dma_start(out=xq[:], in_=xr_in[:, g * G:(g + 1) * G, :]), "qX")
        ld_x = l.cnt["qX"]
        ready = [("qX", ld_x)]
        if g >= 1:
            ready.append(("qN", xnst_after[g - 1]))
        apply_dv[g] = emit_ln(l, xq, xng, stat, ready=ready, G=G,
                              scratch=(PW, lnps))

        @block.sync
        def _(s, g=g):
            s.wait_ge(l.s("dv"), apply_dv[g])
            l.dinc(s.dma_start(out=xnr[:, g * G:(g + 1) * G, :], in_=xng[:]), "qN")
            xnst_after[g] = l.cnt["qN"]
    ln_total = l.cnt["dv"]

    trs_after = {}
    peProj_all = {}
    dvU = {}
    dvVs = {}
    dvP1free = {-1: 0}
    dvP2free = {-1: 0}
    dvPWfree = {-1: ln_total}

    def emit_transp(tch):
        @block.sync
        def _(s, tch=tch):
            s.wait_ge(l.s("qN"), xnst_after[(tch * TCH) // (G * 128)])
            if tch >= 2:
                s.wait_ge(l.s("pe"), peProj_all[tch - 2])
            for ft in range(8):
                l.dinc(s.dma_start(
                    out=xnT[:, tch % 2, ft, :],
                    in_=xn_dram[tch * TCH:(tch + 1) * TCH, ft * 128:(ft + 1) * 128],
                    transpose=True), "trS")
            trs_after[tch] = l.cnt["trS"]

    emit_transp(0)
    emit_transp(1)
    for tch in range(NTCH):
        sl = slice(tch * TCH, (tch + 1) * TCH)

        @block.tensor
        def _(t, tch=tch):
            t.wait_ge(l.s("trS"), trs_after[tch])
            if tch == 0:
                t.wait_ge(l.s("qW"), ld_w)
            t.wait_ge(l.s("dv"), dvP1free[tch - 1])
            for ft in range(8):
                mm = t.matmul(P1[:], wq_sb[:, ft, :], xnT[:, tch % 2, ft, :],
                              start=(ft == 0), stop=(ft == 7))
            l.inc(mm, "pe")
        peQ = l.cnt["pe"]

        @block.vector
        def _(v, tch=tch, sl=sl):
            def fin(instr):
                n = l.inc(instr, "dv")
                v.wait_ge(l.s("dv"), n)
            v.wait_ge(l.s("pe"), peQ)
            fin(v.tensor_scalar(QT[:, sl], P1[:], bqc[:, 0:1], None, ALU.add))
        dvQT = l.cnt["dv"]

        @block.tensor
        def _(t, tch=tch):
            t.wait_ge(l.s("dv"), dvP2free[tch - 1])
            for ft in range(8):
                mm = t.matmul(P2[:], wk_sb[:, ft, :], xnT[:, tch % 2, ft, :],
                              start=(ft == 0), stop=(ft == 7))
            l.inc(mm, "pe")
        peK = l.cnt["pe"]

        @block.vector
        def _(v, tch=tch, sl=sl):
            def fin(instr):
                n = l.inc(instr, "dv")
                v.wait_ge(l.s("dv"), n)
            v.wait_ge(l.s("pe"), peK)
            fin(v.tensor_scalar(KTf[:], P2[:], bkc[:, 0:1], None, ALU.add))
            fin(v.scalar_tensor_tensor(prodsb[:], P1[:], bqc[:, 0:1], KTf[:],
                                       ALU.add, ALU.mult))
            fin(v.tensor_tensor(sqb[:], KTf[:], KTf[:], ALU.mult))
        dvProd = l.cnt["dv"]

        @block.tensor
        def _(t, tch=tch):
            t.wait_ge(l.s("dv"), dvProd)
            for ft in range(8):
                mm = t.matmul(P1[:], wv_sb[:, ft, :], xnT[:, tch % 2, ft, :],
                              start=(ft == 0), stop=(ft == 7))
            t.matmul(P2[0:1, :], onescol[:], prodsb[:], start=True, stop=True)
            mm = t.matmul(P2[32:33, :], onescol[:], sqb[:], start=True, stop=True)
            l.inc(mm, "pe")
        peV = l.cnt["pe"]

        @block.scalar
        def _(a, tch=tch):
            a.wait_ge(l.s("pe"), peV)
            l.inc(a.activation(rowA[:], P2[32:33, :], AF.Sqrt), "ac")
        acSq = l.cnt["ac"]

        @block.vector
        def _(v, tch=tch, sl=sl):
            def fin(instr):
                n = l.inc(instr, "dv")
                v.wait_ge(l.s("dv"), n)
            v.wait_ge(l.s("ac"), acSq)
            fin(v.tensor_scalar(VTf[:], P1[:], bvc[:, 0:1], None, ALU.add))
            fin(v.tensor_scalar(rowB[:], rowA[:], EPS, None, ALU.add))
            fin(v.reciprocal(rowA[:], rowB[:]))
            fin(v.tensor_copy(srow[:], P2[0:1, :]))
        dvRows = l.cnt["dv"]

        @block.tensor
        def _(t, tch=tch):
            t.wait_ge(l.s("dv"), dvRows)
            t.wait_ge(l.s("dv"), dvPWfree[tch - 1])
            t.matmul(PW[:, 0:TCH], ones1[:], rowA[:], start=True, stop=True)
            mm = t.matmul(PW[:, TCH:2 * TCH], ones1[:], srow[:], start=True,
                          stop=True)
            l.inc(mm, "pe")
        peBc = l.cnt["pe"]

        @block.vector
        def _(v, tch=tch, sl=sl):
            def fin(instr):
                n = l.inc(instr, "dv")
                v.wait_ge(l.s("dv"), n)
            v.wait_ge(l.s("pe"), peBc)
            fin(v.tensor_tensor(UT[:, sl], KTf[:], PW[:, 0:TCH], ALU.mult))
            fin(v.tensor_tensor(VsT[:, sl], VTf[:], PW[:, TCH:2 * TCH], ALU.mult))
        dvU[tch] = l.cnt["dv"]
        dvVs[tch] = l.cnt["dv"]
        dvP1free[tch] = dvRows
        dvP2free[tch] = dvRows
        dvPWfree[tch] = l.cnt["dv"]
        peProj_all[tch] = peV
        if tch + 2 < NTCH:
            emit_transp(tch + 2)
    proj_dv_total = l.cnt["dv"]

    dvUtok = {-1: 0}
    dvSnap = {-1: 0}
    nfacc = {-1: 0}
    peG_i = {}
    for i in range(NCH):
        csl = slice(i * H, (i + 1) * H)

        @block.tensor
        def _(t, i=i, csl=csl):
            t.wait_ge(l.s("dv"), dvU[i // 4])
            t.wait_ge(l.s("dv"), dvUtok[i - 1])
            l.inc(t.transpose(PT[:], UT[:, csl], identb[:]), "pe")
        peT = l.cnt["pe"]

        @block.vector
        def _(v, i=i, csl=csl):
            def fin(instr):
                n = l.inc(instr, "dv")
                v.wait_ge(l.s("dv"), n)
            v.wait_ge(l.s("pe"), peT)
            fin(v.tensor_copy(Utok[:, csl], PT[:]))
            if i == 0:
                fin(v.tensor_copy(Ms[:, csl], identf[:]))
                fin(v.memset(Gf[:], 0.0))
            else:
                fin(v.tensor_tensor(Ms[:, csl], PG[:], identf[:], ALU.add))
                fin(v.tensor_copy(Gf[:], PG[:]))
            fin(v.tensor_tensor(Gfsq[:], Gf[:], Gf[:], ALU.mult))
        dvUtok[i] = l.cnt["dv"]
        dvSnap[i] = l.cnt["dv"]

        @block.tensor
        def _(t, i=i):
            t.wait_ge(l.s("dv"), dvSnap[i])
            t.wait_ge(l.s("dv"), nfacc[i - 1])
            t.matmul(P2[64:65, 0:128], onescol[:], Gfsq[:], start=True, stop=True)
            mm = t.matmul(PG[:], Utok[:, i * H:(i + 1) * H],
                          Utok[:, i * H:(i + 1) * H],
                          start=(i == 0), stop=True)
            l.inc(mm, "pe")
        peG_i[i] = l.cnt["pe"]

        @block.vector
        def _(v, i=i):
            v.wait_ge(l.s("pe"), peG_i[i])
            n = l.inc(v.tensor_scalar(rowscr[:], P2[64:65, 0:128], 1.0, 0.0,
                                      ALU.mult, ALU.add,
                                      accum_out=nfrow[:, i:i + 1]), "dv")
            v.wait_ge(l.s("dv"), n)
        nfacc[i] = l.cnt["dv"]

    @block.scalar
    def _(a):
        a.wait_ge(l.s("dv"), nfacc[NCH - 1])
        l.inc(a.activation(nfs[:, 0:NCH], nfrow[:], AF.Sqrt), "ac")
    ac_nf = l.cnt["ac"]

    @block.vector
    def _(v):
        def fin(instr):
            n = l.inc(instr, "dv")
            v.wait_ge(l.s("dv"), n)
        v.wait_ge(l.s("ac"), ac_nf)
        fin(v.tensor_scalar(nfs[:, NCH:2 * NCH], nfs[:, 0:NCH], 2.0, None,
                            ALU.add))
        fin(v.reciprocal(nfs[:, 2 * NCH:3 * NCH], nfs[:, NCH:2 * NCH]))
        fin(v.tensor_scalar(nfs[:, 3 * NCH:4 * NCH], nfs[:, 2 * NCH:3 * NCH],
                            2.0, None, ALU.mult))
    dv_c2 = l.cnt["dv"]

    @block.tensor
    def _(t):
        t.wait_ge(l.s("dv"), dv_c2)
        l.inc(t.matmul(P2[:, 0:NCH], ones1[:], nfs[:, 3 * NCH:4 * NCH],
                       start=True, stop=True), "pe")
    pe_cb = l.cnt["pe"]

    @block.vector
    def _(v):
        def fin(instr):
            n = l.inc(instr, "dv")
            v.wait_ge(l.s("dv"), n)
        v.wait_ge(l.s("pe"), pe_cb)
        fin(v.tensor_copy(cbsb[:], P2[:, 0:NCH]))
        for i in range(NCH):
            fin(v.tensor_scalar(Xb[:, i * H:(i + 1) * H], identf[:],
                                cbsb[:, i:i + 1], None, ALU.mult))
    dv_x0 = l.cnt["dv"]

    last_PW = {"v": dv_x0}
    last_PV = {"v": 0}
    for it in range(NS_IT):
        for grp in range(NGR):
            gsl = slice(grp * 8 * H, (grp + 1) * 8 * H)

            @block.tensor
            def _(t, it=it, grp=grp):
                t.wait_ge(l.s("dv"), last_PW["v"])
                t.wait_ge(l.s("dv"), last_PV["v"])
                for ci in range(8):
                    i = grp * 8 + ci
                    t.transpose(PV[:, ci * H:(ci + 1) * H],
                                Xb[:, i * H:(i + 1) * H], identb[:])
                for ci in range(8):
                    i = grp * 8 + ci
                    mm = t.matmul(PW[:, ci * H:(ci + 1) * H],
                                  Ms[:, i * H:(i + 1) * H],
                                  Xb[:, i * H:(i + 1) * H],
                                  start=True, stop=True)
                l.inc(mm, "pe")
            peMX = l.cnt["pe"]

            @block.vector
            def _(v, it=it, grp=grp, gsl=gsl):
                def fin(instr):
                    n = l.inc(instr, "dv")
                    v.wait_ge(l.s("dv"), n)
                v.wait_ge(l.s("pe"), peMX)
                fin(v.tensor_copy(XTb[:], PV[:]))
                fin(v.scalar_tensor_tensor(tbb[:], PW[:], -1.0, twoI[:],
                                           ALU.mult, ALU.add))
            last_PW["v"] = l.cnt["dv"]
            dvTb = l.cnt["dv"]

            @block.tensor
            def _(t, it=it, grp=grp):
                t.wait_ge(l.s("dv"), dvTb)
                for ci in range(8):
                    mm = t.matmul(PW[:, ci * H:(ci + 1) * H],
                                  XTb[:, ci * H:(ci + 1) * H],
                                  tbb[:, ci * H:(ci + 1) * H],
                                  start=True, stop=True)
                l.inc(mm, "pe")
            peXp = l.cnt["pe"]

            @block.vector
            def _(v, it=it, grp=grp, gsl=gsl):
                v.wait_ge(l.s("pe"), peXp)
                n = l.inc(v.tensor_copy(Xb[:, gsl], PW[:]), "dv")
                v.wait_ge(l.s("dv"), n)
            last_PW["v"] = l.cnt["dv"]
            last_PV["v"] = dvTb
    ns_total = l.cnt["dv"]

    dvWT = {-1: ns_total}
    dvB = {-1: 0}
    for i in range(NCH):
        csl = slice(i * H, (i + 1) * H)

        @block.tensor
        def _(t, i=i, csl=csl):
            t.wait_ge(l.s("dv"), dvWT[i - 1])
            l.inc(t.matmul(P1[0:H, :H], Xb[:, csl], UT[:, csl],
                           start=True, stop=True), "pe")
        peWT = l.cnt["pe"]

        @block.vector
        def _(v, i=i, csl=csl):
            v.wait_ge(l.s("pe"), peWT)
            n = l.inc(v.tensor_copy(WTb[:, csl], P1[0:H, :H]), "dv")
            v.wait_ge(l.s("dv"), n)
        dvWT[i] = l.cnt["dv"]

        @block.tensor
        def _(t, i=i, csl=csl):
            t.wait_ge(l.s("dv"), dvWT[i])
            t.wait_ge(l.s("dv"), dvB[i - 1])
            l.inc(t.matmul(P2[0:H, :H], UT[:, csl], WTb[:, csl],
                           start=True, stop=True), "pe")
        peS = l.cnt["pe"]

        @block.vector
        def _(v, i=i, csl=csl):
            def fin(instr):
                n = l.inc(instr, "dv")
                v.wait_ge(l.s("dv"), n)
            v.wait_ge(l.s("pe"), peS)
            fin(v.tensor_copy(Sb[:, csl], P2[0:H, :H]))
            fin(v.tensor_tensor(Bbf[:, csl], trif[:], P2[0:H, :H], ALU.mult))
        dvB[i] = l.cnt["dv"]

    d_ = 1.0 + CHEB_L / 2.0
    cc_ = CHEB_L / 2.0
    sigma1 = d_ / cc_
    rho = 1.0 / sigma1
    coefs = []
    for it in range(KCH - 1):
        rho_new = 1.0 / (2.0 * sigma1 - rho)
        coefs.append((rho_new * rho, 2.0 * rho_new / cc_))
        rho = rho_new

    @block.vector
    def _(v):
        def fin(instr):
            n = l.inc(instr, "dv")
            v.wait_ge(l.s("dv"), n)
        v.wait_ge(l.s("dv"), dvB[NCH - 1])
        for j in range(8):
            fin(v.tensor_copy(maskb[:, j * H:(j + 1) * H], trif[:]))
    dv_mask = l.cnt["dv"]

    last_PWc = {"v": dv_mask}
    for grp in range(NGR):
        gsl = slice(grp * 8 * H, (grp + 1) * 8 * H)

        @block.vector
        def _(v, grp=grp, gsl=gsl):
            def fin(instr):
                n = l.inc(instr, "dv")
                v.wait_ge(l.s("dv"), n)
            fin(v.tensor_scalar(Dl[:], Bbf[:, gsl], 1.0 / d_, None, ALU.mult))
            fin(v.tensor_copy(Yb[:], Dl[:]))
            fin(v.tensor_scalar(Ybn[:, gsl], Dl[:], -1.0, None, ALU.mult))
        last_PWc["v"] = l.cnt["dv"]

        for it in range(KCH - 1):
            c1, c2 = coefs[it]

            @block.tensor
            def _(t, it=it, grp=grp):
                t.wait_ge(l.s("dv"), last_PWc["v"])
                for ci in range(8):
                    i = grp * 8 + ci
                    mm = t.matmul(PW[:, ci * H:(ci + 1) * H],
                                  Sb[:, i * H:(i + 1) * H],
                                  Ybn[:, i * H:(i + 1) * H],
                                  start=True, stop=True)
                l.inc(mm, "pe")
            peSY = l.cnt["pe"]

            @block.vector
            def _(v, it=it, grp=grp, gsl=gsl, c1=c1, c2=c2):
                def fin(instr):
                    n = l.inc(instr, "dv")
                    v.wait_ge(l.s("dv"), n)
                v.wait_ge(l.s("pe"), peSY)
                fin(v.tensor_tensor(tA[:], maskb[:], PW[:], ALU.mult))
                fin(v.tensor_tensor(tB[:], Yb[:], tA[:], ALU.subtract))
                fin(v.tensor_tensor(tA[:], Bbf[:, gsl], tB[:], ALU.subtract))
                fin(v.tensor_scalar(tB[:], Dl[:], c1, None, ALU.mult))
                fin(v.scalar_tensor_tensor(Dl[:], tA[:], c2, tB[:],
                                           ALU.mult, ALU.add))
                fin(v.tensor_tensor(tA[:], Yb[:], Dl[:], ALU.add))
                fin(v.tensor_copy(Yb[:], tA[:]))
                fin(v.tensor_scalar(Ybn[:, gsl], tA[:], -1.0, None, ALU.mult))
            last_PWc["v"] = l.cnt["dv"]
    cheb_total = l.cnt["dv"]

    dvSeq = {-1: cheb_total}
    dvOT = {}
    dvMT = {-1: dv_init}
    for i in range(NCH):
        csl = slice(i * H, (i + 1) * H)

        @block.tensor
        def _(t, i=i, csl=csl):
            t.wait_ge(l.s("dv"), dvSeq[i - 1])
            l.inc(t.transpose(PT[:], WTb[:, csl], identb[:]), "pe")
        peWtokT = l.cnt["pe"]

        @block.vector
        def _(v, i=i):
            v.wait_ge(l.s("pe"), peWtokT)
            n = l.inc(v.tensor_copy(Wtok[:], PT[:]), "dv")
            v.wait_ge(l.s("dv"), n)
        dvWtok = l.cnt["dv"]

        @block.tensor
        def _(t, i=i, csl=csl):
            t.wait_ge(l.s("dv"), dvWtok)
            t.matmul(P1[0:H, :H], identb[:], WTb[:, csl], start=True, stop=False)
            l.inc(t.matmul(P1[0:H, :H], Wtok[:], Ybn[:, csl], start=False,
                           stop=True), "pe")
        peZT = l.cnt["pe"]

        @block.vector
        def _(v, i=i):
            v.wait_ge(l.s("pe"), peZT)
            n = l.inc(v.tensor_copy(ZTsb[:], P1[0:H, :H]), "dv")
            v.wait_ge(l.s("dv"), n)
        dvZT = l.cnt["dv"]

        @block.tensor
        def _(t, i=i, csl=csl):
            t.wait_ge(l.s("dv"), dvZT)
            l.inc(t.transpose(PT[:], ZTsb[:], identb[:]), "pe")
        peZtokT = l.cnt["pe"]

        @block.vector
        def _(v, i=i):
            v.wait_ge(l.s("pe"), peZtokT)
            n = l.inc(v.tensor_copy(Ztok[:], PT[:]), "dv")
            v.wait_ge(l.s("dv"), n)
        dvZtok = l.cnt["dv"]

        @block.tensor
        def _(t, i=i, csl=csl):
            t.wait_ge(l.s("dv"), dvZtok)
            l.inc(t.transpose(PT[:], VsT[:, csl], identb[:]), "pe")
        peVsT = l.cnt["pe"]

        @block.vector
        def _(v, i=i):
            v.wait_ge(l.s("pe"), peVsT)
            n = l.inc(v.tensor_copy(Vstok[:], PT[:]), "dv")
            v.wait_ge(l.s("dv"), n)
        dvVstok = l.cnt["dv"]

        @block.tensor
        def _(t, i=i, csl=csl):
            t.wait_ge(l.s("dv"), dvVstok)
            l.inc(t.matmul(P2[0:H, :H], ZTsb[:], QT[:, csl], start=True,
                           stop=True), "pe")
        peAW = l.cnt["pe"]

        @block.vector
        def _(v, i=i):
            v.wait_ge(l.s("pe"), peAW)
            n = l.inc(v.tensor_tensor(awm[:], trif[:], P2[0:H, :H], ALU.mult),
                      "dv")
            v.wait_ge(l.s("dv"), n)
        dvAwm = l.cnt["dv"]

        @block.tensor
        def _(t, i=i, csl=csl):
            t.wait_ge(l.s("dv"), dvAwm)
            t.wait_ge(l.s("dv"), dvMT[i - 1])
            t.matmul(P1[0:H, :H], MTsb[:], QT[:, csl], start=True, stop=False)
            l.inc(t.matmul(P1[0:H, :H], Vstok[:], awm[:], start=False,
                           stop=True), "pe")
            l.inc(t.matmul(PG[:], Ztok[:], Vstok[:], start=(i == 0),
                           stop=True), "pe")
        peO_i = l.cnt["pe"]

        @block.vector
        def _(v, i=i, csl=csl):
            def fin(instr):
                n = l.inc(instr, "dv")
                v.wait_ge(l.s("dv"), n)
            v.wait_ge(l.s("pe"), peO_i)
            fin(v.tensor_copy(OT[:, csl], P1[0:H, :H]))
            fin(v.tensor_copy(MTsb[:], PG[:]))
        dvOT[i] = l.cnt["dv"]
        dvMT[i] = l.cnt["dv"]
        dvSeq[i] = l.cnt["dv"]

    stA_after = {-1: 0}
    dvEvb = {-1: 0}
    for tch in range(NTCH):
        sl = slice(tch * TCH, (tch + 1) * TCH)
        for dt in range(8):
            @block.tensor
            def _(t, tch=tch, dt=dt, sl=sl):
                if dt == 0:
                    t.wait_ge(l.s("dv"), dvOT[(tch + 1) * 4 - 1])
                t.wait_ge(l.s("dv"), dvEvb[(tch, dt - 1) if dt else (tch - 1, 7)]
                          if (dt or tch) else -1) if False else None
                prev = (tch, dt - 1) if dt else (tch - 1, 7)
                if prev in dvEvb:
                    t.wait_ge(l.s("dv"), dvEvb[prev])
                l.inc(t.matmul(P2[:], wo_sb[:, dt * H:(dt + 1) * H], OT[:, sl],
                               start=True, stop=True), "pe")
            peOut = l.cnt["pe"]

            @block.vector
            def _(v, tch=tch, dt=dt):
                v.wait_ge(l.s("pe"), peOut)
                if dt == 0 and tch >= 1:
                    v.wait_ge(l.s("stA"), stA_after[tch - 1])
                n = l.inc(v.tensor_scalar(evb[:, dt, :], P2[:],
                                          bob_sb[:, dt:dt + 1], None, ALU.add),
                          "dv")
                v.wait_ge(l.s("dv"), n)
            dvEvb[(tch, dt)] = l.cnt["dv"]

        @block.sync
        def _(s, tch=tch, sl=sl):
            s.wait_ge(l.s("dv"), dvEvb[(tch, 7)])
            ar = a2fm.rearrange("(dt p) t -> p dt t", p=128)
            l.dinc(s.dma_start(out=ar[:, :, sl], in_=evb[:]), "stA")
            stA_after[tch] = l.cnt["stA"]

    add_dv = {}
    stx_after = {}
    xo_r = xout_h.rearrange("(i p) d -> p i d", p=128)
    x4t = xq[:, 0:2, :]
    x5t = xq[:, 2:4, :]
    for i in range(NT):
        @block.sync
        def _(s, i=i):
            if i == 0:
                s.wait_ge(l.s("qN"), l.cnt["qN"])
            s.wait_ge(l.s("stA"), stA_after[(i * 128) // TCH])
            if i >= 2:
                s.wait_ge(l.s("dv"), add_dv[i - 2])
            l.dinc(s.dma_start(out=a2tm[:, i % 2, :],
                               in_=a2fm[:, i * 128:(i + 1) * 128],
                               transpose=True), "trA")
            l.dinc(s.dma_start(out=x4t[:, i % 2, :], in_=xr_in[:, i, :]), "qX2")
        tra = l.cnt["trA"]
        qx2 = l.cnt["qX2"]

        @block.vector
        def _(v, i=i):
            v.wait_ge(l.s("trA"), tra)
            v.wait_ge(l.s("qX2"), qx2)
            if i >= 2:
                v.wait_ge(l.s("stX"), stx_after[i - 2])
            n = l.inc(v.tensor_tensor(x5t[:, i % 2, :], x4t[:, i % 2, :],
                                      a2tm[:, i % 2, :], ALU.add), "dv")
            v.wait_ge(l.s("dv"), n)
            add_dv[i] = l.cnt["dv"]

        @block.sync
        def _(s, i=i):
            s.wait_ge(l.s("dv"), add_dv[i])
            l.dinc(s.dma_start(out=xo_r[:, i, :], in_=x5t[:, i % 2, :]), "stX")
            stx_after[i] = l.cnt["stX"]

    if debug:
        @block.sync
        def _(s):
            s.wait_ge(l.s("dv"), l.cnt["dv"])
            for nm, src in (("dbg_qt", QT), ("dbg_ut", UT), ("dbg_vst", VsT),
                            ("dbg_x", Xb), ("dbg_s", Sb), ("dbg_yn", Ybn),
                            ("dbg_ot", OT), ("dbg_wt", WTb), ("dbg_ms", Ms)):
                l.dinc(s.dma_start(out=dbg[nm], in_=src[:]), "st")

    @block.sync
    def _(s):
        s.wait_ge(l.s("stX"), l.cnt["stX"])
        s.wait_ge(l.s("stA"), l.cnt["stA"])
        s.wait_ge(l.s("st"), l.cnt["st"])
        s.wait_ge(l.s("qN"), l.cnt["qN"])
    return l


_progs = {}


def _prog(key, builder):
    if key not in _progs:
        _progs[key] = builder()
    return _progs[key]


def _consts_inputs():
    return {
        "identb": np.eye(128, dtype=BFNP),
        "trib": np.triu(np.ones((128, 128), dtype=BFNP), 1),
    }


def _run(l, maps):
    return run_bass_kernel_spmd(l.nc, maps, core_ids=list(range(NCORE))).results


def _moe_layer(I, xcur, EE, KK, CAP, PLOC, DFF, wg, wfc, wpr, drop, nexp_l,
               consts):
    ids = list(range(NCORE))
    lp = _prog(("prep", EE), lambda: build_prep_launch(EE, KK, PLOC))
    r1 = _run(lp, [dict(xin=np.ascontiguousarray(xcur[c]), wgate=wg, **consts)
                   for c in ids])
    cnt_dev = np.stack([r1[c]["counts"][0] for c in ids])

    le = _prog(("exp", EE), lambda: build_expert_launch(nexp_l, CAP, PLOC, DFF))
    zrow = np.zeros((1, D), dtype=r1[0]["slotTab"].dtype)
    i128 = (np.arange(CAP // 128) * 128).astype(np.float32).reshape(1, -1)
    maps = []
    for c in ids:
        parts = []
        cnts = np.zeros((1, nexp_l * 8), np.float32)
        for el in range(nexp_l):
            e = nexp_l * c + el
            for sc in ids:
                parts.append(r1[sc]["slotTab"].reshape(EE, PLOC, D)[e])
                cnts[0, el * 8 + sc] = cnt_dev[sc, e]
        tabs = np.concatenate(parts + [zrow], axis=0)
        maps.append(dict(tabs=tabs, cnts=cnts,
                         wfc=np.ascontiguousarray(wfc[nexp_l * c:nexp_l * (c + 1)]),
                         wpr=np.ascontiguousarray(wpr[nexp_l * c:nexp_l * (c + 1)]),
                         i128=i128))
    r2 = _run(le, maps)

    lc = _prog(("comb", EE), lambda: build_combine_launch(EE, KK, CAP, drop))
    yall = np.concatenate(
        [np.ascontiguousarray(r2[e // nexp_l]["yout"][e % nexp_l].T)
         for e in range(EE)], axis=0)
    maps = [dict(xin=np.ascontiguousarray(xcur[c]), yall=yall,
                 meta=r1[c]["meta"], meta2=r1[c]["meta2"], meta3=r1[c]["meta3"],
                 cntall=cnt_dev,
                 pmask=(np.arange(8) < c).astype(np.float32).reshape(8, 1))
            for c in ids]
    r3 = _run(lc, maps)
    return np.stack([r3[c]["xout"] for c in ids])


def kernel(**inputs):
    I = inputs
    ids = list(range(NCORE))
    x0 = np.ascontiguousarray(np.asarray(I["L0_attn_x"], np.float32))
    consts = _consts_inputs()

    def rs8(v):
        return np.ascontiguousarray(np.asarray(v, np.float32).reshape(8, 128))

    x1 = _moe_layer(I, x0, E0, K0, CAP0, PLOC0, DFF0, I["L0_ffn_w_gate"],
                    I["L0_ffn_c_fc"], I["L0_ffn_c_proj"], True, 1, consts)

    lg = _prog(("grif",), build_griffin_launch)
    gmaps = [dict(xin=np.ascontiguousarray(x1[c]),
                  win=I["L0_attn_W_in"], wr=I["L0_attn_W_r"],
                  wi=I["L0_attn_W_i"], wout=I["L0_attn_W_out"],
                  binb=rs8(I["L0_attn_b_in"]), brb=rs8(I["L0_attn_b_r"]),
                  bib=rs8(I["L0_attn_b_i"]), boutb=rs8(I["L0_attn_b_out"]),
                  logab=rs8(I["L0_attn_log_a"])) for c in ids]
    rg = _run(lg, gmaps)
    x2 = np.stack([rg[c]["xout"] for c in ids])

    x3 = _moe_layer(I, x2, E0, K0, CAP0, PLOC0, DFF0, I["L0_ffn_w_gate"],
                    I["L0_ffn_c_fc"], I["L0_ffn_c_proj"], True, 1, consts)

    x4 = _moe_layer(I, x3, E1, K1, CAP1, PLOC1, 2 * I1, I["L1_ffn_router_w"],
                    I["L1_ffn_w1"], I["L1_ffn_w2"], False, 8, consts)

    lm = _prog(("mesa",), build_mesa_launch)
    mmaps = [dict(xin=np.ascontiguousarray(x4[c]),
                  wqT=np.ascontiguousarray(np.asarray(I["L1_attn_Wq"]).T),
                  wkT=np.ascontiguousarray(np.asarray(I["L1_attn_Wk"]).T),
                  wvT=np.ascontiguousarray(np.asarray(I["L1_attn_Wv"]).T),
                  woT=np.ascontiguousarray(np.asarray(I["L1_attn_Wo"]).T),
                  bq=np.asarray(I["L1_attn_bq"], np.float32).reshape(128, 1),
                  bk=np.asarray(I["L1_attn_bk"], np.float32).reshape(128, 1),
                  bv=np.asarray(I["L1_attn_bv"], np.float32).reshape(128, 1),
                  bob=rs8(I["L1_attn_bo"]),
                  identb=np.eye(128, dtype=BFNP),
                  trif=np.triu(np.ones((128, 128), np.float32)),
                  identf=np.eye(128, dtype=np.float32)) for c in ids]
    rm = _run(lm, mmaps)
    x5 = np.stack([rm[c]["xout"] for c in ids])

    x6 = _moe_layer(I, x5, E1, K1, CAP1, PLOC1, 2 * I1, I["L1_ffn_router_w"],
                    I["L1_ffn_w1"], I["L1_ffn_w2"], False, 8, consts)
    return x6


# version 05
# speedup vs baseline: 1.0891x; 1.0270x over previous
import numpy as np
import ml_dtypes
from contextlib import ExitStack

import concourse.bass as bass
import concourse.mybir as mybir
from concourse.bass import IndirectOffsetOnAxis
from concourse.bass_utils import run_bass_kernel_spmd
from concourse.alu_op_type import AluOpType as ALU

F32 = mybir.dt.float32
BF16 = mybir.dt.bfloat16
U32 = mybir.dt.uint32
AF = mybir.ActivationFunctionType
BFNP = ml_dtypes.bfloat16

NCORE = 8
T = 4096
D = 1024
NT = 32

E0, K0, CAP0, PLOC0, DFF0 = 8, 2, 4096, 2048, 3072
E1, K1, CAP1, PLOC1, I1 = 64, 4, 1024, 512, 512

OOB = float(1 << 30)


class L:

    def __init__(self, name):
        self.nc = bass.Bass("TRN2", target_bir_lowering=False, debug=False)
        self.ctx = ExitStack()
        self.name = name
        self._sems = {}
        self.cnt = {}

    def dram(self, name, shape, dt, kind="Internal"):
        return self.nc.dram_tensor(name, shape, dt, kind=kind).ap()

    def sb(self, name, shape, dt):
        return self.ctx.enter_context(self.nc.sbuf_tensor(name, shape, dt))

    def ps(self, name, shape, dt=F32):
        return self.ctx.enter_context(self.nc.psum_tensor(name, shape, dt))

    def sem(self, name):
        h = self.ctx.enter_context(self.nc.semaphore(name))
        self._sems[name] = h
        self.cnt[name] = 0
        return h

    def s(self, name):
        return self._sems[name]

    def inc(self, instr, name, k=1):
        instr.then_inc(self._sems[name], k)
        self.cnt[name] += k
        return self.cnt[name]

    def dinc(self, instr, name):
        return self.inc(instr, name, 16)

    def open_block(self):
        self.block = self.ctx.enter_context(self.nc.Block())
        for nm in ("qLD", "qSC", "qGP", "pe", "dv", "ac", "po", "st",
                   "qX", "qW", "qN", "qC"):
            self.sem(nm)



def emit_consts(l, identb_h, trib_h, identf_h=None, warm=("Ln", "Exp", "Square")):
    block = l.block
    wrm = l.sb("warm_sb", [128, 8], F32)

    @block.scalar
    def _(a):
        for fn in warm:
            l.inc(a.activation(wrm[:], wrm[:], getattr(AF, fn)), "ac")
    identb = l.sb("identb_sb", [128, 128], BF16)
    tri_us = l.sb("tri_us", [128, 128], BF16)
    ones_b = l.sb("ones_b", [128, 128], BF16)
    onescol_b = l.sb("onescol_b", [128, 1], BF16)
    onesrow_b = l.sb("onesrow_b", [1, 128], BF16)
    identf = l.sb("identf_sb", [128, 128], F32) if identf_h is not None else None

    @block.sync
    def _(s):
        l.dinc(s.dma_start(out=identb[:], in_=identb_h[:]), "qC")
        l.dinc(s.dma_start(out=tri_us[:], in_=trib_h[:]), "qC")
        if identf_h is not None:
            l.dinc(s.dma_start(out=identf[:], in_=identf_h[:]), "qC")
    ld = l.cnt["qC"]

    @block.vector
    def _(v):
        v.wait_ge(l.s("qC"), ld)
        l.inc(v.memset(ones_b[:], 1.0), "dv")
        l.inc(v.memset(onescol_b[:], 1.0), "dv")
        l.inc(v.memset(onesrow_b[:], 1.0), "dv")
    return dict(identf=identf, identb=identb, tri_us=tri_us, ones_b=ones_b,
                onescol_b=onescol_b, onesrow_b=onesrow_b)


def emit_ln(l, x_sb, xn_sb, stat, ready=(), pad_out=None, dbg=None, G=NT,
            scratch=None):
    block = l.block
    sx = stat[:, 0:G]
    sxx = stat[:, G:2 * G]
    mu = stat[:, 2 * G:3 * G]
    rstd = stat[:, 3 * G:4 * G]
    if scratch is None:
        dmp_a = l.ps(f"lndmp_a{l.cnt['ac']}", [128, D], F32)
        lnps = l.sb(f"lnps{l.cnt['dv']}", [128, 5, G], F32)
    else:
        dmp_a, lnps = scratch
    scr = lnps[:, 0, :]


    @block.vector
    def _(v):
        def fin(instr):
            n = l.inc(instr, "dv")
            v.wait_ge(l.s("dv"), n)
        for nm, vv in ready:
            v.wait_ge(l.s(nm), vv)
        fin(v.memset(stat[:], -1.0e9))
        for i in range(G):
            l.inc(v.tensor_scalar(xn_sb[:, i, :], x_sb[:, i, :], 1.0, 0.0, ALU.mult,
                                  ALU.add, accum_out=sx[:, i:i + 1]), "dv")
            l.inc(v.tensor_tensor(dmp_a[:], x_sb[:, i, :], x_sb[:, i, :], ALU.mult), "dv")
            l.inc(v.tensor_scalar(xn_sb[:, i, :], dmp_a[:], 1.0, 0.0, ALU.mult,
                                  ALU.add, accum_out=sxx[:, i:i + 1]), "dv")
        v.wait_ge(l.s("dv"), l.cnt["dv"])
        fin(v.tensor_scalar(mu[:], sx[:], 1.0 / D, None, ALU.mult))
        fin(v.tensor_tensor(scr[:], mu[:], mu[:], ALU.mult))
        fin(v.scalar_tensor_tensor(rstd[:], sxx[:], 1.0 / D, scr[:],
                                   ALU.mult, ALU.subtract))
        if dbg is not None:
            fin(v.tensor_copy(dbg[:, 0:G], rstd))
    dv_var = l.cnt["dv"]

    scr2 = lnps[:, 1, :]
    scr3 = lnps[:, 2, :]
    scr5 = scr
    scr4 = lnps[:, 3, :]
    dmp_pad = dmp_a

    @block.vector
    def _(v):
        def fin(instr):
            n = l.inc(instr, "dv")
            v.wait_ge(l.s("dv"), n)
        fin(v.memset(scr2[:], -1.0e9))
        fin(v.memset(scr3[:], -1.0e9))
        fin(v.memset(scr4[:], -1.0e9))
        fin(v.tensor_scalar(scr2[:], rstd, 1e-5, None, ALU.add))
        fin(v.tensor_scalar(scr3[:], scr2[:], 1.0, None, ALU.add))
        fin(v.reciprocal(scr4[:], scr3[:]))
        if dbg is not None:
            fin(v.tensor_copy(dbg[:, G:2 * G], scr4[:]))
        rA, rB = scr4, lnps[:, 4, :]
        for it in range(14):
            fin(v.tensor_tensor(scr3[:], rA[:], scr2[:], ALU.mult))
            fin(v.tensor_tensor(scr5[:], rA[:], scr3[:], ALU.mult))
            fin(v.tensor_scalar(scr3[:], scr5[:], -0.5, 1.5, ALU.mult, ALU.add))
            fin(v.tensor_tensor(rB[:], rA[:], scr3[:], ALU.mult))
            rA, rB = rB, rA
        fin(v.tensor_copy(rstd, rA[:]))
    ac_rstd = l.cnt["ac"]

    @block.vector
    def _(v):
        v.wait_ge(l.s("ac"), ac_rstd)
        for i in range(G):
            l.inc(v.tensor_scalar(xn_sb[:, i, :], x_sb[:, i, :], mu[:, i:i + 1],
                                  rstd[:, i:i + 1], ALU.subtract, ALU.mult), "dv")
    if pad_out is not None:
        pad_out.append(dmp_a)
    return l.cnt["dv"]


def emit_xn_store(l, xn_sb, xn_dram, dv_ready):
    block = l.block

    @block.sync
    def _(s):
        s.wait_ge(l.s("dv"), dv_ready)
        xr = xn_dram.rearrange("(i p) d -> p i d", p=128)
        for q in range(4):
            l.dinc(s.dma_start(out=xr[:, q * 8:(q + 1) * 8, :],
                               in_=xn_sb[:, q * 8:(q + 1) * 8, :]), "qN")
    return l.cnt["qN"]


def emit_prep(l, io, EE, KK, PLOC, xn_sb, xn_dram, consts, xn_ready, tag=""):
    nc, block = l.nc, l.block

    wg_b = l.sb(f"wg_b{tag}", [128, 8, EE], BF16)
    pad_b = io["pad"]

    @block.gpsimd
    def _(g):
        l.dinc(g.dma_start(out=wg_b[:],
                           in_=io["wgate"].rearrange("(c p) e -> p c e", p=128)), "qW")
    ld_wg = l.cnt["qW"]

    @block.vector
    def _(v):
        v.wait_ge(l.s("qW"), ld_wg)
        l.inc(v.tensor_copy(pad_b[:], xn_sb[:, 0, 0:256]), "dv")
    dv_wg = l.cnt["dv"]

    SL = 128 if EE > 8 else 256
    NSL, TPJ = T // SL, SL // 128
    xnt_buf = l.sb(f"xnt{tag}", [128, 2, 8, SL], BF16)
    lg_buf = l.sb(f"lgbuf{tag}", [EE, 2, SL], BF16)
    lg_tmS = l.sb(f"lg_tmS{tag}", [128, 2, TPJ, EE], BF16)
    lgt = l.sb(f"lgt{tag}", [128, 2, EE], BF16)
    lg_dram = l.dram(f"lg_scr{tag}", [NT, 128, EE], BF16)
    lg_ps = l.ps(f"lg_ps{tag}", [EE, SL])
    tp_ps = l.ps(f"tp_ps{tag}", [128, EE], BF16)
    lgst_after = {}
    lgld_after = {}
    mmS, evS, tpS, tpE = (f"mmS{tag}", f"evS{tag}", f"tpS{tag}", f"tpE{tag}")
    for _nm in (mmS, evS, tpS, tpE):
        l.sem(_nm)
    xp = [f"xp0{tag}", f"xp1{tag}"]
    for _nm in xp:
        l.sem(_nm)

    @block.scalar
    def _(a):
        a.wait_ge(l.s("qN"), xn_ready)
        for ts_ in range(NSL):
            if ts_ >= 2:
                a.wait_ge(l.s(mmS), ts_ - 1)
            for dc in range(8):
                l.dinc(a.dma_start(
                    out=xnt_buf[:, ts_ % 2, dc, :],
                    in_=xn_dram[ts_ * SL:(ts_ + 1) * SL, dc * 128:(dc + 1) * 128],
                    transpose=True), xp[ts_ % 2])

    for ts_ in range(NSL):
        @block.tensor
        def _(t, ts_=ts_):
            if ts_ == 0:
                t.wait_ge(l.s("dv"), dv_wg)
            t.wait_ge(l.s(xp[ts_ % 2]), (ts_ // 2 + 1) * 8 * 16)
            if ts_ > 0:
                t.wait_ge(l.s(evS), ts_)
            for dc in range(8):
                mm = t.matmul(lg_ps[:], wg_b[:, dc, :],
                              xnt_buf[:, ts_ % 2, dc, :],
                              start=(dc == 0), stop=(dc == 7))
            l.inc(mm, mmS)

        @block.vector
        def _(v, ts_=ts_):
            v.wait_ge(l.s(mmS), ts_ + 1)
            if ts_ >= 2:
                v.wait_ge(l.s(tpS), (ts_ - 1) * TPJ)
            v.tensor_copy(lg_buf[:, ts_ % 2, :], lg_ps[:])
            l.inc(v.tensor_copy(pad_b[:], xn_sb[:, 0, 0:256]), evS)

        @block.tensor
        def _(t, ts_=ts_):
            t.wait_ge(l.s(evS), ts_ + 1)
            for j in range(TPJ):
                i = ts_ * TPJ + j
                if i > 0:
                    t.wait_ge(l.s(tpE), i)
                l.inc(t.transpose(tp_ps[:], lg_buf[0:EE, ts_ % 2, j * 128:(j + 1) * 128],
                                  consts["identb"][0:EE, 0:EE]), tpS)

        @block.vector
        def _(v, ts_=ts_):
            for j in range(TPJ):
                i = ts_ * TPJ + j
                v.wait_ge(l.s(tpS), i + 1)
                if ts_ >= 2:
                    v.wait_ge(l.s("qLD"), lgst_after[(ts_ - 2) * TPJ + j])
                l.inc(v.tensor_copy(lg_tmS[:, ts_ % 2, j, :], tp_ps[:]), tpE)

        @block.sync
        def _(s, ts_=ts_):
            for j in range(TPJ):
                i = ts_ * TPJ + j
                s.wait_ge(l.s(tpE), i + 1)
                l.dinc(s.dma_start(out=lg_dram[i], in_=lg_tmS[:, ts_ % 2, j, :]),
                       "qLD")
                lgst_after[i] = l.cnt["qLD"]
    dv_lgtm = l.cnt["dv"]

    probs = l.sb(f"probs{tag}", [128, EE], F32)
    sume = l.sb(f"sume{tag}", [128, 3], F32)
    mx8 = l.sb(f"mx8{tag}", [128, 8], F32)
    idx8 = l.sb(f"idx8{tag}", [128, 8], U32)
    oh = l.sb(f"oh{tag}", [128, KK, EE], BF16)
    ind = l.sb(f"ind{tag}", [128, EE], BF16)
    ind2 = l.sb(f"ind2{tag}", [128, EE], BF16)
    ind3 = l.sb(f"ind3{tag}", [128, EE], BF16)
    iotaEf = l.sb(f"iotaEf{tag}", [128, EE], F32)
    ek = l.sb(f"ek{tag}", [128, NT * KK], F32)
    posk = l.sb(f"posk{tag}", [128, NT * KK], F32)
    wk = l.sb(f"wk{tag}", [128, NT * KK], F32)
    ovf = l.sb(f"ovf{tag}", [128, NT * KK], F32)
    sidx_u = l.sb(f"sidx_u{tag}", [128, NT * KK], U32)
    scrE2 = l.sb(f"scrE2{tag}", [128, EE], F32)
    runoff_all = l.sb(f"runoff{tag}", [128, 2 * EE], F32)
    runoff2 = lambda a: runoff_all[:, a * EE:(a + 1) * EE]

    routeps = l.ps(f"routeps{tag}", [128, 3 * EE + 2 * NT * KK], F32)
    probs2 = routeps[:, 0:EE]
    pos_t = routeps[:, EE:2 * EE]
    scrE = routeps[:, 2 * EE:3 * EE]
    sidx = routeps[:, 3 * EE:3 * EE + NT * KK]
    sidx2 = routeps[:, 3 * EE + NT * KK:3 * EE + 2 * NT * KK]

    psAB = l.ps(f"psAB{tag}", [128, 2, EE])
    psA = psAB[:, 0, :]
    psB = psAB[:, 1, :]

    dbg_sm = None
    pe_pos_prev = [None]

    @block.gpsimd
    def _(g):
        l.inc(g.iota(iotaEf[:], pattern=[[1, EE]], base=0, channel_multiplier=0,
                     allow_small_or_imprecise_dtypes=True), "po")
    po_i0 = l.cnt["po"]

    @block.vector
    def _(v):
        v.wait_ge(l.s("po"), po_i0)
        l.inc(v.memset(runoff_all[:], 0.0), "dv")
    po_init = l.cnt["po"]

    def fin(v, instr):
        n = l.inc(instr, "dv")
        v.wait_ge(l.s("dv"), n)
        return n

    ac_exp_hist = {}
    for i in range(NT):
        @block.sync
        def _(s, i=i):
            s.wait_ge(l.s("qLD"), lgst_after[i])
            if i >= 2:
                s.wait_ge(l.s("ac"), ac_exp_hist[i - 2])
            l.dinc(s.dma_start(out=lgt[:, i % 2, :], in_=lg_dram[i]), "qSC")
        lgld = l.cnt["qSC"]

        @block.vector
        def _(v, i=i):
            if i == 0:
                v.wait_ge(l.s("po"), po_init)
            if pe_pos_prev[0] is not None:
                v.wait_ge(l.s("pe"), pe_pos_prev[0])
            pass
        dv_mx = l.cnt["dv"]

        @block.scalar
        def _(a, i=i):
            a.wait_ge(l.s("qSC"), lgld)
            a.wait_ge(l.s("dv"), dv_mx)
            l.inc(a.activation(probs[:], lgt[:, i % 2, :], AF.Exp,
                               accum_out=sume[:, 0:1]), "ac")
            l.inc(a.activation(sume[:, 2:3], sume[:, 0:1], AF.Copy), "ac")
        ac_exp = l.cnt["ac"]
        ac_exp_hist[i] = ac_exp

        @block.vector
        def _(v, i=i):
            v.wait_ge(l.s("ac"), ac_exp)
            fin(v, v.reciprocal(sume[:, 1:2], sume[:, 2:3]))
            fin(v, v.tensor_scalar(probs2[:], probs[:], sume[:, 1:2], None, ALU.mult))
            fin(v, v.max(mx8[:], probs2[:]))
            fin(v, v.max_index(idx8[:], mx8[:], probs2[:]))
            fin(v, v.tensor_copy(ek[:, i * KK:i * KK + KK], idx8[:, 0:KK]))
            fin(v, v.tensor_copy(wk[:, i * KK:i * KK + KK], mx8[:, 0:KK]))
            for k in range(KK):
                fin(v, v.tensor_scalar(oh[:, k, :], iotaEf[:],
                                       ek[:, i * KK + k:i * KK + k + 1], None,
                                       ALU.is_equal))
            if KK == 2:
                fin(v, v.tensor_tensor(ind[:], oh[:, 0, :], oh[:, 1, :], ALU.add))
            elif KK == 4:
                fin(v, v.tensor_tensor(ind2[:], oh[:, 0, :], oh[:, 1, :], ALU.add))
                fin(v, v.tensor_tensor(ind3[:], oh[:, 2, :], oh[:, 3, :], ALU.add))
                fin(v, v.tensor_tensor(ind[:], ind2[:], ind3[:], ALU.add))
            else:
                raise NotImplementedError
            l.inc(v.tensor_copy(pad_b[:], xn_sb[:, 0, 0:256]), "dv")
        dv_ind = l.cnt["dv"]

        @block.tensor
        def _(t, i=i):
            t.wait_ge(l.s("dv"), dv_ind)
            t.matmul(psA[:], consts["tri_us"][:], ind[:], start=True, stop=True)
            l.inc(t.matmul(psB[:], consts["ones_b"][:], ind[:], start=True, stop=True),
                  "pe")
        pe_pos = l.cnt["pe"]
        pe_pos_prev[0] = pe_pos

        @block.vector
        def _(v, i=i):
            v.wait_ge(l.s("pe"), pe_pos)
            p_, q_ = i % 2, 1 - i % 2
            fin(v, v.tensor_tensor(pos_t[:], psA[:], runoff2(p_)[:], ALU.add))
            fin(v, v.scalar_tensor_tensor(runoff2(q_)[:], psB[:], 1.0,
                                          runoff2(p_)[:], ALU.mult, ALU.add))

            for k in range(KK):
                col = i * KK + k
                fin(v, v.scalar_tensor_tensor(scrE2[:], oh[:, k, :], 1.0,
                                              pos_t[:], ALU.mult, ALU.mult,
                                              accum_out=posk[:, col:col + 1]))

    cnt_sb = l.sb(f"cnt_sb{tag}", [1, EE], F32)

    @block.vector
    def _(v):
        def finw(instr):
            n = l.inc(instr, "dv")
            v.wait_ge(l.s("dv"), n)
        finw(v.scalar_tensor_tensor(sidx[:], ek[:], float(PLOC), posk[:],
                                    ALU.mult, ALU.add))
        finw(v.tensor_scalar(ovf[:], posk[:], float(PLOC), None, ALU.is_ge))
        finw(v.scalar_tensor_tensor(sidx2[:], ovf[:], OOB, sidx[:],
                                    ALU.mult, ALU.add))
        finw(v.tensor_copy(sidx_u[:], sidx2[:]))
        v.tensor_copy(cnt_sb[:], runoff2(NT % 2)[0:1, :])
        l.inc(v.tensor_copy(pad_b[:], xn_sb[:, 0, 0:256]), "dv")
    dv_all = l.cnt["dv"]
    dv_cnt = l.cnt["dv"]

    @block.gpsimd
    def _(g):
        g.wait_ge(l.s("dv"), dv_all)
        breg = g.alloc_register()
        g.reg_mov(breg, EE * PLOC - 1)
        for i in range(NT):
            for k in range(KK):
                col = i * KK + k
                l.dinc(g.indirect_dma_start(
                    out=io["slotTab"][:, :],
                    out_offset=IndirectOffsetOnAxis(sidx_u[:, col:col + 1], 0),
                    in_=xn_sb[:, i, :], in_offset=None,
                    bounds_check=breg, oob_is_err=False), "qGP")
        g.free_register(breg)

    @block.sync
    def _(s):
        s.wait_ge(l.s("dv"), dv_cnt)
        l.dinc(s.dma_start(out=io["meta"], in_=ek[:]), "st")
        l.dinc(s.dma_start(out=io["meta2"], in_=posk[:]), "st")
        l.dinc(s.dma_start(out=io["meta3"], in_=wk[:]), "st")
        l.dinc(s.dma_start(out=io["counts"], in_=cnt_sb[:]), "st")
        if "dbg_stat" in io:
            l.dinc(s.dma_start(out=io["dbg_stat"], in_=io["_stat"][:]), "st")


def emit_tail(l):
    block = l.block

    @block.sync
    def _(s):
        s.wait_ge(l.s("st"), l.cnt["st"])
        s.wait_ge(l.s("qGP"), l.cnt["qGP"])



def build_prep_launch(EE, KK, PLOC, debug=False):
    l = L("prep")
    xin = l.dram("xin", [T, D], F32, kind="ExternalInput")
    wgate = l.dram("wgate", [D, EE], F32, kind="ExternalInput")
    identb_h = l.dram("identb", [128, 128], BF16, kind="ExternalInput")
    trib_h = l.dram("trib", [128, 128], BF16, kind="ExternalInput")
    io = dict(
        slotTab=l.dram("slotTab", [EE * PLOC, D], BF16, kind="ExternalOutput"),
        counts=l.dram("counts", [1, EE], F32, kind="ExternalOutput"),
        meta=l.dram("meta", [128, NT * KK], F32, kind="ExternalOutput"),
        meta2=l.dram("meta2", [128, NT * KK], F32, kind="ExternalOutput"),
        meta3=l.dram("meta3", [128, NT * KK], F32, kind="ExternalOutput"),
        wgate=wgate,
    )
    xn_dram = l.dram("xn_scr", [T, D], BF16,
                     kind="ExternalOutput" if debug else "Internal")
    if debug:
        io["dbg_stat"] = l.dram("dbg_stat", [128, 4 * NT], F32, kind="ExternalOutput")
        io["dbg_r0"] = l.dram("dbg_r0", [128, 2 * NT], F32, kind="ExternalOutput")

    x_sb = l.sb("x_sb", [128, NT, D], F32)
    xn_sb = l.sb("xn_sb", [128, NT, D], BF16)
    stat = l.sb("stat", [128, 4 * NT], F32)
    dbg_sb = l.ps("dbg_ps", [128, 2 * NT]) if debug else None

    l.open_block()
    block = l.block

    @block.sync
    def _(s):
        xr = xin.rearrange("(i p) d -> p i d", p=128)
        for q in range(4):
            l.dinc(s.dma_start(out=x_sb[:, q * 8:(q + 1) * 8, :],
                               in_=xr[:, q * 8:(q + 1) * 8, :]), "qX")
    ld_x = l.cnt["qX"]

    consts = emit_consts(l, identb_h, trib_h)
    pads = []
    dv_xn = emit_ln(l, x_sb, xn_sb, stat, ready=[("qX", ld_x)], pad_out=pads,
                    dbg=dbg_sb)
    xn_ready = emit_xn_store(l, xn_sb, xn_dram, dv_xn)
    io["_stat"] = stat
    io["_dbg_sb"] = dbg_sb
    io["pad"] = pads[0][:, 0:256]
    emit_prep(l, io, EE, KK, PLOC, xn_sb, xn_dram, consts, xn_ready)
    if debug:
        dv_now = l.cnt["dv"]

        @block.vector
        def _(v):
            v.wait_ge(l.s("dv"), dv_now)
            l.inc(v.tensor_copy(x_sb[:, 0, 0:2 * NT], dbg_sb[:]), "dv")
        dv_dbg = l.cnt["dv"]

        @block.sync
        def _(s):
            s.wait_ge(l.s("dv"), dv_dbg)
            l.dinc(s.dma_start(out=io["dbg_r0"], in_=x_sb[:, 0, 0:2 * NT]), "st")
    emit_tail(l)
    return l


def build_expert_launch(NEXP_L, CAP, PLOC, DFF, debug=False):
    TCH = 512
    NTC = CAP // 128
    NCH = CAP // TCH
    NH2 = DFF // 256
    NROWS = NEXP_L * 8 * PLOC
    WPAR = 1 if NEXP_L == 1 else 2
    NGC = NEXP_L * NCH

    l = L("exp")
    tabs = l.dram("tabs", [NROWS + 1, D], BF16, kind="ExternalInput")
    cnts_h = l.dram("cnts", [1, NEXP_L * 8], F32, kind="ExternalInput")
    wfc_h = l.dram("wfc", [NEXP_L, D, DFF], F32, kind="ExternalInput")
    wpr_h = l.dram("wpr", [NEXP_L, DFF // 2, D], F32, kind="ExternalInput")
    i128_h = l.dram("i128", [1, NTC], F32, kind="ExternalInput")
    y_h = l.dram("yout", [NEXP_L, D, CAP], BF16, kind="ExternalOutput")
    xg_dram = l.dram("xg_scr", [NEXP_L, CAP, D], BF16,
                     kind="ExternalOutput" if debug else "Internal")

    xg = l.sb("xg", [128, WPAR * NTC, D], BF16)
    wfc = l.sb("wfc_sb", [128, WPAR * 8, DFF], BF16)
    wpr = l.sb("wpr_sb", [128, WPAR * NH2, D], BF16)
    xT = l.sb("xT", [128, 2, 8, TCH], BF16)
    hsb = l.sb("hsb", [128, 2, NH2, TCH], BF16)
    yT = l.sb("yT", [128, 2, 8, TCH], BF16)
    silA = l.sb("silA", [128, 2, TCH], F32)
    cnt_sb = l.sb("cnt_sb", [1, NEXP_L * 8], F32)
    catrow = l.sb("catrow", [1, 16 + NTC], F32)
    ones8 = l.sb("ones8", [1, 8], F32)
    onesf = l.sb("onesf", [1, 128], F32)
    iotaP = l.sb("iotaP", [128, 1], U32)
    iotaPf = l.sb("iotaPf", [128, 1], F32)
    cumb = l.sb("cumb", [128, 16 + NTC], F32)
    exclb = l.sb("exclb", [128, 8], F32)
    svb = l.sb("svb", [128, NTC], F32)
    accA = l.sb("accA", [128, NTC], F32)
    accB = l.sb("accB", [128, NTC], F32)
    eqb = l.sb("eqb", [128, NTC], F32)
    rowA = l.sb("rowA", [128, NTC], F32)
    rowB = l.sb("rowB", [128, NTC], F32)
    oobm = l.sb("oobm", [128, NTC], F32)
    rowu = l.sb("rowu", [128, NEXP_L * NTC], U32)
    rowf = l.sb("rowf", [128, NEXP_L * NTC], F32)
    wrm = l.sb("wrm", [128, 8], F32)

    l.open_block()
    block = l.block
    l.sem("trS")

    psH = [l.ps(f"psH{j}", [128, TCH]) for j in range(4)]
    psY = [l.ps(f"psY{j}", [128, TCH]) for j in range(2)]
    psCum = l.ps("psCum", [128, 16 + NTC])

    @block.scalar
    def _(a):
        l.inc(a.activation(wrm[:], wrm[:], AF.Silu), "ac")

    @block.sync
    def _(s):
        l.dinc(s.dma_start(out=cnt_sb[:], in_=cnts_h[:]), "qC")
        l.dinc(s.dma_start(out=catrow[:, 16:16 + NTC], in_=i128_h[:]), "qC")
    ld_c = l.cnt["qC"]

    @block.gpsimd
    def _(g):
        l.inc(g.iota(iotaP[:], pattern=[[0, 1]], base=0, channel_multiplier=1), "po")
    po_iota = l.cnt["po"]

    @block.vector
    def _(v):
        def fin(i):
            n = l.inc(i, "dv")
            v.wait_ge(l.s("dv"), n)
        v.wait_ge(l.s("po"), po_iota)
        v.wait_ge(l.s("qC"), ld_c)
        fin(v.memset(ones8[:], 1.0))
        fin(v.memset(onesf[:], 1.0))
        fin(v.tensor_copy(iotaPf[:], iotaP[:]))

    dv_gather = {}
    for e in range(NEXP_L):
        @block.vector
        def _(v, e=e):
            def fin(i):
                n = l.inc(i, "dv")
                v.wait_ge(l.s("dv"), n)
            fin(v.tensor_copy(catrow[:, 8:16], cnt_sb[:, e * 8:(e + 1) * 8]))
            fin(v.tensor_tensor_scan(catrow[:, 0:8], ones8[:], catrow[:, 8:16],
                                     0.0, ALU.mult, ALU.add))

        @block.tensor
        def _(t, e=e):
            t.wait_ge(l.s("dv"), l.cnt["dv"])
            l.inc(t.matmul(psCum[:], onesf[:], catrow[:], start=True, stop=True),
                  "pe")
        pe_cum = l.cnt["pe"]

        @block.vector
        def _(v, e=e):
            def fin(i):
                n = l.inc(i, "dv")
                v.wait_ge(l.s("dv"), n)
            v.wait_ge(l.s("pe"), pe_cum)
            fin(v.tensor_copy(cumb[:], psCum[:]))
            fin(v.tensor_tensor(exclb[:], cumb[:, 0:8], cumb[:, 8:16], ALU.subtract))
            fin(v.tensor_scalar(svb[:], cumb[:, 16:16 + NTC], iotaPf[:, 0:1],
                                None, ALU.add))
            fin(v.memset(accA[:], 0.0))
            a, b = accA, accB
            for sg in range(8):
                fin(v.scalar_tensor_tensor(b[:], svb[:], cumb[:, sg:sg + 1], a[:],
                                           ALU.is_ge, ALU.add))
                a, b = b, a
            sig = a
            fin(v.memset(rowA[:], 0.0))
            a2, b2 = rowA, rowB
            for sg in range(8):
                fin(v.tensor_scalar(eqb[:], sig[:], float(sg), None, ALU.is_equal))
                fin(v.scalar_tensor_tensor(b2[:], eqb[:], exclb[:, sg:sg + 1], a2[:],
                                           ALU.mult, ALU.add))
                a2, b2 = b2, a2
            base = a2
            fin(v.scalar_tensor_tensor(b2[:], sig[:], float(PLOC), svb[:],
                                       ALU.mult, ALU.add))
            fin(v.tensor_tensor(eqb[:], b2[:], base[:], ALU.subtract))
            fin(v.tensor_scalar(oobm[:], svb[:], cumb[:, 7:8], None, ALU.is_ge))
            fin(v.scalar_tensor_tensor(b2[:], oobm[:], float(1 << 22), eqb[:],
                                       ALU.mult, ALU.add))
            fin(v.tensor_scalar(eqb[:], b2[:], float(e * 8 * PLOC), None, ALU.add))
            fin(v.tensor_scalar_min(rowf[:, e * NTC:(e + 1) * NTC], eqb[:],
                                    float(NROWS)))
            fin(v.tensor_copy(rowu[:, e * NTC:(e + 1) * NTC],
                              rowf[:, e * NTC:(e + 1) * NTC]))
        dv_gather[e] = l.cnt["dv"]

    wq_after = {}
    gp_after = {}
    qn_after = {}
    trs_after = {}
    peG = {}
    peY = {}
    peY_last = {}
    acS = {}
    gate_dv = {}
    gate_all = {}
    evac_dv = {}
    st_after = {}
    last_a_sil = {0: 0, 1: 0}
    last_g_gate = {0: 0, 1: 0}
    last_sil_gate = {0: 0, 1: 0}
    last_evac = {0: 0, 1: 0}
    pe_all_e = {}

    def ech(gc):
        return gc // NCH, gc % NCH

    def emit_gpsimd(e):
        @block.gpsimd
        def _(g, e=e):
            wp = e % WPAR
            if WPAR == 2 and e >= 2:
                g.wait_ge(l.s("pe"), pe_all_e[e - 2])
            for ft in range(8):
                l.dinc(g.dma_start(out=wfc[:, wp * 8 + ft, :],
                                   in_=wfc_h[e, ft * 128:(ft + 1) * 128, :]), "qW")
            for hp in range(NH2):
                l.dinc(g.dma_start(out=wpr[:, wp * NH2 + hp, :],
                                   in_=wpr_h[e, hp * 128:(hp + 1) * 128, :]), "qW")
            wq_after[e] = l.cnt["qW"]
            g.wait_ge(l.s("dv"), dv_gather[e])
            if WPAR == 2 and e >= 2:
                g.wait_ge(l.s("qN"), qn_after[(e - 2, NCH - 1)])
            breg = g.alloc_register()
            g.reg_mov(breg, NROWS)
            for i in range(NTC):
                l.dinc(g.indirect_dma_start(
                    out=xg[:, wp * NTC + i, :], out_offset=None,
                    in_=tabs[:, :],
                    in_offset=IndirectOffsetOnAxis(
                        rowu[:, e * NTC + i:e * NTC + i + 1], 0),
                    bounds_check=breg, oob_is_err=False), "qGP")
            g.free_register(breg)
            gp_after[e] = l.cnt["qGP"]

    def emit_bounce(e):
        @block.sync
        def _(s, e=e):
            s.wait_ge(l.s("qGP"), gp_after[e])
            wp = e % WPAR
            xr = xg_dram[e].rearrange("(i p) d -> p i d", p=128)
            for ch in range(NCH):
                l.dinc(s.dma_start(
                    out=xr[:, ch * 4:(ch + 1) * 4, :],
                    in_=xg[:, wp * NTC + ch * 4:wp * NTC + (ch + 1) * 4, :]), "qN")
                qn_after[(e, ch)] = l.cnt["qN"]

    def emit_ystore(gc):
        e, ch = ech(gc)

        @block.sync
        def _(s, gc=gc, e=e, ch=ch):
            s.wait_ge(l.s("dv"), evac_dv[(gc, 7)])
            yr = y_h[e].rearrange("(dt p) t -> p dt t", p=128)
            l.dinc(s.dma_start(out=yr[:, :, ch * TCH:(ch + 1) * TCH],
                               in_=yT[:, gc % 2, :, :]), "st")
            st_after[gc] = l.cnt["st"]

    def emit_transp(gc):
        e, ch = ech(gc)

        @block.scalar
        def _(a, gc=gc, e=e, ch=ch):
            a.wait_ge(l.s("qN"), qn_after[(e, ch)])
            for ft in range(8):
                l.dinc(a.dma_start(
                    out=xT[:, gc % 2, ft, :],
                    in_=xg_dram[e, ch * TCH:(ch + 1) * TCH, ft * 128:(ft + 1) * 128],
                    transpose=True), "trS")
            trs_after[gc] = l.cnt["trS"]

    def emit_pe_pair(gc, mh):
        e, ch = ech(gc)
        wp = e % WPAR
        pb = mh % 2

        @block.tensor
        def _(t, gc=gc, mh=mh, wp=wp, pb=pb, ch=ch, e=e):
            if mh == 0:
                t.wait_ge(l.s("trS"), trs_after[gc])
                if ch == 0:
                    t.wait_ge(l.s("qW"), wq_after[e])
            if last_a_sil[pb]:
                t.wait_ge(l.s("ac"), last_a_sil[pb])
            if last_g_gate[pb]:
                t.wait_ge(l.s("dv"), last_g_gate[pb])
            for ft in range(8):
                t.matmul(psH[pb][:], wfc[:, wp * 8 + ft, mh * 128:(mh + 1) * 128],
                         xT[:, gc % 2, ft, :], start=(ft == 0), stop=(ft == 7))
            for ft in range(8):
                mm = t.matmul(psH[2 + pb][:],
                              wfc[:, wp * 8 + ft,
                                  DFF // 2 + mh * 128:DFF // 2 + (mh + 1) * 128],
                              xT[:, gc % 2, ft, :], start=(ft == 0), stop=(ft == 7))
            l.inc(mm, "pe")
            peG[(gc, mh)] = l.cnt["pe"]

    def emit_silu(gc, mh):
        pb = mh % 2

        @block.scalar
        def _(a, gc=gc, mh=mh, pb=pb):
            a.wait_ge(l.s("pe"), peG[(gc, mh)])
            if last_sil_gate[pb]:
                a.wait_ge(l.s("dv"), last_sil_gate[pb])
            l.inc(a.activation(silA[:, pb, :], psH[pb][:], AF.Silu), "ac")
        acS[(gc, mh)] = l.cnt["ac"]
        last_a_sil[pb] = l.cnt["ac"]

    def emit_gate(gc, mh):
        pb = mh % 2

        @block.vector
        def _(v, gc=gc, mh=mh, pb=pb):
            v.wait_ge(l.s("ac"), acS[(gc, mh)])
            if mh == 0 and gc >= 2:
                v.wait_ge(l.s("pe"), peY_last[gc - 2])
            l.inc(v.tensor_tensor(hsb[:, gc % 2, mh, :], silA[:, pb, :],
                                  psH[2 + pb][:], ALU.mult), "dv")
        gate_dv[(gc, mh)] = l.cnt["dv"]
        last_g_gate[pb] = l.cnt["dv"]
        last_sil_gate[pb] = l.cnt["dv"]

    def emit_pe_y(gc, dt):
        e, ch = ech(gc)
        wp = e % WPAR
        yb = dt % 2

        @block.tensor
        def _(t, gc=gc, dt=dt, wp=wp, yb=yb):
            if dt == 0:
                t.wait_ge(l.s("dv"), gate_all[gc])
            if last_evac[yb]:
                t.wait_ge(l.s("dv"), last_evac[yb])
            for hp in range(NH2):
                mm = t.matmul(psY[yb][:],
                              wpr[:, wp * NH2 + hp, dt * 128:(dt + 1) * 128],
                              hsb[:, gc % 2, hp, :],
                              start=(hp == 0), stop=(hp == NH2 - 1))
            l.inc(mm, "pe")
        peY[(gc, dt)] = l.cnt["pe"]
        if dt == 7:
            peY_last[gc] = l.cnt["pe"]
            pe_all_e[ech(gc)[0]] = l.cnt["pe"]

    def emit_evac(gc, dt):
        yb = dt % 2

        @block.vector
        def _(v, gc=gc, dt=dt, yb=yb):
            v.wait_ge(l.s("pe"), peY[(gc, dt)])
            if dt == 0 and gc >= 2:
                v.wait_ge(l.s("st"), st_after[gc - 2])
            l.inc(v.tensor_copy(yT[:, gc % 2, dt, :], psY[yb][:]), "dv")
        evac_dv[(gc, dt)] = l.cnt["dv"]
        last_evac[yb] = l.cnt["dv"]

    emit_gpsimd(0)
    emit_bounce(0)
    if NEXP_L > 1:
        emit_gpsimd(1)
        emit_bounce(1)
    emit_transp(0)
    if NGC > 1:
        emit_transp(1)
    for gc in range(NGC):
        emit_pe_pair(gc, 0)
        emit_pe_pair(gc, 1)
        for mh in range(2, NH2):
            emit_silu(gc, mh - 2)
            emit_gate(gc, mh - 2)
            emit_pe_pair(gc, mh)
        for mh in (NH2 - 2, NH2 - 1):
            emit_silu(gc, mh)
            emit_gate(gc, mh)
        gate_all[gc] = l.cnt["dv"]
        emit_pe_y(gc, 0)
        emit_pe_y(gc, 1)
        for dt in range(2, 8):
            emit_evac(gc, dt - 2)
            emit_pe_y(gc, dt)
        emit_evac(gc, 6)
        emit_evac(gc, 7)
        emit_ystore(gc)
        if gc + 2 < NGC:
            ee, cch = ech(gc + 2)
            if cch == 0 and ee >= 2:
                emit_gpsimd(ee)
                emit_bounce(ee)
            emit_transp(gc + 2)

    @block.sync
    def _(s):
        s.wait_ge(l.s("st"), l.cnt["st"])
        s.wait_ge(l.s("qN"), l.cnt["qN"])
        s.wait_ge(l.s("qGP"), l.cnt["qGP"])
        s.wait_ge(l.s("trS"), l.cnt["trS"])
        s.wait_ge(l.s("qW"), l.cnt["qW"])
    return l


def build_combine_launch(EE, KK, CAP, drop):
    NC = NT * KK
    l = L("comb")
    xin = l.dram("xin", [T, D], F32, kind="ExternalInput")
    yall_h = l.dram("yall", [EE * CAP, D], BF16, kind="ExternalInput")
    meta_h = l.dram("meta", [128, NC], F32, kind="ExternalInput")
    meta2_h = l.dram("meta2", [128, NC], F32, kind="ExternalInput")
    meta3_h = l.dram("meta3", [128, NC], F32, kind="ExternalInput")
    cntall_h = l.dram("cntall", [8, EE], F32, kind="ExternalInput")
    pmask_h = l.dram("pmask", [8, 1], F32, kind="ExternalInput")
    xout_h = l.dram("xout", [T, D], F32, kind="ExternalOutput")

    x_sb = l.sb("x_sb", [128, NT, D], F32)
    ekall = l.sb("ekall", [128, NC], F32)
    pkall = l.sb("pkall", [128, NC], F32)
    wkall = l.sb("wkall", [128, NC], F32)
    cnt_sb = l.sb("cnt_sb", [8, EE], F32)
    pm_sb = l.sb("pm_sb", [8, 1], F32)
    onesf = l.sb("onesf", [1, 128], F32)
    offrow = l.sb("offrow", [1, EE], F32)
    off_b = l.sb("off_b", [128, EE], F32)
    eqm = l.sb("eqm", [128, NC], F32)
    oaccA = l.sb("oaccA", [128, NC], F32)
    oaccB = l.sb("oaccB", [128, NC], F32)
    posg = l.sb("posg", [128, NC], F32)
    wv = l.sb("wv", [128, NC], F32)
    gidx = l.sb("gidx", [128, NC], F32)
    gidx_u = l.sb("gidx_u", [128, NC], U32)
    yr = l.sb("yr", [128, 4, D], BF16)
    acc = l.sb("acc", [128, 4, D], F32)

    l.open_block()
    block = l.block
    psO = l.ps("psO", [1, EE])
    psB2 = l.ps("psB2", [128, EE])

    @block.sync
    def _(s):
        xr = xin.rearrange("(i p) d -> p i d", p=128)
        for q in range(4):
            l.dinc(s.dma_start(out=x_sb[:, q * 8:(q + 1) * 8, :],
                               in_=xr[:, q * 8:(q + 1) * 8, :]), "qX")
        l.dinc(s.dma_start(out=ekall[:], in_=meta_h[:]), "qC")
        l.dinc(s.dma_start(out=pkall[:], in_=meta2_h[:]), "qC")
        l.dinc(s.dma_start(out=wkall[:], in_=meta3_h[:]), "qC")
        l.dinc(s.dma_start(out=cnt_sb[:], in_=cntall_h[:]), "qC")
        l.dinc(s.dma_start(out=pm_sb[:], in_=pmask_h[:]), "qC")
    ld_x = l.cnt["qX"]
    ld_c = l.cnt["qC"]

    @block.vector
    def _(v):
        v.wait_ge(l.s("qC"), ld_c)
        n = l.inc(v.memset(onesf[:], 1.0), "dv")
        v.wait_ge(l.s("dv"), n)
    dv_ones = l.cnt["dv"]

    @block.tensor
    def _(t):
        t.wait_ge(l.s("dv"), dv_ones)
        l.inc(t.matmul(psO[:], pm_sb[:], cnt_sb[:], start=True, stop=True), "pe")
    pe_off1 = l.cnt["pe"]

    @block.vector
    def _(v):
        v.wait_ge(l.s("pe"), pe_off1)
        n = l.inc(v.tensor_copy(offrow[:], psO[:]), "dv")
        v.wait_ge(l.s("dv"), n)
    dv_offrow = l.cnt["dv"]

    @block.tensor
    def _(t):
        t.wait_ge(l.s("dv"), dv_offrow)
        l.inc(t.matmul(psB2[:], onesf[:], offrow[:], start=True, stop=True), "pe")
    pe_off2 = l.cnt["pe"]

    @block.vector
    def _(v):
        def fin(i):
            n = l.inc(i, "dv")
            v.wait_ge(l.s("dv"), n)
        v.wait_ge(l.s("pe"), pe_off2)
        fin(v.tensor_copy(off_b[:], psB2[:]))
        fin(v.memset(oaccA[:], 0.0))
        a, b = oaccA, oaccB
        for e in range(EE):
            fin(v.tensor_scalar(eqm[:], ekall[:], float(e), off_b[:, e:e + 1],
                                ALU.is_equal, ALU.mult))
            fin(v.tensor_tensor(b[:], a[:], eqm[:], ALU.add))
            a, b = b, a
        fin(v.tensor_tensor(posg[:], a[:], pkall[:], ALU.add))
        fin(v.tensor_scalar_min(b[:], posg[:], float(CAP - 1)))
        fin(v.scalar_tensor_tensor(gidx[:], ekall[:], float(CAP), b[:],
                                   ALU.mult, ALU.add))
        fin(v.tensor_copy(gidx_u[:], gidx[:]))
        if drop:
            fin(v.tensor_scalar(eqm[:], posg[:], float(CAP), None, ALU.is_lt))
            fin(v.tensor_tensor(wv[:], wkall[:], eqm[:], ALU.mult))
        else:
            fin(v.tensor_copy(wv[:], wkall[:]))
    dv_gidx = l.cnt["dv"]

    gp_col = {}
    dv_use = {0: 0, 1: 0, 2: 0, 3: 0}
    dv_tile = {}

    for i in range(NT):
        for k in range(KK):
            col = i * KK + k
            r = col % 4

            @block.gpsimd
            def _(g, col=col, r=r):
                if col == 0:
                    g.wait_ge(l.s("dv"), dv_gidx)
                    breg_l = g.alloc_register()
                    g.reg_mov(breg_l, EE * CAP - 1)
                    l._breg = breg_l
                if dv_use[r]:
                    g.wait_ge(l.s("dv"), dv_use[r])
                l.dinc(g.indirect_dma_start(
                    out=yr[:, r, :], out_offset=None,
                    in_=yall_h[:, :],
                    in_offset=IndirectOffsetOnAxis(gidx_u[:, col:col + 1], 0),
                    bounds_check=l._breg, oob_is_err=False), "qGP")
                if col == NC - 1:
                    g.free_register(l._breg)
            gp_col[col] = l.cnt["qGP"]

            @block.vector
            def _(v, i=i, k=k, col=col, r=r):
                def fin(instr):
                    n = l.inc(instr, "dv")
                    v.wait_ge(l.s("dv"), n)
                v.wait_ge(l.s("qGP"), gp_col[col])
                if i == 0 and k == 0:
                    v.wait_ge(l.s("qX"), ld_x)
                ai = (i % 2) * 2
                if k == 0:
                    fin(v.tensor_scalar(acc[:, ai, :], yr[:, r, :],
                                        wv[:, col:col + 1], None, ALU.mult))
                else:
                    src = ai + (k - 1) % 2
                    dst = ai + k % 2
                    fin(v.scalar_tensor_tensor(acc[:, dst, :], yr[:, r, :],
                                               wv[:, col:col + 1], acc[:, src, :],
                                               ALU.mult, ALU.add))
                dv_use[r] = l.cnt["dv"]
                if k == KK - 1:
                    last = ai + (KK - 1) % 2
                    fin(v.scalar_tensor_tensor(x_sb[:, i, :], acc[:, last, :], 0.5,
                                               x_sb[:, i, :], ALU.mult, ALU.add))
                    dv_tile[i] = l.cnt["dv"]

    @block.sync
    def _(s):
        xr = xout_h.rearrange("(i p) d -> p i d", p=128)
        for q in range(4):
            s.wait_ge(l.s("dv"), dv_tile[q * 8 + 7])
            l.dinc(s.dma_start(out=xr[:, q * 8:(q + 1) * 8, :],
                               in_=x_sb[:, q * 8:(q + 1) * 8, :]), "st")
        s.wait_ge(l.s("st"), l.cnt["st"])
        s.wait_ge(l.s("qGP"), l.cnt["qGP"])
    return l


def build_griffin_launch(debug=False):
    G = 8
    NG = NT // G
    TCH = 512
    NTCH = T // TCH
    l = L("grif")
    xin = l.dram("xin", [T, D], F32, kind="ExternalInput")
    win_h = l.dram("win", [D, D], F32, kind="ExternalInput")
    wr_h = l.dram("wr", [D, D], F32, kind="ExternalInput")
    wi_h = l.dram("wi", [D, D], F32, kind="ExternalInput")
    wout_h = l.dram("wout", [D, D], F32, kind="ExternalInput")
    bin_h = l.dram("binb", [8, 128], F32, kind="ExternalInput")
    br_h = l.dram("brb", [8, 128], F32, kind="ExternalInput")
    bi_h = l.dram("bib", [8, 128], F32, kind="ExternalInput")
    bout_h = l.dram("boutb", [8, 128], F32, kind="ExternalInput")
    loga_h = l.dram("logab", [8, 128], F32, kind="ExternalInput")
    xout_h = l.dram("xout", [T, D], F32, kind="ExternalOutput")
    xn_dram = l.dram("xn_scr", [T, D], BF16, kind="Internal")
    a1fm = l.dram("a1fm", [D, T], BF16,
                  kind="ExternalOutput" if debug else "Internal")

    xq = l.sb("xq", [128, G, D], F32)
    xng = l.sb("xng", [128, 2, G, D], BF16)
    stat = l.sb("stat", [128, 4 * G], F32)
    lnps = l.sb("lnps", [128, 5, G], F32)
    wini = l.sb("wini", [128, 8, D], BF16)
    wri = l.sb("wri", [128, 8, D], BF16)
    wii = l.sb("wii", [128, 8, D], BF16)
    wouti = l.sb("wouti", [128, 8, D], BF16)
    xnT = l.sb("xnT", [128, 2, 8, TCH], BF16)
    bb = l.sb("bb", [128, 4, 8], F32)
    loga_sb = l.sb("loga_sb", [128, 8], F32)
    a_ch = l.sb("a_ch", [128, 8], F32)
    rsb = l.sb("rsb", [128, TCH], F32)
    isb = l.sb("isb", [128, TCH], F32)
    inbb = l.sb("inbb", [128, TCH], F32)
    h1b = l.sb("h1b", [128, TCH], F32)
    cpb = l.sb("cpb", [128, TCH], F32)
    ones512 = l.sb("ones512", [128, TCH], F32)
    hstate = l.sb("hstate", [128, 2, 8], F32)
    hb = l.sb("hb", [128, 2, 8, TCH], BF16)
    evb = l.sb("evb", [128, 8, TCH], BF16)
    a1tm = l.sb("a1tm", [128, 2, D], BF16)
    a1f = l.sb("a1f", [128, D], F32)
    x1t = l.sb("x1t", [128, 2, D], F32)
    x2t = l.sb("x2t", [128, 2, D], F32)
    wrm = l.sb("wrm", [128, 8], F32)

    l.open_block()
    block = l.block
    for nm in ("trS", "stA", "trA", "qX2", "stX"):
        l.sem(nm)
    dmp_a = l.ps("lndmp", [128, D], F32)
    psR = l.ps("psR", [128, TCH])
    psI = l.ps("psI", [128, TCH])
    psIn = l.ps("psIn", [128, TCH])
    psO = [l.ps(f"psO{j}", [128, TCH]) for j in range(2)]

    @block.scalar
    def _(a):
        l.inc(a.activation(wrm[:], wrm[:], AF.Sigmoid), "ac")
    ac_warm = l.cnt["ac"]

    @block.sync
    def _(s):
        for bh, j in ((bin_h, 0), (br_h, 1), (bi_h, 2), (bout_h, 3)):
            l.dinc(s.dma_start(out=bb[:, j, :], in_=bh.rearrange("c p -> p c"),
                               allow_slow_non_contiguous=True), "qC")
        l.dinc(s.dma_start(out=loga_sb[:], in_=loga_h.rearrange("c p -> p c"),
                           allow_slow_non_contiguous=True), "qC")
    ld_c = l.cnt["qC"]

    @block.gpsimd
    def _(g):
        for wh, wsb in ((win_h, wini), (wr_h, wri), (wi_h, wii), (wout_h, wouti)):
            for ft in range(8):
                l.dinc(g.dma_start(out=wsb[:, ft, :],
                                   in_=wh[ft * 128:(ft + 1) * 128, :]), "qW")
    ld_w = l.cnt["qW"]

    @block.scalar
    def _(a):
        a.wait_ge(l.s("qC"), ld_c)
        l.inc(a.activation(a_ch[:], loga_sb[:], AF.Sigmoid), "ac")
    ac_ach = l.cnt["ac"]

    @block.vector
    def _(v):
        n = l.inc(v.memset(ones512[:], 1.0), "dv")
        n = l.inc(v.memset(hstate[:], 0.0), "dv")
        v.wait_ge(l.s("dv"), n)
    dv_init = l.cnt["dv"]

    apply_dv = {}
    xnst_after = {}
    xr_in = xin.rearrange("(i p) d -> p i d", p=128)
    xnr = xn_dram.rearrange("(i p) d -> p i d", p=128)
    for g in range(NG):
        @block.sync
        def _(s, g=g):
            if g >= 1:
                s.wait_ge(l.s("dv"), apply_dv[g - 1])
            l.dinc(s.dma_start(out=xq[:], in_=xr_in[:, g * G:(g + 1) * G, :]), "qX")
        ld_x = l.cnt["qX"]
        ready = [("qX", ld_x)]
        if g >= 2:
            ready.append(("qN", xnst_after[g - 2]))
        apply_dv[g] = emit_ln(l, xq, xng[:, g % 2], stat, ready=ready, G=G,
                              scratch=(dmp_a, lnps))

        @block.sync
        def _(s, g=g):
            s.wait_ge(l.s("dv"), apply_dv[g])
            l.dinc(s.dma_start(out=xnr[:, g * G:(g + 1) * G, :],
                               in_=xng[:, g % 2]), "qN")
            xnst_after[g] = l.cnt["qN"]

    trs_after = {}
    peProj = {}
    peProj_all = {}
    actG = {}
    dveG = {}
    hb_all = {}
    peO = {}
    peO_all = {}
    evbAct = {}
    stA_after = {}

    def emit_transp(tch):
        @block.sync
        def _(s, tch=tch):
            s.wait_ge(l.s("qN"), xnst_after[(tch * TCH) // (G * 128)])
            if tch >= 2:
                s.wait_ge(l.s("pe"), peProj_all[tch - 2])
            for ft in range(8):
                l.dinc(s.dma_start(
                    out=xnT[:, tch % 2, ft, :],
                    in_=xn_dram[tch * TCH:(tch + 1) * TCH, ft * 128:(ft + 1) * 128],
                    transpose=True), "trS")
            trs_after[tch] = l.cnt["trS"]

    emit_transp(0)
    emit_transp(1)
    for tch in range(NTCH):
        for ct in range(8):
            @block.tensor
            def _(t, tch=tch, ct=ct):
                if ct == 0:
                    t.wait_ge(l.s("trS"), trs_after[tch])
                    if tch == 0:
                        t.wait_ge(l.s("qW"), ld_w)
                prev = (tch, ct - 1) if ct > 0 else (tch - 1, 7)
                if prev in actG:
                    t.wait_ge(l.s("ac"), actG[prev])
                if prev in dveG:
                    t.wait_ge(l.s("dv"), dveG[prev])
                for ps, wsb in ((psR, wri), (psI, wii), (psIn, wini)):
                    for ft in range(8):
                        mm = t.matmul(ps[:], wsb[:, ft, ct * 128:(ct + 1) * 128],
                                      xnT[:, tch % 2, ft, :],
                                      start=(ft == 0), stop=(ft == 7))
                l.inc(mm, "pe")
                peProj[(tch, ct)] = l.cnt["pe"]
                if ct == 7:
                    peProj_all[tch] = l.cnt["pe"]

            @block.scalar
            def _(a, tch=tch, ct=ct):
                a.wait_ge(l.s("pe"), peProj[(tch, ct)])
                if tch == 0 and ct == 0:
                    a.wait_ge(l.s("ac"), ac_ach)
                prev = (tch, ct - 1) if ct > 0 else (tch - 1, 7)
                if prev in dveG:
                    a.wait_ge(l.s("dv"), dveG[prev])
                l.inc(a.activation(rsb[:], psR[:], AF.Sigmoid,
                                   bias=bb[:, 1, ct:ct + 1]), "ac")
                l.inc(a.activation(isb[:], psI[:], AF.Sigmoid,
                                   bias=bb[:, 2, ct:ct + 1]), "ac")
                actG[(tch, ct)] = l.cnt["ac"]

            @block.vector
            def _(v, tch=tch, ct=ct):
                def fin(instr):
                    n = l.inc(instr, "dv")
                    v.wait_ge(l.s("dv"), n)
                v.wait_ge(l.s("ac"), actG[(tch, ct)])
                if tch == 0 and ct == 0:
                    v.wait_ge(l.s("dv"), dv_init)
                if ct == 0 and tch >= 2:
                    v.wait_ge(l.s("pe"), peO_all[tch - 2])
                fin(v.tensor_scalar(inbb[:], psIn[:], bb[:, 0, ct:ct + 1], None,
                                    ALU.add))
                fin(v.tensor_scalar(h1b[:], rsb[:], a_ch[:, ct:ct + 1], None,
                                    ALU.mult))
                at = h1b
                fin(v.tensor_tensor(rsb[:], isb[:], inbb[:], ALU.mult))
                fin(v.tensor_scalar(isb[:], at[:], -1.0, 1.0, ALU.mult,
                                    ALU.add))
                fin(v.tensor_tensor(inbb[:], isb[:], rsb[:], ALU.mult))
                fin(v.tensor_tensor_scan(cpb[:], at[:], inbb[:], 0.0,
                                         ALU.mult, ALU.add))
                fin(v.tensor_tensor_scan(rsb[:], at[:], ones512[:], 1.0,
                                         ALU.mult, ALU.mult))
                fin(v.scalar_tensor_tensor(isb[:], rsb[:],
                                           hstate[:, tch % 2, ct:ct + 1], cpb[:],
                                           ALU.mult, ALU.add))
                fin(v.tensor_copy(hstate[:, (tch + 1) % 2, ct:ct + 1],
                                  isb[:, TCH - 1:TCH]))
                fin(v.tensor_copy(hb[:, tch % 2, ct, :], isb[:]))
                dveG[(tch, ct)] = l.cnt["dv"]
        hb_all[tch] = l.cnt["dv"]

        for dt in range(8):
            @block.tensor
            def _(t, tch=tch, dt=dt):
                if dt == 0:
                    t.wait_ge(l.s("dv"), hb_all[tch])
                prevo = (tch, dt - 2) if dt >= 2 else (tch - 1, dt % 2 + 6)
                if prevo in evbAct:
                    t.wait_ge(l.s("dv"), evbAct[prevo])
                for ct in range(8):
                    mm = t.matmul(psO[dt % 2][:],
                                  wouti[:, ct, dt * 128:(dt + 1) * 128],
                                  hb[:, tch % 2, ct, :],
                                  start=(ct == 0), stop=(ct == 7))
                l.inc(mm, "pe")
                peO[(tch, dt)] = l.cnt["pe"]
                if dt == 7:
                    peO_all[tch] = l.cnt["pe"]

            @block.vector
            def _(v, tch=tch, dt=dt):
                v.wait_ge(l.s("pe"), peO[(tch, dt)])
                if dt == 0 and tch >= 1:
                    v.wait_ge(l.s("stA"), stA_after[tch - 1])
                n = l.inc(v.tensor_scalar(evb[:, dt, :], psO[dt % 2][:],
                                          bb[:, 3, dt:dt + 1], None, ALU.add),
                          "dv")
                v.wait_ge(l.s("dv"), n)
                evbAct[(tch, dt)] = l.cnt["dv"]

        @block.sync
        def _(s, tch=tch):
            s.wait_ge(l.s("dv"), evbAct[(tch, 7)])
            ar = a1fm.rearrange("(dt p) t -> p dt t", p=128)
            l.dinc(s.dma_start(out=ar[:, :, tch * TCH:(tch + 1) * TCH],
                               in_=evb[:]), "stA")
            stA_after[tch] = l.cnt["stA"]
        if tch + 2 < NTCH:
            emit_transp(tch + 2)

    add_dv = {}
    stx_after = {}
    xo_r = xout_h.rearrange("(i p) d -> p i d", p=128)
    for i in range(NT):
        @block.sync
        def _(s, i=i):
            s.wait_ge(l.s("stA"), stA_after[(i * 128) // TCH])
            if i >= 2:
                s.wait_ge(l.s("dv"), add_dv[i - 2])
            l.dinc(s.dma_start(out=a1tm[:, i % 2, :],
                               in_=a1fm[:, i * 128:(i + 1) * 128],
                               transpose=True), "trA")
            l.dinc(s.dma_start(out=x1t[:, i % 2, :], in_=xr_in[:, i, :]), "qX2")
        tra = l.cnt["trA"]
        qx2 = l.cnt["qX2"]

        @block.vector
        def _(v, i=i):
            def fin(instr):
                n = l.inc(instr, "dv")
                v.wait_ge(l.s("dv"), n)
            v.wait_ge(l.s("trA"), tra)
            v.wait_ge(l.s("qX2"), qx2)
            if i >= 2:
                v.wait_ge(l.s("stX"), stx_after[i - 2])
            fin(v.tensor_copy(a1f[:], a1tm[:, i % 2, :]))
            fin(v.tensor_tensor(x2t[:, i % 2, :], x1t[:, i % 2, :], a1f[:],
                                ALU.add))
            add_dv[i] = l.cnt["dv"]

        @block.sync
        def _(s, i=i):
            s.wait_ge(l.s("dv"), add_dv[i])
            l.dinc(s.dma_start(out=xo_r[:, i, :], in_=x2t[:, i % 2, :]), "stX")
            stx_after[i] = l.cnt["stX"]

    @block.sync
    def _(s):
        s.wait_ge(l.s("stX"), l.cnt["stX"])
        s.wait_ge(l.s("stA"), l.cnt["stA"])
        s.wait_ge(l.s("qN"), l.cnt["qN"])
    return l


def build_mesa_launch(debug=False, ns_it=None):
    H, C, NCH, NGR = 128, 128, 32, 4
    TCH, NTCH = 512, 8
    NS_IT, KCH, CHEB_L = 12, 14, 16.0
    if ns_it is not None:
        NS_IT = ns_it
    EPS = 1e-6
    G = 8
    NG = NT // G

    l = L("mesa")
    xin = l.dram("xin", [T, D], F32, kind="ExternalInput")
    wq_h = l.dram("wqT", [D, H], F32, kind="ExternalInput")
    wk_h = l.dram("wkT", [D, H], F32, kind="ExternalInput")
    wv_h = l.dram("wvT", [D, H], F32, kind="ExternalInput")
    wo_h = l.dram("woT", [H, D], F32, kind="ExternalInput")
    bq_h = l.dram("bq", [128, 1], F32, kind="ExternalInput")
    bk_h = l.dram("bk", [128, 1], F32, kind="ExternalInput")
    bv_h = l.dram("bv", [128, 1], F32, kind="ExternalInput")
    bob_h = l.dram("bob", [8, 128], F32, kind="ExternalInput")
    identb_h = l.dram("identb", [128, 128], BF16, kind="ExternalInput")
    trif_h = l.dram("trif", [128, 128], F32, kind="ExternalInput")
    identf_h = l.dram("identf", [128, 128], F32, kind="ExternalInput")
    xout_h = l.dram("xout", [T, D], F32, kind="ExternalOutput")
    xn_dram = l.dram("xn_scr", [T, D], BF16, kind="Internal")
    a2fm = l.dram("a2fm", [D, T], BF16,
                  kind="ExternalOutput" if debug else "Internal")
    dbg = {}
    if debug:
        for nm, w in (("dbg_qt", T), ("dbg_ut", T), ("dbg_vst", T),
                      ("dbg_x", NCH * H), ("dbg_s", NCH * H),
                      ("dbg_yn", NCH * H), ("dbg_ot", T), ("dbg_wt", NCH * H),
                      ("dbg_ms", NCH * H), ("dbg_utok", NCH * H)):
            dbg[nm] = l.dram(nm, [128, w], BF16, kind="ExternalOutput")

    xq = l.sb("xq", [128, G, D], F32)
    xng = l.sb("xng", [128, G, D], BF16)
    stat = l.sb("stat", [128, 4 * G], F32)
    lnps = l.sb("lnps", [128, 5, G], F32)
    xnT = l.sb("xnT", [128, 2, 8, TCH], BF16)
    wq_sb = l.sb("wq_sb", [128, 8, H], BF16)
    wk_sb = l.sb("wk_sb", [128, 8, H], BF16)
    wv_sb = l.sb("wv_sb", [128, 8, H], BF16)
    wo_sb = l.sb("wo_sb", [128, 8 * H], BF16)
    bqc = l.sb("bqc", [128, 1], F32)
    bkc = l.sb("bkc", [128, 1], F32)
    bvc = l.sb("bvc", [128, 1], F32)
    bob_sb = l.sb("bob_sb", [128, 8], F32)
    identb = l.sb("identb_sb", [128, 128], BF16)
    trif = l.sb("trif_sb", [128, 128], F32)
    identf = l.sb("identf_sb", [128, 128], F32)
    QT = l.sb("QT", [128, T], BF16)
    UT = l.sb("UT", [128, T], BF16)
    VsT = l.sb("VsT", [128, T], BF16)
    OT = l.sb("OT", [128, T], BF16)
    arena = l.sb("arena", [128, 2048], F32)
    KTf = arena[:, 0:TCH]
    VTf = arena[:, TCH:2 * TCH]
    prodsb = arena[:, 2 * TCH:3 * TCH]
    sqb = arena[:, 3 * TCH:4 * TCH]
    rowA = l.sb("rowA", [1, TCH], F32)
    rowB = l.sb("rowB", [1, TCH], F32)
    srow = l.sb("srow", [1, TCH], F32)
    nfrow = l.sb("nfrow", [1, NCH], F32)
    nfs = l.sb("nfs", [1, 4 * NCH], F32)
    rowscr = l.sb("rowscr", [1, 128], F32)
    ones1 = l.sb("ones1", [1, 128], F32)
    onescol = l.sb("onescol", [128, 1], F32)
    cbsb = l.sb("cbsb", [128, NCH], F32)
    Gf = arena[:, 0:128]
    Gfsq = arena[:, 128:256]
    Utok = l.sb("Utok", [128, NCH * H], BF16)
    Sb = Utok
    Ms = l.sb("Ms", [128, NCH * H], BF16)
    Xb = l.sb("Xb", [128, NCH * H], BF16)
    tbb = l.sb("tbb", [128, 8 * H], BF16)
    XTb = l.sb("XTb", [128, 8 * H], BF16)
    mtw = l.sb("mtw", [128, 8 * H], F32)
    twoI = mtw
    maskb = mtw
    WTb = l.sb("WTb", [128, NCH * H], BF16)
    Bbf = l.sb("Bbf", [128, NCH * H], BF16)
    Yb = l.sb("Yb", [128, 8 * H], F32)
    Dl = l.sb("Dl", [128, 8 * H], F32)
    Ybn = l.sb("Ybn", [128, NCH * H], BF16)
    tA = arena[:, 0:8 * H]
    tB = arena[:, 8 * H:16 * H]
    ZTsb = l.sb("ZTsb", [128, H], BF16)
    Wtok = l.sb("Wtok", [128, H], BF16)
    Ztok = l.sb("Ztok", [128, H], BF16)
    Vstok = l.sb("Vstok", [128, H], BF16)
    awm = l.sb("awm", [128, H], BF16)
    MTsb = l.sb("MTsb", [128, H], BF16)
    wrm = l.sb("wrm", [128, 8], F32)

    evb = xnT[:, 0, :, :]
    a2tm = xng[:, 0:2, :]
    l.open_block()
    block = l.block
    for nm in ("trS", "stA", "trA", "qX2", "stX"):
        l.sem(nm)
    P1 = l.ps("P1", [128, TCH])
    P2 = l.ps("P2", [128, TCH])
    PG = l.ps("PG", [128, H])
    PW = l.ps("PW", [128, 1024])
    PV = l.ps("PV", [128, 1024], BF16)
    PT = l.ps("PT", [128, H], BF16)

    @block.scalar
    def _(a):
        l.inc(a.activation(wrm[:], wrm[:], AF.Sqrt), "ac")

    @block.sync
    def _(s):
        for hh, sb_ in ((bq_h, bqc), (bk_h, bkc), (bv_h, bvc)):
            l.dinc(s.dma_start(out=sb_[:], in_=hh[:],
                               allow_slow_non_contiguous=True), "qC")
        l.dinc(s.dma_start(out=bob_sb[:], in_=bob_h.rearrange("c p -> p c"),
                           allow_slow_non_contiguous=True), "qC")
        l.dinc(s.dma_start(out=identb[:], in_=identb_h[:]), "qC")
        l.dinc(s.dma_start(out=trif[:], in_=trif_h[:]), "qC")
        l.dinc(s.dma_start(out=identf[:], in_=identf_h[:]), "qC")
    ld_c = l.cnt["qC"]

    @block.gpsimd
    def _(g):
        for wh, wsb in ((wq_h, wq_sb), (wk_h, wk_sb), (wv_h, wv_sb)):
            for ft in range(8):
                l.dinc(g.dma_start(out=wsb[:, ft, :],
                                   in_=wh[ft * 128:(ft + 1) * 128, :]), "qW")
        l.dinc(g.dma_start(out=wo_sb[:], in_=wo_h[:, :]), "qW")
    ld_w = l.cnt["qW"]

    @block.vector
    def _(v):
        def fin(instr):
            n = l.inc(instr, "dv")
            v.wait_ge(l.s("dv"), n)
        v.wait_ge(l.s("qC"), ld_c)
        fin(v.memset(ones1[:], 1.0))
        fin(v.memset(onescol[:], 1.0))
        fin(v.memset(MTsb[:], 0.0))
        fin(v.memset(nfrow[:], 0.0))
        for j in range(8):
            fin(v.tensor_scalar(twoI[:, j * H:(j + 1) * H], identf[:], 2.0,
                                None, ALU.mult))
    dv_init = l.cnt["dv"]

    apply_dv = {}
    xnst_after = {}
    xr_in = xin.rearrange("(i p) d -> p i d", p=128)
    xnr = xn_dram.rearrange("(i p) d -> p i d", p=128)
    for g in range(NG):
        @block.sync
        def _(s, g=g):
            if g >= 1:
                s.wait_ge(l.s("dv"), apply_dv[g - 1])
            l.dinc(s.dma_start(out=xq[:], in_=xr_in[:, g * G:(g + 1) * G, :]), "qX")
        ld_x = l.cnt["qX"]
        ready = [("qX", ld_x)]
        if g >= 1:
            ready.append(("qN", xnst_after[g - 1]))
        apply_dv[g] = emit_ln(l, xq, xng, stat, ready=ready, G=G,
                              scratch=(PW, lnps))

        @block.sync
        def _(s, g=g):
            s.wait_ge(l.s("dv"), apply_dv[g])
            l.dinc(s.dma_start(out=xnr[:, g * G:(g + 1) * G, :], in_=xng[:]), "qN")
            xnst_after[g] = l.cnt["qN"]
    ln_total = l.cnt["dv"]

    trs_after = {}
    peProj_all = {}
    dvU = {}
    dvVs = {}
    dvP1free = {-1: 0}
    dvP2free = {-1: 0}
    dvPWfree = {-1: ln_total}

    def emit_transp(tch):
        @block.sync
        def _(s, tch=tch):
            s.wait_ge(l.s("qN"), xnst_after[(tch * TCH) // (G * 128)])
            if tch >= 2:
                s.wait_ge(l.s("pe"), peProj_all[tch - 2])
            for ft in range(8):
                l.dinc(s.dma_start(
                    out=xnT[:, tch % 2, ft, :],
                    in_=xn_dram[tch * TCH:(tch + 1) * TCH, ft * 128:(ft + 1) * 128],
                    transpose=True), "trS")
            trs_after[tch] = l.cnt["trS"]

    emit_transp(0)
    emit_transp(1)
    for tch in range(NTCH):
        sl = slice(tch * TCH, (tch + 1) * TCH)

        @block.tensor
        def _(t, tch=tch):
            t.wait_ge(l.s("trS"), trs_after[tch])
            if tch == 0:
                t.wait_ge(l.s("qW"), ld_w)
            t.wait_ge(l.s("dv"), dvP1free[tch - 1])
            for ft in range(8):
                mm = t.matmul(P1[:], wq_sb[:, ft, :], xnT[:, tch % 2, ft, :],
                              start=(ft == 0), stop=(ft == 7))
            l.inc(mm, "pe")
        peQ = l.cnt["pe"]

        @block.vector
        def _(v, tch=tch, sl=sl):
            def fin(instr):
                n = l.inc(instr, "dv")
                v.wait_ge(l.s("dv"), n)
            v.wait_ge(l.s("pe"), peQ)
            fin(v.tensor_scalar(QT[:, sl], P1[:], bqc[:, 0:1], None, ALU.add))
        dvQT = l.cnt["dv"]

        @block.tensor
        def _(t, tch=tch):
            t.wait_ge(l.s("dv"), dvP2free[tch - 1])
            for ft in range(8):
                mm = t.matmul(P2[:], wk_sb[:, ft, :], xnT[:, tch % 2, ft, :],
                              start=(ft == 0), stop=(ft == 7))
            l.inc(mm, "pe")
        peK = l.cnt["pe"]

        @block.vector
        def _(v, tch=tch, sl=sl):
            def fin(instr):
                n = l.inc(instr, "dv")
                v.wait_ge(l.s("dv"), n)
            v.wait_ge(l.s("pe"), peK)
            fin(v.tensor_scalar(KTf[:], P2[:], bkc[:, 0:1], None, ALU.add))
            fin(v.scalar_tensor_tensor(prodsb[:], P1[:], bqc[:, 0:1], KTf[:],
                                       ALU.add, ALU.mult))
            fin(v.tensor_tensor(sqb[:], KTf[:], KTf[:], ALU.mult))
        dvProd = l.cnt["dv"]

        @block.tensor
        def _(t, tch=tch):
            t.wait_ge(l.s("dv"), dvProd)
            for ft in range(8):
                mm = t.matmul(P1[:], wv_sb[:, ft, :], xnT[:, tch % 2, ft, :],
                              start=(ft == 0), stop=(ft == 7))
            t.matmul(P2[0:1, :], onescol[:], prodsb[:], start=True, stop=True)
            mm = t.matmul(P2[32:33, :], onescol[:], sqb[:], start=True, stop=True)
            l.inc(mm, "pe")
        peV = l.cnt["pe"]

        @block.scalar
        def _(a, tch=tch):
            a.wait_ge(l.s("pe"), peV)
            l.inc(a.activation(rowA[:], P2[32:33, :], AF.Sqrt), "ac")
        acSq = l.cnt["ac"]

        @block.vector
        def _(v, tch=tch, sl=sl):
            def fin(instr):
                n = l.inc(instr, "dv")
                v.wait_ge(l.s("dv"), n)
            v.wait_ge(l.s("ac"), acSq)
            fin(v.tensor_scalar(VTf[:], P1[:], bvc[:, 0:1], None, ALU.add))
            fin(v.tensor_scalar(rowB[:], rowA[:], EPS, None, ALU.add))
            fin(v.reciprocal(rowA[:], rowB[:]))
            fin(v.tensor_copy(srow[:], P2[0:1, :]))
        dvRows = l.cnt["dv"]

        @block.tensor
        def _(t, tch=tch):
            t.wait_ge(l.s("dv"), dvRows)
            t.wait_ge(l.s("dv"), dvPWfree[tch - 1])
            t.matmul(PW[:, 0:TCH], ones1[:], rowA[:], start=True, stop=True)
            mm = t.matmul(PW[:, TCH:2 * TCH], ones1[:], srow[:], start=True,
                          stop=True)
            l.inc(mm, "pe")
        peBc = l.cnt["pe"]

        @block.vector
        def _(v, tch=tch, sl=sl):
            def fin(instr):
                n = l.inc(instr, "dv")
                v.wait_ge(l.s("dv"), n)
            v.wait_ge(l.s("pe"), peBc)
            fin(v.tensor_tensor(UT[:, sl], KTf[:], PW[:, 0:TCH], ALU.mult))
            fin(v.tensor_tensor(VsT[:, sl], VTf[:], PW[:, TCH:2 * TCH], ALU.mult))
        dvU[tch] = l.cnt["dv"]
        dvVs[tch] = l.cnt["dv"]
        dvP1free[tch] = dvRows
        dvP2free[tch] = dvRows
        dvPWfree[tch] = l.cnt["dv"]
        peProj_all[tch] = peV
        if tch + 2 < NTCH:
            emit_transp(tch + 2)
    proj_dv_total = l.cnt["dv"]

    dvUtok = {-1: 0}
    dvSnap = {-1: 0}
    nfacc = {-1: 0}
    peG_i = {}
    for i in range(NCH):
        csl = slice(i * H, (i + 1) * H)

        @block.tensor
        def _(t, i=i, csl=csl):
            t.wait_ge(l.s("dv"), dvU[i // 4])
            t.wait_ge(l.s("dv"), dvUtok[i - 1])
            l.inc(t.transpose(PT[:], UT[:, csl], identb[:]), "pe")
        peT = l.cnt["pe"]

        @block.vector
        def _(v, i=i, csl=csl):
            def fin(instr):
                n = l.inc(instr, "dv")
                v.wait_ge(l.s("dv"), n)
            v.wait_ge(l.s("pe"), peT)
            fin(v.tensor_copy(Utok[:, csl], PT[:]))
            if i == 0:
                fin(v.tensor_copy(Ms[:, csl], identf[:]))
                fin(v.memset(Gf[:], 0.0))
            else:
                fin(v.tensor_tensor(Ms[:, csl], PG[:], identf[:], ALU.add))
                fin(v.tensor_copy(Gf[:], PG[:]))
            fin(v.tensor_tensor(Gfsq[:], Gf[:], Gf[:], ALU.mult))
        dvUtok[i] = l.cnt["dv"]
        dvSnap[i] = l.cnt["dv"]

        @block.tensor
        def _(t, i=i):
            t.wait_ge(l.s("dv"), dvSnap[i])
            t.wait_ge(l.s("dv"), nfacc[i - 1])
            t.matmul(P2[64:65, 0:128], onescol[:], Gfsq[:], start=True, stop=True)
            mm = t.matmul(PG[:], Utok[:, i * H:(i + 1) * H],
                          Utok[:, i * H:(i + 1) * H],
                          start=(i == 0), stop=True)
            l.inc(mm, "pe")
        peG_i[i] = l.cnt["pe"]

        @block.vector
        def _(v, i=i):
            v.wait_ge(l.s("pe"), peG_i[i])
            n = l.inc(v.tensor_scalar(rowscr[:], P2[64:65, 0:128], 1.0, 0.0,
                                      ALU.mult, ALU.add,
                                      accum_out=nfrow[:, i:i + 1]), "dv")
            v.wait_ge(l.s("dv"), n)
        nfacc[i] = l.cnt["dv"]

    @block.scalar
    def _(a):
        a.wait_ge(l.s("dv"), nfacc[NCH - 1])
        l.inc(a.activation(nfs[:, 0:NCH], nfrow[:], AF.Sqrt), "ac")
    ac_nf = l.cnt["ac"]

    @block.vector
    def _(v):
        def fin(instr):
            n = l.inc(instr, "dv")
            v.wait_ge(l.s("dv"), n)
        v.wait_ge(l.s("ac"), ac_nf)
        fin(v.tensor_scalar(nfs[:, NCH:2 * NCH], nfs[:, 0:NCH], 2.0, None,
                            ALU.add))
        fin(v.reciprocal(nfs[:, 2 * NCH:3 * NCH], nfs[:, NCH:2 * NCH]))
        fin(v.tensor_scalar(nfs[:, 3 * NCH:4 * NCH], nfs[:, 2 * NCH:3 * NCH],
                            2.0, None, ALU.mult))
    dv_c2 = l.cnt["dv"]

    @block.tensor
    def _(t):
        t.wait_ge(l.s("dv"), dv_c2)
        l.inc(t.matmul(P2[:, 0:NCH], ones1[:], nfs[:, 3 * NCH:4 * NCH],
                       start=True, stop=True), "pe")
    pe_cb = l.cnt["pe"]

    @block.vector
    def _(v):
        def fin(instr):
            n = l.inc(instr, "dv")
            v.wait_ge(l.s("dv"), n)
        v.wait_ge(l.s("pe"), pe_cb)
        fin(v.tensor_copy(cbsb[:], P2[:, 0:NCH]))
        for i in range(NCH):
            fin(v.tensor_scalar(Xb[:, i * H:(i + 1) * H], identf[:],
                                cbsb[:, i:i + 1], None, ALU.mult))
    dv_x0 = l.cnt["dv"]

    last_PW = {"v": dv_x0}
    last_PV = {"v": 0}
    for it in range(NS_IT):
        for grp in range(NGR):
            gsl = slice(grp * 8 * H, (grp + 1) * 8 * H)

            @block.tensor
            def _(t, it=it, grp=grp):
                t.wait_ge(l.s("dv"), last_PW["v"])
                t.wait_ge(l.s("dv"), last_PV["v"])
                for ci in range(8):
                    i = grp * 8 + ci
                    t.transpose(PV[:, ci * H:(ci + 1) * H],
                                Xb[:, i * H:(i + 1) * H], identb[:])
                for ci in range(8):
                    i = grp * 8 + ci
                    mm = t.matmul(PW[:, ci * H:(ci + 1) * H],
                                  Ms[:, i * H:(i + 1) * H],
                                  Xb[:, i * H:(i + 1) * H],
                                  start=True, stop=True)
                l.inc(mm, "pe")
            peMX = l.cnt["pe"]

            @block.vector
            def _(v, it=it, grp=grp, gsl=gsl):
                def fin(instr):
                    n = l.inc(instr, "dv")
                    v.wait_ge(l.s("dv"), n)
                v.wait_ge(l.s("pe"), peMX)
                fin(v.tensor_copy(XTb[:], PV[:]))
                fin(v.scalar_tensor_tensor(tbb[:], PW[:], -1.0, twoI[:],
                                           ALU.mult, ALU.add))
            last_PW["v"] = l.cnt["dv"]
            dvTb = l.cnt["dv"]

            @block.tensor
            def _(t, it=it, grp=grp):
                t.wait_ge(l.s("dv"), dvTb)
                for ci in range(8):
                    mm = t.matmul(PW[:, ci * H:(ci + 1) * H],
                                  XTb[:, ci * H:(ci + 1) * H],
                                  tbb[:, ci * H:(ci + 1) * H],
                                  start=True, stop=True)
                l.inc(mm, "pe")
            peXp = l.cnt["pe"]

            @block.vector
            def _(v, it=it, grp=grp, gsl=gsl):
                v.wait_ge(l.s("pe"), peXp)
                n = l.inc(v.tensor_copy(Xb[:, gsl], PW[:]), "dv")
                v.wait_ge(l.s("dv"), n)
            last_PW["v"] = l.cnt["dv"]
            last_PV["v"] = dvTb
    ns_total = l.cnt["dv"]

    dvWT = {-1: ns_total}
    dvB = {-1: 0}
    for i in range(NCH):
        csl = slice(i * H, (i + 1) * H)

        @block.tensor
        def _(t, i=i, csl=csl):
            t.wait_ge(l.s("dv"), dvWT[i - 1])
            l.inc(t.matmul(P1[0:H, :H], Xb[:, csl], UT[:, csl],
                           start=True, stop=True), "pe")
        peWT = l.cnt["pe"]

        @block.vector
        def _(v, i=i, csl=csl):
            v.wait_ge(l.s("pe"), peWT)
            n = l.inc(v.tensor_copy(WTb[:, csl], P1[0:H, :H]), "dv")
            v.wait_ge(l.s("dv"), n)
        dvWT[i] = l.cnt["dv"]

        @block.tensor
        def _(t, i=i, csl=csl):
            t.wait_ge(l.s("dv"), dvWT[i])
            t.wait_ge(l.s("dv"), dvB[i - 1])
            l.inc(t.matmul(P2[0:H, :H], UT[:, csl], WTb[:, csl],
                           start=True, stop=True), "pe")
        peS = l.cnt["pe"]

        @block.vector
        def _(v, i=i, csl=csl):
            def fin(instr):
                n = l.inc(instr, "dv")
                v.wait_ge(l.s("dv"), n)
            v.wait_ge(l.s("pe"), peS)
            fin(v.tensor_copy(Sb[:, csl], P2[0:H, :H]))
            fin(v.tensor_tensor(Bbf[:, csl], trif[:], P2[0:H, :H], ALU.mult))
        dvB[i] = l.cnt["dv"]

    d_ = 1.0 + CHEB_L / 2.0
    cc_ = CHEB_L / 2.0
    sigma1 = d_ / cc_
    rho = 1.0 / sigma1
    coefs = []
    for it in range(KCH - 1):
        rho_new = 1.0 / (2.0 * sigma1 - rho)
        coefs.append((rho_new * rho, 2.0 * rho_new / cc_))
        rho = rho_new

    @block.vector
    def _(v):
        def fin(instr):
            n = l.inc(instr, "dv")
            v.wait_ge(l.s("dv"), n)
        v.wait_ge(l.s("dv"), dvB[NCH - 1])
        for j in range(8):
            fin(v.tensor_copy(maskb[:, j * H:(j + 1) * H], trif[:]))
    dv_mask = l.cnt["dv"]

    last_PWc = {"v": dv_mask}
    for grp in range(NGR):
        gsl = slice(grp * 8 * H, (grp + 1) * 8 * H)

        @block.vector
        def _(v, grp=grp, gsl=gsl):
            def fin(instr):
                n = l.inc(instr, "dv")
                v.wait_ge(l.s("dv"), n)
            fin(v.tensor_scalar(Dl[:], Bbf[:, gsl], 1.0 / d_, None, ALU.mult))
            fin(v.tensor_copy(Yb[:], Dl[:]))
            fin(v.tensor_scalar(Ybn[:, gsl], Dl[:], -1.0, None, ALU.mult))
        last_PWc["v"] = l.cnt["dv"]

        for it in range(KCH - 1):
            c1, c2 = coefs[it]

            @block.tensor
            def _(t, it=it, grp=grp):
                t.wait_ge(l.s("dv"), last_PWc["v"])
                for ci in range(8):
                    i = grp * 8 + ci
                    mm = t.matmul(PW[:, ci * H:(ci + 1) * H],
                                  Sb[:, i * H:(i + 1) * H],
                                  Ybn[:, i * H:(i + 1) * H],
                                  start=True, stop=True)
                l.inc(mm, "pe")
            peSY = l.cnt["pe"]

            @block.vector
            def _(v, it=it, grp=grp, gsl=gsl, c1=c1, c2=c2):
                def fin(instr):
                    n = l.inc(instr, "dv")
                    v.wait_ge(l.s("dv"), n)
                v.wait_ge(l.s("pe"), peSY)
                fin(v.tensor_tensor(tA[:], maskb[:], PW[:], ALU.mult))
                fin(v.tensor_tensor(tB[:], Yb[:], tA[:], ALU.subtract))
                fin(v.tensor_tensor(tA[:], Bbf[:, gsl], tB[:], ALU.subtract))
                fin(v.tensor_scalar(tB[:], Dl[:], c1, None, ALU.mult))
                fin(v.scalar_tensor_tensor(Dl[:], tA[:], c2, tB[:],
                                           ALU.mult, ALU.add))
                fin(v.tensor_tensor(tA[:], Yb[:], Dl[:], ALU.add))
                fin(v.tensor_copy(Yb[:], tA[:]))
                fin(v.tensor_scalar(Ybn[:, gsl], tA[:], -1.0, None, ALU.mult))
            last_PWc["v"] = l.cnt["dv"]
    cheb_total = l.cnt["dv"]

    dvSeq = {-1: cheb_total}
    dvOT = {}
    dvMT = {-1: dv_init}
    for i in range(NCH):
        csl = slice(i * H, (i + 1) * H)

        @block.tensor
        def _(t, i=i, csl=csl):
            t.wait_ge(l.s("dv"), dvSeq[i - 1])
            l.inc(t.transpose(PT[:], WTb[:, csl], identb[:]), "pe")
        peWtokT = l.cnt["pe"]

        @block.vector
        def _(v, i=i):
            v.wait_ge(l.s("pe"), peWtokT)
            n = l.inc(v.tensor_copy(Wtok[:], PT[:]), "dv")
            v.wait_ge(l.s("dv"), n)
        dvWtok = l.cnt["dv"]

        @block.tensor
        def _(t, i=i, csl=csl):
            t.wait_ge(l.s("dv"), dvWtok)
            t.matmul(P1[0:H, :H], identb[:], WTb[:, csl], start=True, stop=False)
            l.inc(t.matmul(P1[0:H, :H], Wtok[:], Ybn[:, csl], start=False,
                           stop=True), "pe")
        peZT = l.cnt["pe"]

        @block.vector
        def _(v, i=i):
            v.wait_ge(l.s("pe"), peZT)
            n = l.inc(v.tensor_copy(ZTsb[:], P1[0:H, :H]), "dv")
            v.wait_ge(l.s("dv"), n)
        dvZT = l.cnt["dv"]

        @block.tensor
        def _(t, i=i, csl=csl):
            t.wait_ge(l.s("dv"), dvZT)
            l.inc(t.transpose(PT[:], ZTsb[:], identb[:]), "pe")
        peZtokT = l.cnt["pe"]

        @block.vector
        def _(v, i=i):
            v.wait_ge(l.s("pe"), peZtokT)
            n = l.inc(v.tensor_copy(Ztok[:], PT[:]), "dv")
            v.wait_ge(l.s("dv"), n)
        dvZtok = l.cnt["dv"]

        @block.tensor
        def _(t, i=i, csl=csl):
            t.wait_ge(l.s("dv"), dvZtok)
            l.inc(t.transpose(PT[:], VsT[:, csl], identb[:]), "pe")
        peVsT = l.cnt["pe"]

        @block.vector
        def _(v, i=i):
            v.wait_ge(l.s("pe"), peVsT)
            n = l.inc(v.tensor_copy(Vstok[:], PT[:]), "dv")
            v.wait_ge(l.s("dv"), n)
        dvVstok = l.cnt["dv"]

        @block.tensor
        def _(t, i=i, csl=csl):
            t.wait_ge(l.s("dv"), dvVstok)
            l.inc(t.matmul(P2[0:H, :H], ZTsb[:], QT[:, csl], start=True,
                           stop=True), "pe")
        peAW = l.cnt["pe"]

        @block.vector
        def _(v, i=i):
            v.wait_ge(l.s("pe"), peAW)
            n = l.inc(v.tensor_tensor(awm[:], trif[:], P2[0:H, :H], ALU.mult),
                      "dv")
            v.wait_ge(l.s("dv"), n)
        dvAwm = l.cnt["dv"]

        @block.tensor
        def _(t, i=i, csl=csl):
            t.wait_ge(l.s("dv"), dvAwm)
            t.wait_ge(l.s("dv"), dvMT[i - 1])
            t.matmul(P1[0:H, :H], MTsb[:], QT[:, csl], start=True, stop=False)
            l.inc(t.matmul(P1[0:H, :H], Vstok[:], awm[:], start=False,
                           stop=True), "pe")
            l.inc(t.matmul(PG[:], Ztok[:], Vstok[:], start=(i == 0),
                           stop=True), "pe")
        peO_i = l.cnt["pe"]

        @block.vector
        def _(v, i=i, csl=csl):
            def fin(instr):
                n = l.inc(instr, "dv")
                v.wait_ge(l.s("dv"), n)
            v.wait_ge(l.s("pe"), peO_i)
            fin(v.tensor_copy(OT[:, csl], P1[0:H, :H]))
            fin(v.tensor_copy(MTsb[:], PG[:]))
        dvOT[i] = l.cnt["dv"]
        dvMT[i] = l.cnt["dv"]
        dvSeq[i] = l.cnt["dv"]

    stA_after = {-1: 0}
    dvEvb = {-1: 0}
    for tch in range(NTCH):
        sl = slice(tch * TCH, (tch + 1) * TCH)
        for dt in range(8):
            @block.tensor
            def _(t, tch=tch, dt=dt, sl=sl):
                if dt == 0:
                    t.wait_ge(l.s("dv"), dvOT[(tch + 1) * 4 - 1])
                t.wait_ge(l.s("dv"), dvEvb[(tch, dt - 1) if dt else (tch - 1, 7)]
                          if (dt or tch) else -1) if False else None
                prev = (tch, dt - 1) if dt else (tch - 1, 7)
                if prev in dvEvb:
                    t.wait_ge(l.s("dv"), dvEvb[prev])
                l.inc(t.matmul(P2[:], wo_sb[:, dt * H:(dt + 1) * H], OT[:, sl],
                               start=True, stop=True), "pe")
            peOut = l.cnt["pe"]

            @block.vector
            def _(v, tch=tch, dt=dt):
                v.wait_ge(l.s("pe"), peOut)
                if dt == 0 and tch >= 1:
                    v.wait_ge(l.s("stA"), stA_after[tch - 1])
                n = l.inc(v.tensor_scalar(evb[:, dt, :], P2[:],
                                          bob_sb[:, dt:dt + 1], None, ALU.add),
                          "dv")
                v.wait_ge(l.s("dv"), n)
            dvEvb[(tch, dt)] = l.cnt["dv"]

        @block.sync
        def _(s, tch=tch, sl=sl):
            s.wait_ge(l.s("dv"), dvEvb[(tch, 7)])
            ar = a2fm.rearrange("(dt p) t -> p dt t", p=128)
            l.dinc(s.dma_start(out=ar[:, :, sl], in_=evb[:]), "stA")
            stA_after[tch] = l.cnt["stA"]

    add_dv = {}
    stx_after = {}
    xo_r = xout_h.rearrange("(i p) d -> p i d", p=128)
    x4t = xq[:, 0:2, :]
    x5t = xq[:, 2:4, :]
    for i in range(NT):
        @block.sync
        def _(s, i=i):
            if i == 0:
                s.wait_ge(l.s("qN"), l.cnt["qN"])
            s.wait_ge(l.s("stA"), stA_after[(i * 128) // TCH])
            if i >= 2:
                s.wait_ge(l.s("dv"), add_dv[i - 2])
            l.dinc(s.dma_start(out=a2tm[:, i % 2, :],
                               in_=a2fm[:, i * 128:(i + 1) * 128],
                               transpose=True), "trA")
            l.dinc(s.dma_start(out=x4t[:, i % 2, :], in_=xr_in[:, i, :]), "qX2")
        tra = l.cnt["trA"]
        qx2 = l.cnt["qX2"]

        @block.vector
        def _(v, i=i):
            v.wait_ge(l.s("trA"), tra)
            v.wait_ge(l.s("qX2"), qx2)
            if i >= 2:
                v.wait_ge(l.s("stX"), stx_after[i - 2])
            n = l.inc(v.tensor_tensor(x5t[:, i % 2, :], x4t[:, i % 2, :],
                                      a2tm[:, i % 2, :], ALU.add), "dv")
            v.wait_ge(l.s("dv"), n)
            add_dv[i] = l.cnt["dv"]

        @block.sync
        def _(s, i=i):
            s.wait_ge(l.s("dv"), add_dv[i])
            l.dinc(s.dma_start(out=xo_r[:, i, :], in_=x5t[:, i % 2, :]), "stX")
            stx_after[i] = l.cnt["stX"]

    if debug:
        @block.sync
        def _(s):
            s.wait_ge(l.s("dv"), l.cnt["dv"])
            for nm, src in (("dbg_qt", QT), ("dbg_ut", UT), ("dbg_vst", VsT),
                            ("dbg_x", Xb), ("dbg_s", Sb), ("dbg_yn", Ybn),
                            ("dbg_ot", OT), ("dbg_wt", WTb), ("dbg_ms", Ms)):
                l.dinc(s.dma_start(out=dbg[nm], in_=src[:]), "st")

    @block.sync
    def _(s):
        s.wait_ge(l.s("stX"), l.cnt["stX"])
        s.wait_ge(l.s("stA"), l.cnt["stA"])
        s.wait_ge(l.s("st"), l.cnt["st"])
        s.wait_ge(l.s("qN"), l.cnt["qN"])
    return l


_progs = {}


def _prog(key, builder):
    if key not in _progs:
        _progs[key] = builder()
    return _progs[key]


def _consts_inputs():
    return {
        "identb": np.eye(128, dtype=BFNP),
        "trib": np.triu(np.ones((128, 128), dtype=BFNP), 1),
    }


def _run(l, maps):
    return run_bass_kernel_spmd(l.nc, maps, core_ids=list(range(NCORE))).results


def _moe_layer(I, xcur, EE, KK, CAP, PLOC, DFF, wg, wfc, wpr, drop, nexp_l,
               consts):
    ids = list(range(NCORE))
    lp = _prog(("prep", EE), lambda: build_prep_launch(EE, KK, PLOC))
    r1 = _run(lp, [dict(xin=np.ascontiguousarray(xcur[c]), wgate=wg, **consts)
                   for c in ids])
    cnt_dev = np.stack([r1[c]["counts"][0] for c in ids])

    le = _prog(("exp", EE), lambda: build_expert_launch(nexp_l, CAP, PLOC, DFF))
    zrow = np.zeros((1, D), dtype=r1[0]["slotTab"].dtype)
    i128 = (np.arange(CAP // 128) * 128).astype(np.float32).reshape(1, -1)
    maps = []
    for c in ids:
        parts = []
        cnts = np.zeros((1, nexp_l * 8), np.float32)
        for el in range(nexp_l):
            e = nexp_l * c + el
            for sc in ids:
                parts.append(r1[sc]["slotTab"].reshape(EE, PLOC, D)[e])
                cnts[0, el * 8 + sc] = cnt_dev[sc, e]
        tabs = np.concatenate(parts + [zrow], axis=0)
        maps.append(dict(tabs=tabs, cnts=cnts,
                         wfc=np.ascontiguousarray(wfc[nexp_l * c:nexp_l * (c + 1)]),
                         wpr=np.ascontiguousarray(wpr[nexp_l * c:nexp_l * (c + 1)]),
                         i128=i128))
    r2 = _run(le, maps)

    lc = _prog(("comb", EE), lambda: build_combine_launch(EE, KK, CAP, drop))
    yall = np.concatenate(
        [np.ascontiguousarray(r2[e // nexp_l]["yout"][e % nexp_l].T)
         for e in range(EE)], axis=0)
    maps = [dict(xin=np.ascontiguousarray(xcur[c]), yall=yall,
                 meta=r1[c]["meta"], meta2=r1[c]["meta2"], meta3=r1[c]["meta3"],
                 cntall=cnt_dev,
                 pmask=(np.arange(8) < c).astype(np.float32).reshape(8, 1))
            for c in ids]
    r3 = _run(lc, maps)
    return np.stack([r3[c]["xout"] for c in ids])


def kernel(**inputs):
    I = inputs
    ids = list(range(NCORE))
    x0 = np.ascontiguousarray(np.asarray(I["L0_attn_x"], np.float32))
    consts = _consts_inputs()

    def rs8(v):
        return np.ascontiguousarray(np.asarray(v, np.float32).reshape(8, 128))

    x1 = _moe_layer(I, x0, E0, K0, CAP0, PLOC0, DFF0, I["L0_ffn_w_gate"],
                    I["L0_ffn_c_fc"], I["L0_ffn_c_proj"], True, 1, consts)

    lg = _prog(("grif",), build_griffin_launch)
    gmaps = [dict(xin=np.ascontiguousarray(x1[c]),
                  win=I["L0_attn_W_in"], wr=I["L0_attn_W_r"],
                  wi=I["L0_attn_W_i"], wout=I["L0_attn_W_out"],
                  binb=rs8(I["L0_attn_b_in"]), brb=rs8(I["L0_attn_b_r"]),
                  bib=rs8(I["L0_attn_b_i"]), boutb=rs8(I["L0_attn_b_out"]),
                  logab=rs8(I["L0_attn_log_a"])) for c in ids]
    rg = _run(lg, gmaps)
    x2 = np.stack([rg[c]["xout"] for c in ids])

    x3 = _moe_layer(I, x2, E0, K0, CAP0, PLOC0, DFF0, I["L0_ffn_w_gate"],
                    I["L0_ffn_c_fc"], I["L0_ffn_c_proj"], True, 1, consts)

    x4 = _moe_layer(I, x3, E1, K1, CAP1, PLOC1, 2 * I1, I["L1_ffn_router_w"],
                    I["L1_ffn_w1"], I["L1_ffn_w2"], False, 8, consts)

    lm = _prog(("mesa",), build_mesa_launch)
    mmaps = [dict(xin=np.ascontiguousarray(x4[c]),
                  wqT=np.ascontiguousarray(np.asarray(I["L1_attn_Wq"]).T),
                  wkT=np.ascontiguousarray(np.asarray(I["L1_attn_Wk"]).T),
                  wvT=np.ascontiguousarray(np.asarray(I["L1_attn_Wv"]).T),
                  woT=np.ascontiguousarray(np.asarray(I["L1_attn_Wo"]).T),
                  bq=np.asarray(I["L1_attn_bq"], np.float32).reshape(128, 1),
                  bk=np.asarray(I["L1_attn_bk"], np.float32).reshape(128, 1),
                  bv=np.asarray(I["L1_attn_bv"], np.float32).reshape(128, 1),
                  bob=rs8(I["L1_attn_bo"]),
                  identb=np.eye(128, dtype=BFNP),
                  trif=np.triu(np.ones((128, 128), np.float32)),
                  identf=np.eye(128, dtype=np.float32)) for c in ids]
    rm = _run(lm, mmaps)
    x5 = np.stack([rm[c]["xout"] for c in ids])

    x6 = _moe_layer(I, x5, E1, K1, CAP1, PLOC1, 2 * I1, I["L1_ffn_router_w"],
                    I["L1_ffn_w1"], I["L1_ffn_w2"], False, 8, consts)
    return x6
